# version 01
# baseline (speedup 1.0000x reference)
import contextlib
import numpy as np
import concourse.bass as bass
import concourse.mybir as mybir
from concourse.bass_utils import run_bass_kernel_spmd
from concourse.alu_op_type import AluOpType

F32 = mybir.dt.float32
BF16 = mybir.dt.bfloat16
AF = mybir.ActivationFunctionType
AX = mybir.AxisListType

D = 1024
T = 2048
DC = D // 128
NT = T // 512
PADW = 32
PADT = T + PADW

ENGINES = ("te", "ve", "se", "sy", "ge")


class Chan:
    __slots__ = ("sem", "n", "name")

    def __init__(self, sem, name):
        self.sem = sem
        self.n = 0
        self.name = name


class Builder:
    def __init__(self, nc, es):
        self.nc = nc
        self.es = es
        self.ops = {e: [] for e in ENGINES}
        self.chans = {}
        self.dma_chans = []

    def ch(self, name, dma=False):
        if name not in self.chans:
            sem = self.es.enter_context(self.nc.semaphore(name))
            self.chans[name] = Chan(sem, name)
            if dma:
                self.dma_chans.append(self.chans[name])
        return self.chans[name]

    def dma_barrier(self, engines=("ge", "sy")):
        for c in self.dma_chans:
            if c.n:
                for e in engines:
                    self.wait(e, c)

    def op(self, eng, f, incs=()):
        for c, a in incs:
            c.n += a
        semincs = tuple((c.sem, a) for c, a in incs)

        def g(h, f=f, semincs=semincs):
            ins = f(h)
            for s, a in semincs:
                ins = ins.then_inc(s, a)
        self.ops[eng].append(g)

    def wait(self, eng, c, tgt=None):
        t = c.n if tgt is None else tgt
        assert t <= c.n, (c.name, t, c.n)
        self.ops[eng].append(lambda h, s=c.sem, t=t: h.wait_ge(s, t))

    def emit(self, blk):
        for name, attr in (("te", "tensor"), ("ve", "vector"), ("se", "scalar"),
                           ("sy", "sync"), ("ge", "gpsimd")):
            ops = self.ops[name]

            def section(h, ops=ops):
                for f in ops:
                    f(h)
            getattr(blk, attr)(section)


def ceil_div(a, b):
    return (a + b - 1) // b


def build_program(phases):
    nc = bass.Bass("TRN2", target_bir_lowering=False)

    def din(name, shape, dtype=F32):
        return nc.dram_tensor(name, list(shape), dtype, kind="ExternalInput").ap()

    xT = din("xT", (D, T))
    w = {}
    w["l0_wq"] = din("L0_attn_wq", (D, D))
    w["l0_wk"] = din("L0_attn_wk", (D, D))
    w["l0_wv"] = din("L0_attn_wv", (D, D))
    w["l0_wo"] = din("L0_attn_wo", (D, D))
    w["l0_sel"] = din("L0_ffn_expert_sel", (16, D))
    w["l0_keys"] = din("L0_ffn_keys", (16, D, 128))
    w["l0_vals"] = din("L0_ffn_values", (16, 128, D))
    w["l1_W1"] = din("L1_attn_W1", (D, D))
    w["l1_W2"] = din("L1_attn_W2", (D, D))
    w["l1_bias"] = din("L1_attn_bias", (D,))
    w["l1_et"] = din("L1_attn_embeddings_table", (21, D))
    w["l1_rw"] = din("L1_ffn_router_w", (16, D))
    w["l1_rb"] = din("L1_ffn_router_bias", (16,))
    w["l1_up"] = din("L1_ffn_up_proj", (16, D, 512))
    w["l1_gate"] = din("L1_ffn_gate_proj", (16, D, 512))
    w["l1_down"] = din("L1_ffn_down_proj", (16, 512, D))
    w["l1_shg"] = din("L1_ffn_sh_gate", (D, D))
    w["l1_shu"] = din("L1_ffn_sh_up", (D, D))
    w["l1_shd"] = din("L1_ffn_sh_down", (D, D))
    identb = din("identb", (128, 128), BF16)
    identf = din("identf", (128, 128), F32)
    onesb = din("onesb", (128, 1), BF16)
    onesfr = din("onesfr", (1, 128), F32)

    yT = nc.dram_tensor("yT", [D, T], F32, kind="ExternalOutput").ap()
    scr = nc.dram_tensor("scr", [64, T], F32, kind="Internal").ap()

    es = contextlib.ExitStack()
    with es:
        b = Builder(nc, es)
        sb = lambda name, shape, dtype: es.enter_context(nc.sbuf_tensor(name, list(shape), dtype))
        ps = lambda name, shape, dtype=F32: es.enter_context(nc.psum_tensor(name, list(shape), dtype))

        XSB = sb("XSB", (128, DC * T), F32)
        XN = sb("XN", (128, DC * PADT), BF16)
        AR = sb("AR", (128, 49152), BF16)
        IDB = sb("IDB", (128, 128), BF16)
        IDF = sb("IDF", (128, 128), F32)
        ONB = sb("ONB", (128, 1), BF16)
        ONFR = sb("ONFR", (1, 128), F32)
        EPS = sb("EPS", (128, 1), F32)
        NEG8 = sb("NEG8", (128, 1), F32)
        SMa = sb("SMa", (1, 512), F32)
        SMb = sb("SMb", (1, 512), F32)
        SMc = sb("SMc", (1, 512), F32)
        LNC = sb("LNC", (128, 4096), BF16)

        PA = ps("PA", (128, 1024))
        PB = ps("PB", (128, 1024))
        P4 = ps("P4", (128, 512))
        P5 = ps("P5", (128, 512))
        P6 = ps("P6", (128, 512))
        P7 = ps("P7", (128, 512))

        c_te = b.ch("c_te")
        c_ve = b.ch("c_ve")
        c_se = b.ch("c_se")
        c_ge = b.ch("c_ge")
        c_sy = b.ch("c_sy", dma=True)
        c_init = b.ch("c_init", dma=True)
        c_xl = b.ch("c_xl", dma=True)
        c_gw = b.ch("c_gw", dma=True)
        c_bc0 = b.ch("c_bc0", dma=True)
        c_bc1 = b.ch("c_bc1", dma=True)
        c_scw = b.ch("c_scw", dma=True)
        c_out = b.ch("c_out", dma=True)

        def xn_sl(c, t0, tw):
            return XN[:, c * PADT + PADW + t0: c * PADT + PADW + t0 + tw]

        def xsb_sl(c, t0, tw):
            return XSB[:, c * T + t0: c * T + t0 + tw]

        class Arena:
            def __init__(self):
                self.top = 0

            def alloc(self, cols, dtype=BF16):
                bcols = cols * (2 if dtype == F32 else 1)
                bcols = (bcols + 63) // 64 * 64
                c0 = self.top
                self.top += bcols
                assert self.top <= 49152, f"arena overflow {self.top}"
                if dtype == BF16:
                    return lambda p0, p1, f0, f1: AR[p0:p1, c0 + f0: c0 + f1]
                else:
                    arf = AR[:].bitcast(F32)
                    c0f = c0 // 2
                    return lambda p0, p1, f0, f1: arf[p0:p1, c0f + f0: c0f + f1]

            def reset(self, top=0):
                self.top = top

        b.op("sy", lambda h: h.dma_start(IDB[:], identb), [(c_init, 16)])
        b.op("sy", lambda h: h.dma_start(IDF[:], identf), [(c_init, 16)])
        b.op("sy", lambda h: h.dma_start(ONB[:], onesb), [(c_init, 16)])
        b.op("sy", lambda h: h.dma_start(ONFR[:], onesfr), [(c_init, 16)])
        b.op("ge", lambda h: h.memset(EPS[:], 1e-5), [(c_ge, 1)])
        b.op("ge", lambda h: h.memset(NEG8[:], -8.0), [(c_ge, 1)])
        for c in range(DC):
            b.op("ge", lambda h, c=c: h.memset(XN[:, c * PADT: c * PADT + PADW], 0.0),
                 [(c_ge, 1)])
        init_ge = c_ge.n

        xTr = xT.rearrange("(c p) t -> p c t", p=128)
        b.op("sy", lambda h: h.dma_start(
            XSB[:].rearrange("p (c t) -> p c t", c=DC), xTr), [(c_xl, 16)])

        b.wait("ve", c_xl)
        b.wait("te", c_init)
        b.wait("te", c_ge, init_ge)
        b.wait("se", c_ge, init_ge)

        def emit_layernorm():
            b.wait("ve", c_ve)
            prev_sumsq = 0
            prev_affine = 0
            for t in range(NT):
                t0 = t * 512
                xv = XSB[:].rearrange("p (c t) -> p c t", c=DC)[:, :, t0:t0 + 512]
                lnc3 = LNC[:].rearrange("p (c t) -> p c t", c=DC)[:, :, 0:512]
                if t > 0:
                    b.wait("ve", c_te, prev_sumsq)
                    b.wait("ve", c_ve, prev_affine)
                b.op("ve", lambda h, xv=xv, lnc3=lnc3: h.tensor_copy(lnc3, xv), [(c_ve, 1)])
                cast_done = c_ve.n
                b.wait("te", c_ve, cast_done)
                for c in range(DC):
                    b.op("te", lambda h, c=c: h.matmul(
                        P4[0:1, :], ONB[:], LNC[:, c * 512:(c + 1) * 512],
                        start=(c == 0), stop=(c == DC - 1)),
                        [(c_te, 1)] if c == DC - 1 else ())
                sum_done = c_te.n
                b.wait("se", c_ve, cast_done)
                b.wait("se", c_te, sum_done)
                b.op("se", lambda h, lnc3=lnc3: h.activation(
                    lnc3, lnc3, AF.Square), [(c_se, 1)])
                sq_done = c_se.n
                b.wait("te", c_se, sq_done)
                for c in range(DC):
                    b.op("te", lambda h, c=c: h.matmul(
                        P5[0:1, :], ONB[:], LNC[:, c * 512:(c + 1) * 512],
                        start=(c == 0), stop=(c == DC - 1)),
                        [(c_te, 1)] if c == DC - 1 else ())
                sumsq_done = c_te.n
                prev_sumsq = sumsq_done
                b.wait("ve", c_te, sum_done)
                b.op("ve", lambda h: h.tensor_scalar(
                    SMa[:], P4[0:1, :], 1.0 / D, None, AluOpType.mult), [(c_ve, 1)])
                mean_done = c_ve.n
                b.wait("ve", c_te, sumsq_done)
                b.op("ve", lambda h: h.tensor_scalar(
                    SMb[:], P5[0:1, :], 1.0 / D, None, AluOpType.mult), [(c_ve, 1)])
                b.wait("se", c_ve, mean_done)
                b.op("se", lambda h: h.activation(SMc[:], SMa[:], AF.Square), [(c_se, 1)])
                b.wait("ve", c_se)
                b.wait("ve", c_ve)
                b.op("ve", lambda h: h.tensor_tensor(
                    SMb[:], SMb[:], SMc[:], AluOpType.subtract), [(c_ve, 1)])
                b.wait("se", c_ve)
                b.op("se", lambda h: h.activation(
                    SMb[:], SMb[:], AF.Sqrt, bias=EPS[0:1, :]), [(c_se, 1)])
                b.wait("ve", c_se)
                b.op("ve", lambda h: h.reciprocal(SMb[:], SMb[:]), [(c_ve, 1)])
                rstd_done = c_ve.n
                b.wait("ve", c_ve, rstd_done)
                b.op("ve", lambda h: h.tensor_tensor(
                    SMc[:], SMa[:], SMb[:], AluOpType.mult), [(c_ve, 1)])
                brow_done = c_ve.n
                b.wait("te", c_ve, brow_done)
                b.op("te", lambda h: h.matmul(P6[:], ONFR[:], SMb[:], start=True, stop=True),
                     [(c_te, 1)])
                b.op("te", lambda h: h.matmul(P7[:], ONFR[:], SMc[:], start=True, stop=True),
                     [(c_te, 1)])
                rep_done = c_te.n
                b.wait("ve", c_te, rep_done)
                for c in range(DC):
                    b.op("ve", lambda h, c=c, t0=t0: h.tensor_tensor(
                        LNC[:, c * 512:(c + 1) * 512],
                        XSB[:, c * T + t0: c * T + t0 + 512],
                        P6[:, 0:512], AluOpType.mult), [(c_ve, 1)])
                mul_done = c_ve.n
                b.wait("ve", c_ve, mul_done)
                for c in range(DC):
                    b.op("ve", lambda h, c=c, t0=t0: h.tensor_tensor(
                        xn_sl(c, t0, 512), LNC[:, c * 512:(c + 1) * 512],
                        P7[:, 0:512], AluOpType.subtract), [(c_ve, 1)])
                prev_affine = c_ve.n
            return c_ve.n

        def emit_l0_ffn(tag):
            ar = Arena()
            KEY = ar.alloc(16 * 1024)
            VAL = ar.alloc(16 * 1024)
            EST = ar.alloc(128)
            ESN = ar.alloc(1024)
            SC = ar.alloc(16 * 512)
            SBC0 = ar.alloc(512)
            SBC1 = ar.alloc(512)
            SELF = ar.alloc(2048, F32)

            entry_te, entry_ve, entry_se = c_te.n, c_ve.n, c_se.n
            xn_done = emit_layernorm()

            for _e in ("ge", "sy"):
                b.wait(_e, c_te, entry_te)
                b.wait(_e, c_ve, entry_ve)
                b.wait(_e, c_se, entry_se)
            b.dma_barrier()
            keys_r = w["l0_keys"].rearrange("e (c p) hh -> e p c hh", p=128)
            vals_r = w["l0_vals"]
            for e in range(16):
                b.op("ge", lambda h, e=e: h.dma_start(
                    KEY(0, 128, e * 1024, (e + 1) * 1024).rearrange("p (c hh) -> p c hh", c=DC),
                    keys_r[e]), [(c_gw, 16)])
                b.op("ge", lambda h, e=e: h.dma_start(
                    VAL(0, 128, e * 1024, (e + 1) * 1024), vals_r[e]), [(c_gw, 16)])
            b.op("ge", lambda h: h.dma_start(
                ESN(0, 16, 0, 1024), w["l0_sel"]), [(c_gw, 16)])
            wload = c_gw.n

            b.wait("te", c_gw, wload)
            pa_b = PA[:].bitcast(BF16)
            for c in range(DC):
                b.op("te", lambda h, c=c: h.transpose(
                    pa_b[:, c * 16:(c + 1) * 16],
                    ESN(0, 16, c * 128, (c + 1) * 128), IDB[0:16, 0:16]),
                    [(c_te, 1)] if c == DC - 1 else ())
            est_tr = c_te.n
            b.wait("ve", c_te, est_tr)
            b.op("ve", lambda h: h.tensor_copy(EST(0, 128, 0, 128), pa_b[:, 0:128]),
                 [(c_ve, 1)])
            est_done = c_ve.n

            b.wait("te", c_ve, xn_done)
            b.wait("te", c_ve, est_done)
            sig_mark = {0: 0, 1: 0}
            for t in range(NT):
                t0 = t * 512
                if sig_mark[t % 2]:
                    b.wait("te", c_se, sig_mark[t % 2])
                for c in range(DC):
                    b.op("te", lambda h, c=c, t0=t0, t=t: h.matmul(
                        PB[0:16, (t % 2) * 512:(t % 2) * 512 + 512],
                        EST(0, 128, c * 16, (c + 1) * 16),
                        xn_sl(c, t0, 512),
                        start=(c == 0), stop=(c == DC - 1)),
                        [(c_te, 1)] if c == DC - 1 else ())
                lg = c_te.n
                b.wait("se", c_te, lg)
                b.op("se", lambda h, t=t, t0=t0: h.activation(
                    SELF(0, 16, t0, t0 + 512), PB[0:16, (t % 2) * 512:(t % 2) * 512 + 512],
                    AF.Sigmoid), [(c_se, 1)])
                sig_mark[t % 2] = c_se.n
            sel_sig = c_se.n
            b.wait("ve", c_se, sel_sig)
            b.op("ve", lambda h: h.tensor_scalar(
                SELF(0, 16, 0, T), SELF(0, 16, 0, T), 0.5, None, AluOpType.mult),
                [(c_ve, 1)])
            sel_done = c_ve.n
            b.wait("ge", c_ve, sel_done)
            b.op("ge", lambda h: h.dma_start(scr[0:16, :], SELF(0, 16, 0, T)), [(c_scw, 16)])
            b.wait("ge", c_scw)

            se_relu_mark = {0: 0, 1: 0}
            sbc_lastread = {0: 0, 1: 0}
            bc_target = {}
            vdone_mark = {}
            val_te_mark = 0
            for t in range(NT):
                t0 = t * 512
                for e in range(16):
                    idx = t * 16 + e
                    half = e % 2
                    bc = (c_bc0, c_bc1)[idx % 2]
                    sbc = (SBC0, SBC1)[idx % 2]
                    if sbc_lastread[idx % 2]:
                        b.wait("ge", c_ve, sbc_lastread[idx % 2])
                    b.op("ge", lambda h, e=e, t0=t0, sbc=sbc: h.dma_start(
                        sbc(0, 128, 0, 512),
                        scr[e:e + 1, t0:t0 + 512].broadcast_to((128, 512))), [(bc, 16)])
                    bc_target[idx] = bc.n
                    if se_relu_mark[half]:
                        b.wait("te", c_se, se_relu_mark[half])
                    if e == 0 and val_te_mark:
                        b.wait("se", c_te, val_te_mark)
                    for c in range(DC):
                        b.op("te", lambda h, c=c, e=e, t0=t0, half=half: h.matmul(
                            PA[:, half * 512: half * 512 + 512],
                            KEY(0, 128, e * 1024 + c * 128, e * 1024 + (c + 1) * 128),
                            xn_sl(c, t0, 512),
                            start=(c == 0), stop=(c == DC - 1)),
                            [(c_te, 1)] if c == DC - 1 else ())
                    key_done = c_te.n
                    b.wait("se", c_te, key_done)
                    b.op("se", lambda h, e=e, half=half: h.activation(
                        SC(0, 128, e * 512, (e + 1) * 512),
                        PA[:, half * 512: half * 512 + 512], AF.Relu), [(c_se, 1)])
                    se_relu_mark[half] = c_se.n
                    b.wait("ve", bc, bc_target[idx])
                    b.wait("ve", c_se, se_relu_mark[half])
                    b.op("ve", lambda h, e=e, sbc=sbc: h.tensor_tensor(
                        SC(0, 128, e * 512, (e + 1) * 512),
                        SC(0, 128, e * 512, (e + 1) * 512),
                        sbc(0, 128, 0, 512), AluOpType.mult), [(c_ve, 1)])
                    sbc_lastread[idx % 2] = c_ve.n
                sc_ready = c_ve.n
                b.wait("te", c_ve, sc_ready)
                for v in range(DC):
                    vidx = t * DC + v
                    pv = (P4, P5, P6)[vidx % 3]
                    if vidx >= 3:
                        b.wait("te", c_ve, vdone_mark[vidx - 3])
                    for e in range(16):
                        b.op("te", lambda h, pv=pv, v=v, e=e: h.matmul(
                            pv[:, 0:512],
                            VAL(0, 128, e * 1024 + v * 128, e * 1024 + (v + 1) * 128),
                            SC(0, 128, e * 512, (e + 1) * 512),
                            start=(e == 0), stop=(e == 15)),
                            [(c_te, 1)] if e == 15 else ())
                    val_done = c_te.n
                    b.wait("ve", c_te, val_done)
                    b.op("ve", lambda h, pv=pv, v=v, t0=t0: h.tensor_tensor(
                        xsb_sl(v, t0, 512), xsb_sl(v, t0, 512), pv[:, 0:512],
                        AluOpType.add), [(c_ve, 1)])
                    vdone_mark[vidx] = c_ve.n
                val_te_mark = c_te.n
            return c_ve.n

        def emit_l0_attn():
            ar = Arena()
            WQT = ar.alloc(8192)
            WKT = ar.alloc(8192)
            WVT = ar.alloc(8192)
            WOT = ar.alloc(8192)
            WSTG = ar.alloc(2048)
            QTM = ar.alloc(1024)
            KTM = ar.alloc(1024)
            QTBe = ar.alloc(1024)
            QTBo = ar.alloc(1024)
            KTBe = ar.alloc(1024)
            KTBo = ar.alloc(1024)
            VTM = ar.alloc(1040)
            PT0 = ar.alloc(1024)
            PT1 = ar.alloc(1024)
            AOTh = ar.alloc(2048)
            RBC = ar.alloc(1024, F32)

            P6b = P6[:].bitcast(BF16)
            P7b = P7[:].bitcast(BF16)

            entry_te, entry_ve, entry_se = c_te.n, c_ve.n, c_se.n
            xn_done = emit_layernorm()
            for _e in ("ge", "sy"):
                b.wait(_e, c_te, entry_te)
                b.wait(_e, c_ve, entry_ve)
                b.wait(_e, c_se, entry_se)
            b.dma_barrier()

            wsrc = [w["l0_wq"], w["l0_wk"], w["l0_wv"], w["l0_wo"]]
            wdst = [WQT, WKT, WVT, WOT]
            c_ws = [b.ch("c_ws0", dma=True), b.ch("c_ws1", dma=True)]
            stg_te_mark = {}
            evac_ve_mark = {}
            b.wait("te", c_ve, xn_done)
            for idx in range(32):
                wi, oc = idx // 8, idx % 8
                slot = idx % 2
                if idx >= 2:
                    b.wait("ge", c_te, stg_te_mark[idx - 2])
                b.op("ge", lambda h, wi=wi, oc=oc, slot=slot: h.dma_start(
                    WSTG(0, 128, slot * 1024, slot * 1024 + 1024),
                    wsrc[wi][oc * 128:(oc + 1) * 128, :]), [(c_ws[slot], 16)])
                b.wait("te", c_ws[slot], 16 * (idx // 2 + 1))
                if idx >= 2:
                    b.wait("te", c_ve, evac_ve_mark[idx - 2])
                pb_t = (P6b, P7b)[slot]
                for cc in range(DC):
                    b.op("te", lambda h, cc=cc, slot=slot, pb_t=pb_t: h.transpose(
                        pb_t[:, cc * 128:(cc + 1) * 128],
                        WSTG(0, 128, slot * 1024 + cc * 128, slot * 1024 + (cc + 1) * 128),
                        IDB[:]), [(c_te, 1)] if cc == DC - 1 else ())
                stg_te_mark[idx] = c_te.n
                b.wait("ve", c_te, stg_te_mark[idx])
                dst = wdst[wi]
                b.op("ve", lambda h, dst=dst, oc=oc, pb_t=pb_t: h.tensor_copy(
                    dst(0, 128, 0, 8192).rearrange("p (cc o) -> p cc o", cc=DC)[
                        :, :, oc * 128:(oc + 1) * 128],
                    pb_t.rearrange("p (cc s) -> p cc s", cc=DC)), [(c_ve, 1)])
                evac_ve_mark[idx] = c_ve.n
            wprep_ve = c_ve.n

            b.op("ge", lambda h: h.memset(
                VTM(0, 128, 0, 1040).rearrange("p (g dd) -> p g dd", dd=65)[:, :, 64:65],
                1.0), [(c_ge, 1)])
            vones = c_ge.n
            b.wait("te", c_ge, vones)
            b.wait("te", c_ve, wprep_ve)
            b.wait("te", c_ve, xn_done)

            NEG8l = NEG8
            exp_mark = {}
            pt_av_mark = {}
            for h16 in range(16):
                hcol = 128 * h16
                aoth0 = (h16 % 2) * 1024
                for half in range(2):
                    for cc in range(DC):
                        b.op("te", lambda h, cc=cc, half=half, hcol=hcol: h.matmul(
                            PA[:, half * 512:(half + 1) * 512],
                            xn_sl(cc, hcol, 128),
                            WQT(0, 128, cc * 1024 + half * 512, cc * 1024 + (half + 1) * 512),
                            start=(cc == 0), stop=(cc == DC - 1)),
                            [(c_te, 1)] if cc == DC - 1 else ())
                q_done = c_te.n
                for half in range(2):
                    for cc in range(DC):
                        b.op("te", lambda h, cc=cc, half=half, hcol=hcol: h.matmul(
                            PB[:, half * 512:(half + 1) * 512],
                            xn_sl(cc, hcol, 128),
                            WKT(0, 128, cc * 1024 + half * 512, cc * 1024 + (half + 1) * 512),
                            start=(cc == 0), stop=(cc == DC - 1)),
                            [(c_te, 1)] if cc == DC - 1 else ())
                k_done = c_te.n
                b.wait("ve", c_te, q_done)
                b.op("ve", lambda h: h.tensor_copy(QTM(0, 128, 0, 1024), PA[:]), [(c_ve, 1)])
                q_ev = c_ve.n
                b.wait("ve", c_te, k_done)
                b.op("ve", lambda h: h.tensor_copy(KTM(0, 128, 0, 1024), PB[:]), [(c_ve, 1)])
                k_ev = c_ve.n
                b.wait("te", c_ve, q_ev)
                for half in range(2):
                    for cc in range(DC):
                        b.op("te", lambda h, cc=cc, half=half, hcol=hcol: h.matmul(
                            PA[:, half * 512:(half + 1) * 512],
                            xn_sl(cc, hcol, 128),
                            WVT(0, 128, cc * 1024 + half * 512, cc * 1024 + (half + 1) * 512),
                            start=(cc == 0), stop=(cc == DC - 1)),
                            [(c_te, 1)] if cc == DC - 1 else ())
                v_done = c_te.n
                b.wait("ve", c_te, v_done)
                for half in range(2):
                    b.op("ve", lambda h, half=half: h.tensor_copy(
                        VTM(0, 128, half * 520, half * 520 + 520).rearrange(
                            "p (g dd) -> p g dd", dd=65)[:, :, 0:64],
                        PA[:, half * 512:(half + 1) * 512].rearrange(
                            "p (g dd) -> p g dd", dd=64)), [(c_ve, 1)])
                v_ev = c_ve.n
                b.wait("te", c_ve, q_ev)
                for bb in range(DC):
                    b.op("te", lambda h, bb=bb: h.transpose(
                        P6b[:, bb * 128:(bb + 1) * 128],
                        QTM(0, 128, bb * 128, (bb + 1) * 128), IDB[:]),
                        [(c_te, 1)] if bb == DC - 1 else ())
                qtr = c_te.n
                b.wait("ve", c_te, qtr)
                b.op("ve", lambda h: h.tensor_copy(QTBe(0, 64, 0, 1024), P6b[0:64, :]), [(c_ve, 1)])
                b.op("ve", lambda h: h.tensor_copy(QTBo(0, 64, 0, 1024), P6b[64:128, :]), [(c_ve, 1)])
                qtb_ev = c_ve.n
                b.wait("te", c_ve, k_ev)
                for bb in range(DC):
                    b.op("te", lambda h, bb=bb: h.transpose(
                        P7b[:, bb * 128:(bb + 1) * 128],
                        KTM(0, 128, bb * 128, (bb + 1) * 128), IDB[:]),
                        [(c_te, 1)] if bb == DC - 1 else ())
                ktr = c_te.n
                b.wait("ve", c_te, ktr)
                b.op("ve", lambda h: h.tensor_copy(KTBe(0, 64, 0, 1024), P7b[0:64, :]), [(c_ve, 1)])
                b.op("ve", lambda h: h.tensor_copy(KTBo(0, 64, 0, 1024), P7b[64:128, :]), [(c_ve, 1)])
                ktb_ev = c_ve.n
                b.wait("te", c_ve, ktb_ev)
                b.wait("se", c_ve, qtb_ev)
                b.wait("se", c_ve, v_ev)
                for pi in range(2):
                    QTB = (QTBe, QTBo)[pi]
                    avp = [(PB, PB), (P4, P5)][pi]
                    avsl = [((0, 512), (512, 1024)), ((0, 512), (0, 512))][pi]
                    for jg in range(16):
                        KTB = (KTBe, KTBo)[jg % 2]
                        jb = jg // 2
                        if exp_mark.get("cur"):
                            b.wait("te", c_se, exp_mark["cur"])
                        for ih in range(2):
                            b.op("te", lambda h, KTB=KTB, jb=jb, QTB=QTB, ih=ih: h.matmul(
                                PA[:, ih * 512:(ih + 1) * 512],
                                KTB(0, 64, jb * 128, (jb + 1) * 128),
                                QTB(0, 64, ih * 512, (ih + 1) * 512),
                                start=True, stop=True),
                                [(c_te, 1)] if ih == 1 else ())
                        qk_done = c_te.n
                        pt = (PT0, PT1)[jg % 2]
                        b.wait("se", c_te, qk_done)
                        if pt_av_mark.get(jg % 2):
                            b.wait("se", c_te, pt_av_mark[jg % 2])
                        b.op("se", lambda h, pt=pt: h.activation(
                            pt(0, 128, 0, 1024), PA[:], AF.Exp,
                            bias=NEG8l[:], scale=0.125), [(c_se, 1)])
                        exp_mark["cur"] = c_se.n
                        b.wait("te", c_se, exp_mark["cur"])
                        for ih in range(2):
                            pvt, (s0, s1) = avp[ih], avsl[ih]
                            b.op("te", lambda h, pvt=pvt, s0=s0, s1=s1, jg=jg, pt=pt, ih=ih: h.matmul(
                                pvt[0:65, s0:s1],
                                VTM(0, 128, jg * 65, (jg + 1) * 65),
                                pt(0, 128, ih * 512, (ih + 1) * 512),
                                start=(jg == 0), stop=(jg == 15)),
                                [(c_te, 1)] if ih == 1 else ())
                        pt_av_mark[jg % 2] = c_te.n
                    av_done = c_te.n
                    b.wait("ve", c_te, av_done)
                    b.op("ve", lambda h, avp=avp, avsl=avsl: h.reciprocal(
                        SMa[:], avp[0][64:65, avsl[0][0]:avsl[0][1]]), [(c_ve, 1)])
                    b.op("ve", lambda h, avp=avp, avsl=avsl: h.reciprocal(
                        SMb[:], avp[1][64:65, avsl[1][0]:avsl[1][1]]), [(c_ve, 1)])
                    rec_done = c_ve.n
                    b.wait("te", c_ve, rec_done)
                    b.op("te", lambda h: h.matmul(P6[0:64, :], ONFR[0:1, 0:64], SMa[:],
                                                  start=True, stop=True), [(c_te, 1)])
                    b.op("te", lambda h: h.matmul(P7[0:64, :], ONFR[0:1, 0:64], SMb[:],
                                                  start=True, stop=True), [(c_te, 1)])
                    rep_done = c_te.n
                    b.wait("ve", c_te, rep_done)
                    b.op("ve", lambda h: h.tensor_copy(RBC(0, 64, 0, 512), P6[0:64, :]), [(c_ve, 1)])
                    b.op("ve", lambda h: h.tensor_copy(RBC(0, 64, 512, 1024), P7[0:64, :]), [(c_ve, 1)])
                    rbc_done = c_ve.n
                    b.wait("ve", c_ve, rbc_done)
                    for ih in range(2):
                        pvt, (s0, s1) = avp[ih], avsl[ih]
                        b.op("ve", lambda h, pvt=pvt, s0=s0, s1=s1, pi=pi, ih=ih, aoth0=aoth0: h.tensor_tensor(
                            AOTh(64 * pi, 64 * pi + 64, aoth0 + ih * 512, aoth0 + (ih + 1) * 512),
                            pvt[64 * 0:64, s0:s1] if False else pvt[0:64, s0:s1],
                            RBC(0, 64, ih * 512, (ih + 1) * 512),
                            AluOpType.mult), [(c_ve, 1)])
                    evac_done = c_ve.n
                    if pi == 0:
                        b.wait("te", c_ve, evac_done)
                aoth_done = c_ve.n
                b.wait("te", c_ve, aoth_done)
                opev = {}
                for ccp in range(DC):
                    bk = (ccp % 2) * 512
                    if ccp >= 2:
                        b.wait("te", c_ve, opev[ccp - 2])
                    for oc in range(DC):
                        b.op("te", lambda h, ccp=ccp, oc=oc, aoth0=aoth0, bk=bk: h.matmul(
                            PB[:, bk:bk + 128],
                            WOT(0, 128, oc * 1024 + ccp * 128, oc * 1024 + (ccp + 1) * 128),
                            AOTh(0, 128, aoth0 + oc * 128, aoth0 + (oc + 1) * 128),
                            start=(oc == 0), stop=(oc == DC - 1)),
                            [(c_te, 1)] if oc == DC - 1 else ())
                    op_done = c_te.n
                    b.wait("ve", c_te, op_done)
                    b.op("ve", lambda h, ccp=ccp, hcol=hcol, bk=bk: h.tensor_tensor(
                        xsb_sl(ccp, hcol, 128), xsb_sl(ccp, hcol, 128),
                        PB[:, bk:bk + 128],
                        AluOpType.add), [(c_ve, 1)])
                    opev[ccp] = c_ve.n
                oproj_ve = c_ve.n
                b.wait("te", c_ve, oproj_ve)
            return c_ve.n

        def emit_l1_moe(tag, scrow):
            ar = Arena()
            CBC0 = ar.alloc(2048)
            CBC1 = ar.alloc(2048)
            RWT = ar.alloc(128)
            RWN = ar.alloc(1024)
            LGT = ar.alloc(2048, F32)
            LTM = ar.alloc(256, F32)
            CTM = ar.alloc(256, F32)
            CBT = ar.alloc(2048, F32)
            RB16 = ar.alloc(1, F32)
            SCL = ar.alloc(8, F32)
            HS = ar.alloc(1024)
            HID0 = ar.alloc(1024)
            HID1 = ar.alloc(1024)
            SHHC = ar.alloc(2048)
            ovl = ar.top
            SHGT = ar.alloc(8192)
            SHUT = ar.alloc(8192)
            SHDT = ar.alloc(8192)
            WSTG2 = ar.alloc(2048)
            ar.reset(ovl)
            ESL0 = ar.alloc(12288)
            ESL1 = ar.alloc(12288)

            P6b = P6[:].bitcast(BF16)
            P7b = P7[:].bitcast(BF16)
            m1c = SCL(0, 128, 0, 1)
            m2c = SCL(0, 128, 1, 2)
            nm1c = SCL(0, 128, 2, 3)
            recc = SCL(0, 128, 3, 4)

            entry_te, entry_ve, entry_se = c_te.n, c_ve.n, c_se.n
            xn_done = emit_layernorm()
            b.wait("ge", c_te, entry_te)
            b.wait("ge", c_ve, entry_ve)
            b.wait("ge", c_se, entry_se)

            c_rt = b.ch("c_rt_" + tag, dma=True)
            b.op("ge", lambda h: h.dma_start(RWN(0, 16, 0, 1024), w["l1_rw"]), [(c_rt, 16)])
            b.op("ge", lambda h: h.dma_start(
                RB16(0, 16, 0, 1), w["l1_rb"].rearrange("(e o) -> e o", o=1)), [(c_rt, 16)])

            c_sh = [b.ch("c_sh0_" + tag, dma=True), b.ch("c_sh1_" + tag, dma=True)]
            stg_te = {}
            evac_ve = {}
            shsrc = [w["l1_shg"], w["l1_shu"], w["l1_shd"]]
            shdst = [SHGT, SHUT, SHDT]
            b.wait("te", c_ve, xn_done)
            for idx in range(24):
                wi, oc = idx // 8, idx % 8
                slot = idx % 2
                if idx >= 2:
                    b.wait("ge", c_te, stg_te[idx - 2])
                b.op("ge", lambda h, wi=wi, oc=oc, slot=slot: h.dma_start(
                    WSTG2(0, 128, slot * 1024, slot * 1024 + 1024),
                    shsrc[wi][oc * 128:(oc + 1) * 128, :]), [(c_sh[slot], 16)])
                b.wait("te", c_sh[slot], 16 * (idx // 2 + 1))
                if idx >= 2:
                    b.wait("te", c_ve, evac_ve[idx - 2])
                pb_t = (P6b, P7b)[slot]
                for cc in range(DC):
                    b.op("te", lambda h, cc=cc, slot=slot, pb_t=pb_t: h.transpose(
                        pb_t[:, cc * 128:(cc + 1) * 128],
                        WSTG2(0, 128, slot * 1024 + cc * 128, slot * 1024 + (cc + 1) * 128),
                        IDB[:]), [(c_te, 1)] if cc == DC - 1 else ())
                stg_te[idx] = c_te.n
                b.wait("ve", c_te, stg_te[idx])
                b.op("ve", lambda h, wi=wi, oc=oc, pb_t=pb_t: h.tensor_copy(
                    shdst[wi](0, 128, 0, 8192).rearrange("p (cc o) -> p cc o", cc=DC)[
                        :, :, oc * 128:(oc + 1) * 128],
                    pb_t.rearrange("p (cc s) -> p cc s", cc=DC)), [(c_ve, 1)])
                evac_ve[idx] = c_ve.n
            shw_ve = c_ve.n

            b.wait("te", c_rt)
            b.wait("te", c_ve, xn_done)
            b.wait("te", c_ve, shw_ve)
            for cc in range(DC):
                b.op("te", lambda h, cc=cc: h.transpose(
                    P6b[:, cc * 16:(cc + 1) * 16],
                    RWN(0, 16, cc * 128, (cc + 1) * 128), IDB[0:16, 0:16]),
                    [(c_te, 1)] if cc == DC - 1 else ())
            rwt_tr = c_te.n
            b.wait("ve", c_te, rwt_tr)
            b.op("ve", lambda h: h.tensor_copy(RWT(0, 128, 0, 128), P6b[:, 0:128]), [(c_ve, 1)])
            rwt_done = c_ve.n
            b.wait("te", c_ve, rwt_done)
            lg_se = {0: 0, 1: 0}
            for t in range(NT):
                t0 = t * 512
                if lg_se[t % 2]:
                    b.wait("te", c_ve, lg_se[t % 2])
                for cc in range(DC):
                    b.op("te", lambda h, cc=cc, t0=t0, t=t: h.matmul(
                        PB[0:16, (t % 2) * 512:(t % 2) * 512 + 512],
                        RWT(0, 128, cc * 16, (cc + 1) * 16),
                        xn_sl(cc, t0, 512),
                        start=(cc == 0), stop=(cc == DC - 1)),
                        [(c_te, 1)] if cc == DC - 1 else ())
                lgm = c_te.n
                b.wait("ve", c_te, lgm)
                b.wait("ve", c_rt)
                b.op("ve", lambda h, t=t, t0=t0: h.tensor_scalar(
                    LGT(0, 16, t0, t0 + 512), PB[0:16, (t % 2) * 512:(t % 2) * 512 + 512],
                    RB16(0, 16, 0, 1), None, AluOpType.add), [(c_ve, 1)])
                lg_se[t % 2] = c_ve.n
            lgt_done = c_ve.n
            b.wait("te", c_ve, lgt_done)
            for tt in range(16):
                pvt = (P4, P5)[tt // 8]
                b.op("te", lambda h, tt=tt, pvt=pvt: h.transpose(
                    pvt[:, (tt % 8) * 16:(tt % 8) * 16 + 16],
                    LGT(0, 16, tt * 128, (tt + 1) * 128), IDF[0:16, 0:16]),
                    [(c_te, 1)] if tt in (7, 15) else ())
            ltm_tr = c_te.n
            b.wait("ve", c_te, ltm_tr)
            b.op("ve", lambda h: h.tensor_copy(LTM(0, 128, 0, 128), P4[:, 0:128]), [(c_ve, 1)])
            b.op("ve", lambda h: h.tensor_copy(LTM(0, 128, 128, 256), P5[:, 0:128]), [(c_ve, 1)])
            ltm_done = c_ve.n
            b.wait("ve", c_ve, ltm_done)
            for tt in range(16):
                lg = LTM(0, 128, tt * 16, (tt + 1) * 16)
                cm = CTM(0, 128, tt * 16, (tt + 1) * 16)
                b.op("ve", lambda h, lg=lg: h.reduce_max(m1c, lg, axis=AX.X), [(c_ve, 1)])
                b.wait("ve", c_ve)
                b.op("ve", lambda h, lg=lg, cm=cm: h.tensor_scalar(
                    cm, lg, m1c, -1e30, AluOpType.is_ge, AluOpType.mult), [(c_ve, 1)])
                b.wait("ve", c_ve)
                b.op("ve", lambda h, lg=lg, cm=cm: h.tensor_tensor(
                    cm, cm, lg, AluOpType.add), [(c_ve, 1)])
                b.wait("ve", c_ve)
                b.op("ve", lambda h, cm=cm: h.reduce_max(m2c, cm, axis=AX.X), [(c_ve, 1)])
                b.op("ve", lambda h: h.tensor_scalar(
                    nm1c, m1c, -1.0, None, AluOpType.mult), [(c_ve, 1)])
                nm1_done = c_ve.n
                b.wait("se", c_ve, nm1_done)
                b.op("se", lambda h, lg=lg, cm=cm: h.activation(
                    cm, lg, AF.Exp, bias=nm1c), [(c_se, 1)])
                eb_done = c_se.n
                b.wait("ve", c_se, eb_done)
                b.op("ve", lambda h, lg=lg: h.tensor_scalar(
                    lg, lg, m2c, None, AluOpType.is_ge), [(c_ve, 1)])
                b.wait("ve", c_ve)
                b.op("ve", lambda h, lg=lg, cm=cm: h.tensor_tensor(
                    cm, cm, lg, AluOpType.mult), [(c_ve, 1)])
                b.wait("ve", c_ve)
                b.op("ve", lambda h, cm=cm: h.reduce_sum(recc, cm, axis=AX.X), [(c_ve, 1)])
                b.wait("ve", c_ve)
                b.op("ve", lambda h: h.reciprocal(recc, recc), [(c_ve, 1)])
                b.wait("ve", c_ve)
                b.op("ve", lambda h, cm=cm: h.tensor_scalar(
                    cm, cm, recc, 0.5, AluOpType.mult, AluOpType.mult), [(c_ve, 1)])
                b.wait("ve", c_ve)
            ctm_done = c_ve.n
            b.wait("te", c_ve, ctm_done)
            cb_ev = {}
            for tt in range(16):
                pvt = (P4, P5)[tt % 2]
                if tt >= 2:
                    b.wait("te", c_ve, cb_ev[tt - 2])
                b.op("te", lambda h, tt=tt, pvt=pvt: h.transpose(
                    pvt[0:16, 0:128],
                    CTM(0, 128, tt * 16, (tt + 1) * 16), IDF[:]), [(c_te, 1)])
                ctr = c_te.n
                b.wait("ve", c_te, ctr)
                b.op("ve", lambda h, tt=tt, pvt=pvt: h.tensor_copy(
                    CBT(0, 16, tt * 128, (tt + 1) * 128), pvt[0:16, 0:128]), [(c_ve, 1)])
                cb_ev[tt] = c_ve.n
            cbt_done = c_ve.n
            b.wait("ge", c_ve, cbt_done)
            c_scw2 = b.ch("c_scw_" + tag, dma=True)
            b.op("ge", lambda h: h.dma_start(
                scr[scrow:scrow + 16, :], CBT(0, 16, 0, T)), [(c_scw2, 16)])
            b.wait("ge", c_scw2)

            b.wait("te", c_ve, shw_ve)
            b.wait("te", c_ve, lgt_done)
            sh_se = 0
            sh_hid_ve = 0
            sh_x_ve = {}
            sh_dn_te = {}
            for t in range(8):
                t0 = t * 256
                for grp in range(2):
                    if sh_se:
                        b.wait("te", c_se, sh_se)
                        b.wait("te", c_ve, sh_hid_ve)
                    for scn4 in range(4):
                        scn = grp * 4 + scn4
                        for cc in range(DC):
                            b.op("te", lambda h, scn=scn, scn4=scn4, cc=cc, t0=t0: h.matmul(
                                PA[:, scn4 * 256:(scn4 + 1) * 256],
                                SHGT(0, 128, cc * 1024 + scn * 128, cc * 1024 + (scn + 1) * 128),
                                xn_sl(cc, t0, 256),
                                start=(cc == 0), stop=(cc == DC - 1)),
                                [(c_te, 1)] if cc == DC - 1 else ())
                    for scn4 in range(4):
                        scn = grp * 4 + scn4
                        for cc in range(DC):
                            b.op("te", lambda h, scn=scn, scn4=scn4, cc=cc, t0=t0: h.matmul(
                                PB[:, scn4 * 256:(scn4 + 1) * 256],
                                SHUT(0, 128, cc * 1024 + scn * 128, cc * 1024 + (scn + 1) * 128),
                                xn_sl(cc, t0, 256),
                                start=(cc == 0), stop=(cc == DC - 1)),
                                [(c_te, 1)] if cc == DC - 1 else ())
                    gu = c_te.n
                    b.wait("se", c_te, gu)
                    if sh_hid_ve:
                        b.wait("se", c_ve, sh_hid_ve)
                    b.op("se", lambda h: h.activation(
                        HS(0, 128, 0, 1024), PA[:], AF.Silu), [(c_se, 1)])
                    sh_se = c_se.n
                    b.wait("ve", c_se, sh_se)
                    b.op("ve", lambda h: h.tensor_tensor(
                        HS(0, 128, 0, 1024), HS(0, 128, 0, 1024), PB[:],
                        AluOpType.mult), [(c_ve, 1)])
                    b.wait("ve", c_ve)
                    b.op("ve", lambda h, grp=grp: h.tensor_scalar(
                        SHHC(0, 128, grp * 1024, (grp + 1) * 1024),
                        HS(0, 128, 0, 1024), 0.5, None, AluOpType.mult), [(c_ve, 1)])
                    sh_hid_ve = c_ve.n
                b.wait("te", c_ve, sh_hid_ve)
                if t >= 2:
                    b.wait("te", c_ve, sh_x_ve[t - 2])
                for v in range(DC):
                    pvt = (P4, P5, P6, P7)[v // 2]
                    off = (v % 2) * 256
                    for scn in range(DC):
                        b.op("te", lambda h, pvt=pvt, off=off, scn=scn, v=v: h.matmul(
                            pvt[:, off:off + 256],
                            SHDT(0, 128, scn * 1024 + v * 128, scn * 1024 + (v + 1) * 128),
                            SHHC(0, 128, scn * 256, (scn + 1) * 256),
                            start=(scn == 0 and v % 2 == 0),
                            stop=(scn == DC - 1 and v % 2 == 1)),
                            [(c_te, 1)] if (scn == DC - 1 and v == DC - 1) else ())
                sh_dn_te[t] = c_te.n
                b.wait("ve", c_te, sh_dn_te[t])
                for bk in range(4):
                    b.op("ve", lambda h, bk=bk, t0=t0: h.tensor_tensor(
                        XSB[:].rearrange("p (c t) -> p c t", c=DC)[:, bk * 2: bk * 2 + 2, t0:t0 + 256],
                        XSB[:].rearrange("p (c t) -> p c t", c=DC)[:, bk * 2: bk * 2 + 2, t0:t0 + 256],
                        (P4, P5, P6, P7)[bk][:].rearrange("p (u t) -> p u t", u=2),
                        AluOpType.add), [(c_ve, 1)])
                sh_x_ve[t] = c_ve.n
            sh_done_te, sh_done_ve, sh_done_se = c_te.n, c_ve.n, c_se.n

            b.wait("ge", c_te, sh_done_te)
            b.wait("ge", c_ve, sh_done_ve)

            c_ew = [b.ch("c_ew0_" + tag, dma=True), b.ch("c_ew1_" + tag, dma=True)]
            c_cb = [b.ch("c_cb0_" + tag, dma=True), b.ch("c_cb1_" + tag, dma=True)]
            gate_r = w["l1_gate"].rearrange("e (cc p) hh -> e p cc hh", p=128)
            up_r = w["l1_up"].rearrange("e (cc p) hh -> e p cc hh", p=128)
            down_r = w["l1_down"].rearrange("e (hc p) v -> e p hc v", p=128)
            eslot = (ESL0, ESL1)
            b.wait("te", c_se, sh_done_se)
            b.wait("te", c_ve, sh_done_ve)
            b.wait("se", c_ve, sh_done_ve)
            e_te_done = {}
            e_ve_done = {}
            silu_se = {}
            hid_ve = {}
            dn_te = {}
            xadd_ve = {}
            for e in range(16):
                sl = e % 2
                ES = eslot[sl]
                CBCe = (CBC0, CBC1)[sl]
                if e >= 2:
                    b.wait("ge", c_te, e_te_done[e - 2])
                    b.wait("ge", c_ve, e_ve_done[e - 2])
                b.op("ge", lambda h, e=e, ES=ES: h.dma_start(
                    ES(0, 128, 0, 4096).rearrange("p (cc hh) -> p cc hh", cc=DC),
                    gate_r[e]), [(c_ew[sl], 16)])
                b.op("ge", lambda h, e=e, ES=ES: h.dma_start(
                    ES(0, 128, 4096, 8192).rearrange("p (cc hh) -> p cc hh", cc=DC),
                    up_r[e]), [(c_ew[sl], 16)])
                b.op("ge", lambda h, e=e, ES=ES: h.dma_start(
                    ES(0, 128, 8192, 12288).rearrange("p (hc v) -> p hc v", hc=4),
                    down_r[e]), [(c_ew[sl], 16)])
                b.op("ge", lambda h, e=e, CBCe=CBCe, scrow=scrow: h.dma_start(
                    CBCe(0, 128, 0, T),
                    scr[scrow + e:scrow + e + 1, :].broadcast_to((128, T))), [(c_cb[sl], 16)])
                b.wait("te", c_ew[sl], 16 * 3 * (e // 2 + 1))
                for t in range(8):
                    g = e * 8 + t
                    t0 = t * 256
                    if g >= 1:
                        b.wait("te", c_se, silu_se[g - 1])
                        b.wait("te", c_ve, hid_ve[g - 1])
                    for hc in range(4):
                        for cc in range(DC):
                            b.op("te", lambda h, hc=hc, cc=cc, t0=t0, ES=ES: h.matmul(
                                PA[:, hc * 256:(hc + 1) * 256],
                                ES(0, 128, cc * 512 + hc * 128, cc * 512 + (hc + 1) * 128),
                                xn_sl(cc, t0, 256),
                                start=(cc == 0), stop=(cc == DC - 1)),
                                [(c_te, 1)] if cc == DC - 1 else ())
                    for hc in range(4):
                        for cc in range(DC):
                            b.op("te", lambda h, hc=hc, cc=cc, t0=t0, ES=ES: h.matmul(
                                PB[:, hc * 256:(hc + 1) * 256],
                                ES(0, 128, 4096 + cc * 512 + hc * 128, 4096 + cc * 512 + (hc + 1) * 128),
                                xn_sl(cc, t0, 256),
                                start=(cc == 0), stop=(cc == DC - 1)),
                                [(c_te, 1)] if cc == DC - 1 else ())
                    gu_done = c_te.n
                    b.wait("se", c_te, gu_done)
                    if g >= 1:
                        b.wait("se", c_ve, hid_ve[g - 1])
                    b.op("se", lambda h: h.activation(
                        HS(0, 128, 0, 1024), PA[:], AF.Silu), [(c_se, 1)])
                    silu_se[g] = c_se.n
                    HD = (HID0, HID1)[g % 2]
                    b.wait("ve", c_se, silu_se[g])
                    if g >= 2:
                        b.wait("ve", c_te, dn_te[g - 2])
                    b.wait("ve", c_cb[sl], 16 * (e // 2 + 1))
                    b.op("ve", lambda h, HD=HD: h.tensor_tensor(
                        HD(0, 128, 0, 1024), HS(0, 128, 0, 1024), PB[:],
                        AluOpType.mult), [(c_ve, 1)])
                    b.wait("ve", c_ve)
                    for hc in range(4):
                        b.op("ve", lambda h, HD=HD, hc=hc, t0=t0, CBCe=CBCe: h.tensor_tensor(
                            HD(0, 128, hc * 256, (hc + 1) * 256),
                            HD(0, 128, hc * 256, (hc + 1) * 256),
                            CBCe(0, 128, t0, t0 + 256), AluOpType.mult), [(c_ve, 1)])
                    hid_ve[g] = c_ve.n
                    b.wait("te", c_ve, hid_ve[g])
                    if g >= 2:
                        b.wait("te", c_ve, xadd_ve[g - 2])
                    for v in range(DC):
                        pvt = (P4, P5, P6, P7)[v // 2]
                        off = (v % 2) * 256
                        for hc in range(4):
                            b.op("te", lambda h, pvt=pvt, off=off, hc=hc, v=v, ES=ES, HD=HD: h.matmul(
                                pvt[:, off:off + 256],
                                ES(0, 128, 8192 + hc * 1024 + v * 128, 8192 + hc * 1024 + (v + 1) * 128),
                                HD(0, 128, hc * 256, (hc + 1) * 256),
                                start=(hc == 0 and v % 2 == 0),
                                stop=(hc == 3 and v % 2 == 1)),
                                [(c_te, 1)] if (hc == 3 and v == DC - 1) else ())
                    dn_te[g] = c_te.n
                    b.wait("ve", c_te, dn_te[g])
                    for bk in range(4):
                        b.op("ve", lambda h, bk=bk, t0=t0: h.tensor_tensor(
                            XSB[:].rearrange("p (c t) -> p c t", c=DC)[:, bk * 2: bk * 2 + 2, t0:t0 + 256],
                            XSB[:].rearrange("p (c t) -> p c t", c=DC)[:, bk * 2: bk * 2 + 2, t0:t0 + 256],
                            (P4, P5, P6, P7)[bk][:].rearrange("p (u t) -> p u t", u=2),
                            AluOpType.add), [(c_ve, 1)])
                    xadd_ve[g] = c_ve.n
                e_te_done[e] = c_te.n
                e_ve_done[e] = c_ve.n
            return c_ve.n

        def emit_l1_conv():
            ar = Arena()
            W1B = ar.alloc(8192)
            W2B = ar.alloc(8192)
            HB = ar.alloc(16384)
            ETN = ar.alloc(1024, F32)
            ETT = ar.alloc(256, F32)
            DGT = ar.alloc(2560)
            MASK = ar.alloc(128)
            IOT = ar.alloc(512)
            BIASR = ar.alloc(1024)
            ONR = ar.alloc(512)

            entry_te, entry_ve, entry_se = c_te.n, c_ve.n, c_se.n
            xn_done = emit_layernorm()
            for _e in ("ge", "sy"):
                b.wait(_e, c_te, entry_te)
                b.wait(_e, c_ve, entry_ve)
                b.wait(_e, c_se, entry_se)
            b.dma_barrier()

            c_cw = b.ch("c_cw", dma=True)
            c_cwh = b.ch("c_cwh", dma=True)
            w1_r = w["l1_W1"].rearrange("(dc p) o -> p dc o", p=128)
            w2_r = w["l1_W2"].rearrange("(dc p) o -> p dc o", p=128)
            b.op("ge", lambda h: h.dma_start(
                W1B(0, 128, 0, 8192).rearrange("p (dc o) -> p dc o", dc=DC), w1_r),
                [(c_cw, 16)])
            b.op("ge", lambda h: h.dma_start(
                W2B(0, 128, 0, 8192).rearrange("p (dc o) -> p dc o", dc=DC), w2_r),
                [(c_cw, 16)])
            b.op("sy", lambda h: h.dma_start(ETN(0, 21, 0, 1024), w["l1_et"]), [(c_cwh, 16)])
            b.op("ge", lambda h: h.dma_start(
                BIASR(0, 1, 0, 1024), w["l1_bias"].rearrange("(o d) -> o d", o=1)),
                [(c_cw, 16)])
            b.op("ge", lambda h: h.memset(ONR(0, 1, 0, 512), 1.0), [(c_ge, 1)])
            pi32 = IOT(0, 128, 0, 256).bitcast(mybir.dt.int32)
            fi32 = IOT(0, 128, 256, 512).bitcast(mybir.dt.int32)
            b.op("ge", lambda h: h.iota(pi32, pattern=[[0, 128]], channel_multiplier=1),
                 [(c_ge, 1)])
            b.op("ge", lambda h: h.iota(fi32, pattern=[[1, 128]], channel_multiplier=0),
                 [(c_ge, 1)])
            init_ge2 = c_ge.n
            b.wait("ve", c_ge, init_ge2)
            b.op("ve", lambda h: h.tensor_tensor(
                MASK(0, 128, 0, 128), pi32, fi32, AluOpType.is_equal), [(c_ve, 1)])
            mask_done = c_ve.n

            b.wait("te", c_cw)
            b.wait("te", c_cwh)
            b.wait("te", c_ve, xn_done)
            for dc in range(DC):
                b.op("te", lambda h, dc=dc: h.transpose(
                    P6[:, dc * 32: dc * 32 + 21],
                    ETN(0, 21, dc * 128, (dc + 1) * 128), IDF[0:21, 0:21]),
                    [(c_te, 1)] if dc == DC - 1 else ())
            ett_tr = c_te.n
            b.wait("ve", c_te, ett_tr)
            b.op("ve", lambda h: h.tensor_copy(ETT(0, 128, 0, 256), P6[:, 0:256]),
                 [(c_ve, 1)])
            ett_done = c_ve.n

            hb_ev = {}
            cv_te = {}
            for dc in range(DC):
                b.wait("ve", c_ve, max(ett_done, mask_done))
                if dc >= 1:
                    b.wait("ve", c_te, cv_te[dc - 1])
                for m in range(1, 21):
                    b.op("ve", lambda h, dc=dc, m=m: h.tensor_scalar(
                        DGT(0, 128, (m - 1) * 128, m * 128),
                        MASK(0, 128, 0, 128),
                        ETT(0, 128, dc * 32 + m, dc * 32 + m + 1), None,
                        AluOpType.mult), [(c_ve, 1)])
                dg_done = c_ve.n
                b.wait("te", c_ve, dg_done)
                for t in range(NT):
                    t0 = t * 512
                    half = (dc * NT + t) % 2
                    if dc * NT + t >= 2:
                        b.wait("te", c_ve, hb_ev[dc * NT + t - 2])
                    for m in range(1, 21):
                        b.op("te", lambda h, dc=dc, m=m, t0=t0, half=half: h.matmul(
                            PA[:, half * 512:(half + 1) * 512],
                            DGT(0, 128, (m - 1) * 128, m * 128),
                            XN[:, dc * PADT + PADW + t0 - m: dc * PADT + PADW + t0 - m + 512],
                            start=(m == 1), stop=(m == 20)),
                            [(c_te, 1)] if m == 20 else ())
                    cv_te[dc] = c_te.n
                    b.wait("ve", c_te, cv_te[dc])
                    b.op("ve", lambda h, dc=dc, t0=t0, half=half: h.tensor_copy(
                        HB(0, 128, dc * T + t0, dc * T + t0 + 512),
                        PA[:, half * 512:(half + 1) * 512]), [(c_ve, 1)])
                    hb_ev[dc * NT + t] = c_ve.n
            conv_ve = c_ve.n

            b.wait("te", c_ve, conv_ve)
            wp_ev = {}
            for t in range(NT):
                t0 = t * 512
                for oc in range(DC):
                    gi = t * DC + oc
                    pvt = (P4, P5, P6)[gi % 3]
                    if gi >= 3:
                        b.wait("te", c_ve, wp_ev[gi - 3])
                    for dc in range(DC):
                        b.op("te", lambda h, dc=dc, oc=oc, t0=t0, pvt=pvt: h.matmul(
                            pvt[:, 0:512],
                            W1B(0, 128, dc * 1024 + oc * 128, dc * 1024 + (oc + 1) * 128),
                            xn_sl(dc, t0, 512),
                            start=(dc == 0), stop=False),
                            ())
                    for dc in range(DC):
                        b.op("te", lambda h, dc=dc, oc=oc, t0=t0, pvt=pvt: h.matmul(
                            pvt[:, 0:512],
                            W2B(0, 128, dc * 1024 + oc * 128, dc * 1024 + (oc + 1) * 128),
                            HB(0, 128, dc * T + t0, dc * T + t0 + 512),
                            start=False, stop=False),
                            ())
                    b.op("te", lambda h, oc=oc, pvt=pvt: h.matmul(
                        pvt[:, 0:512],
                        BIASR(0, 1, oc * 128, (oc + 1) * 128),
                        ONR(0, 1, 0, 512),
                        start=False, stop=True), [(c_te, 1)])
                    wpm = c_te.n
                    b.wait("ve", c_te, wpm)
                    b.op("ve", lambda h, oc=oc, t0=t0, pvt=pvt: h.tensor_tensor(
                        xsb_sl(oc, t0, 512), xsb_sl(oc, t0, 512), pvt[:, 0:512],
                        AluOpType.add), [(c_ve, 1)])
                    wp_ev[gi] = c_ve.n
            return c_ve.n

        res_marks = {}
        for ph in phases:
            if ph in ("l0ffn1", "l0ffn2"):
                emit_l0_ffn(ph)
            elif ph == "l0attn":
                emit_l0_attn()
            elif ph == "l1moe1":
                emit_l1_moe(ph, 16)
            elif ph == "l1moe2":
                emit_l1_moe(ph, 32)
            elif ph == "l1conv":
                emit_l1_conv()

        last_ve = c_ve.n
        b.wait("sy", c_ve, last_ve)
        yTr = yT.rearrange("(c p) t -> p c t", p=128)
        b.op("sy", lambda h: h.dma_start(
            yTr, XSB[:].rearrange("p (c t) -> p c t", c=DC)), [(c_out, 16)])
        b.wait("sy", c_out)

        blk = es.enter_context(nc.Block())
        b.emit(blk)
    return nc


def _mk_inmaps(inputs):
    import ml_dtypes
    x = inputs["L0_attn_x"]
    consts = {
        "identb": np.eye(128, dtype=ml_dtypes.bfloat16),
        "identf": np.eye(128, dtype=np.float32),
        "onesb": np.ones((128, 1), dtype=ml_dtypes.bfloat16),
        "onesfr": np.ones((1, 128), dtype=np.float32),
    }
    wkeys = [k for k in inputs if k != "L0_attn_x"]
    maps = []
    for core in range(8):
        m = {"xT": np.ascontiguousarray(x[core].T)}
        for k in wkeys:
            m[k] = np.ascontiguousarray(inputs[k])
        m.update(consts)
        maps.append(m)
    return maps


def kernel(**inputs):
    nc = build_program(["l0ffn1", "l0attn", "l0ffn2", "l1moe1", "l1conv", "l1moe2"])
    maps = _mk_inmaps(inputs)
    res = run_bass_kernel_spmd(nc, maps, core_ids=list(range(8)))
    out = np.stack([np.asarray(r["yT"]).T for r in res.results], axis=0)
    return np.ascontiguousarray(out.astype(np.float32))


# version 02
# speedup vs baseline: 1.4187x; 1.4187x over previous
import contextlib
import numpy as np
import concourse.bass as bass
import concourse.mybir as mybir
from concourse.bass_utils import run_bass_kernel_spmd
from concourse.alu_op_type import AluOpType

F32 = mybir.dt.float32
BF16 = mybir.dt.bfloat16
AF = mybir.ActivationFunctionType
AX = mybir.AxisListType

D = 1024
T = 2048
DC = D // 128
NT = T // 512
PADW = 32
PADT = T + PADW

ENGINES = ("te", "ve", "se", "sy", "ge")


class Chan:
    __slots__ = ("sem", "n", "name")

    def __init__(self, sem, name):
        self.sem = sem
        self.n = 0
        self.name = name


class Builder:
    def __init__(self, nc, es):
        self.nc = nc
        self.es = es
        self.ops = {e: [] for e in ENGINES}
        self.chans = {}
        self.dma_chans = []

    def ch(self, name, dma=False):
        if name not in self.chans:
            sem = self.es.enter_context(self.nc.semaphore(name))
            self.chans[name] = Chan(sem, name)
            if dma:
                self.dma_chans.append(self.chans[name])
        return self.chans[name]

    def dma_barrier(self, engines=("ge", "sy")):
        for c in self.dma_chans:
            if c.n:
                for e in engines:
                    self.wait(e, c)

    def op(self, eng, f, incs=()):
        for c, a in incs:
            c.n += a
        semincs = tuple((c.sem, a) for c, a in incs)

        def g(h, f=f, semincs=semincs):
            ins = f(h)
            for s, a in semincs:
                ins = ins.then_inc(s, a)
        self.ops[eng].append(g)

    def wait(self, eng, c, tgt=None):
        t = c.n if tgt is None else tgt
        assert t <= c.n, (c.name, t, c.n)
        self.ops[eng].append(lambda h, s=c.sem, t=t: h.wait_ge(s, t))

    def emit(self, blk):
        for name, attr in (("te", "tensor"), ("ve", "vector"), ("se", "scalar"),
                           ("sy", "sync"), ("ge", "gpsimd")):
            ops = self.ops[name]

            def section(h, ops=ops):
                for f in ops:
                    f(h)
            getattr(blk, attr)(section)


def ceil_div(a, b):
    return (a + b - 1) // b


def build_program(phases):
    nc = bass.Bass("TRN2", target_bir_lowering=False)

    def din(name, shape, dtype=F32):
        return nc.dram_tensor(name, list(shape), dtype, kind="ExternalInput").ap()

    xT = din("xT", (D, T))
    w = {}
    w["l0_wq"] = din("L0_attn_wq", (D, D))
    w["l0_wk"] = din("L0_attn_wk", (D, D))
    w["l0_wv"] = din("L0_attn_wv", (D, D))
    w["l0_wo"] = din("L0_attn_wo", (D, D))
    w["l0_sel"] = din("L0_ffn_expert_sel", (16, D))
    w["l0_keys"] = din("L0_ffn_keys", (16, D, 128))
    w["l0_vals"] = din("L0_ffn_values", (16, 128, D))
    w["l1_W1"] = din("L1_attn_W1", (D, D))
    w["l1_W2"] = din("L1_attn_W2", (D, D))
    w["l1_bias"] = din("L1_attn_bias", (D,))
    w["l1_et"] = din("L1_attn_embeddings_table", (21, D))
    w["l1_rw"] = din("L1_ffn_router_w", (16, D))
    w["l1_rb"] = din("L1_ffn_router_bias", (16,))
    w["l1_up"] = din("L1_ffn_up_proj", (16, D, 512))
    w["l1_gate"] = din("L1_ffn_gate_proj", (16, D, 512))
    w["l1_down"] = din("L1_ffn_down_proj", (16, 512, D))
    w["l1_shg"] = din("L1_ffn_sh_gate", (D, D))
    w["l1_shu"] = din("L1_ffn_sh_up", (D, D))
    w["l1_shd"] = din("L1_ffn_sh_down", (D, D))
    identb = din("identb", (128, 128), BF16)
    identf = din("identf", (128, 128), F32)
    onesb = din("onesb", (128, 1), BF16)
    onesfr = din("onesfr", (1, 128), F32)

    yT = nc.dram_tensor("yT", [D, T], F32, kind="ExternalOutput").ap()
    scr = nc.dram_tensor("scr", [64, T], F32, kind="Internal").ap()

    es = contextlib.ExitStack()
    with es:
        b = Builder(nc, es)
        sb = lambda name, shape, dtype: es.enter_context(nc.sbuf_tensor(name, list(shape), dtype))
        ps = lambda name, shape, dtype=F32: es.enter_context(nc.psum_tensor(name, list(shape), dtype))

        XSB = sb("XSB", (128, DC * T), F32)
        XN = sb("XN", (128, DC * PADT), BF16)
        AR = sb("AR", (128, 49152), BF16)
        IDB = sb("IDB", (128, 128), BF16)
        IDF = sb("IDF", (128, 128), F32)
        ONB = sb("ONB", (128, 1), BF16)
        ONFR = sb("ONFR", (1, 128), F32)
        EPS = sb("EPS", (128, 1), F32)
        NEG8 = sb("NEG8", (128, 1), F32)
        SMa = sb("SMa", (1, 512), F32)
        SMb = sb("SMb", (1, 512), F32)
        SMc = sb("SMc", (1, 512), F32)
        LNC = sb("LNC", (128, 4096), BF16)

        PA = ps("PA", (128, 1024))
        PB = ps("PB", (128, 1024))
        P4 = ps("P4", (128, 512))
        P5 = ps("P5", (128, 512))
        P6 = ps("P6", (128, 512))
        P7 = ps("P7", (128, 512))

        c_te = b.ch("c_te")
        c_ve = b.ch("c_ve")
        c_se = b.ch("c_se")
        c_ge = b.ch("c_ge")
        c_sy = b.ch("c_sy", dma=True)
        c_init = b.ch("c_init", dma=True)
        c_xl = b.ch("c_xl", dma=True)
        c_gw = b.ch("c_gw", dma=True)
        c_bc0 = b.ch("c_bc0", dma=True)
        c_bc1 = b.ch("c_bc1", dma=True)
        c_scw = b.ch("c_scw", dma=True)
        c_out = b.ch("c_out", dma=True)

        def xn_sl(c, t0, tw):
            return XN[:, c * PADT + PADW + t0: c * PADT + PADW + t0 + tw]

        def xsb_sl(c, t0, tw):
            return XSB[:, c * T + t0: c * T + t0 + tw]

        class Arena:
            def __init__(self):
                self.top = 0

            def alloc(self, cols, dtype=BF16):
                bcols = cols * (2 if dtype == F32 else 1)
                bcols = (bcols + 63) // 64 * 64
                c0 = self.top
                self.top += bcols
                assert self.top <= 49152, f"arena overflow {self.top}"
                if dtype == BF16:
                    return lambda p0, p1, f0, f1: AR[p0:p1, c0 + f0: c0 + f1]
                else:
                    arf = AR[:].bitcast(F32)
                    c0f = c0 // 2
                    return lambda p0, p1, f0, f1: arf[p0:p1, c0f + f0: c0f + f1]

            def reset(self, top=0):
                self.top = top

        b.op("sy", lambda h: h.dma_start(IDB[:], identb), [(c_init, 16)])
        b.op("sy", lambda h: h.dma_start(IDF[:], identf), [(c_init, 16)])
        b.op("sy", lambda h: h.dma_start(ONB[:], onesb), [(c_init, 16)])
        b.op("sy", lambda h: h.dma_start(ONFR[:], onesfr), [(c_init, 16)])
        b.op("ge", lambda h: h.memset(EPS[:], 1e-5), [(c_ge, 1)])
        b.op("ge", lambda h: h.memset(NEG8[:], -8.0), [(c_ge, 1)])
        for c in range(DC):
            b.op("ge", lambda h, c=c: h.memset(XN[:, c * PADT: c * PADT + PADW], 0.0),
                 [(c_ge, 1)])
        init_ge = c_ge.n

        xTr = xT.rearrange("(c p) t -> p c t", p=128)
        b.op("sy", lambda h: h.dma_start(
            XSB[:].rearrange("p (c t) -> p c t", c=DC), xTr), [(c_xl, 16)])

        b.wait("ve", c_xl)
        b.wait("te", c_init)
        b.wait("te", c_ge, init_ge)
        b.wait("se", c_ge, init_ge)

        def emit_layernorm():
            b.wait("ve", c_ve)
            b.wait("te", c_se)
            prev_sumsq = 0
            prev_affine = 0
            for t in range(NT):
                t0 = t * 512
                xv = XSB[:].rearrange("p (c t) -> p c t", c=DC)[:, :, t0:t0 + 512]
                lnc3 = LNC[:].rearrange("p (c t) -> p c t", c=DC)[:, :, 0:512]
                if t > 0:
                    b.wait("ve", c_te, prev_sumsq)
                    b.wait("ve", c_ve, prev_affine)
                b.op("ve", lambda h, xv=xv, lnc3=lnc3: h.tensor_copy(lnc3, xv), [(c_ve, 1)])
                cast_done = c_ve.n
                b.wait("te", c_ve, cast_done)
                for c in range(DC):
                    b.op("te", lambda h, c=c: h.matmul(
                        PA[0:1, 0:512], ONB[:], LNC[:, c * 512:(c + 1) * 512],
                        start=(c == 0), stop=(c == DC - 1)),
                        [(c_te, 1)] if c == DC - 1 else ())
                sum_done = c_te.n
                b.wait("se", c_ve, cast_done)
                b.wait("se", c_te, sum_done)
                b.op("se", lambda h, lnc3=lnc3: h.activation(
                    lnc3, lnc3, AF.Square), [(c_se, 1)])
                sq_done = c_se.n
                b.wait("te", c_se, sq_done)
                for c in range(DC):
                    b.op("te", lambda h, c=c: h.matmul(
                        PA[0:1, 512:1024], ONB[:], LNC[:, c * 512:(c + 1) * 512],
                        start=(c == 0), stop=(c == DC - 1)),
                        [(c_te, 1)] if c == DC - 1 else ())
                sumsq_done = c_te.n
                prev_sumsq = sumsq_done
                b.wait("ve", c_te, sum_done)
                b.op("ve", lambda h: h.tensor_scalar(
                    SMa[:], PA[0:1, 0:512], 1.0 / D, None, AluOpType.mult), [(c_ve, 1)])
                mean_done = c_ve.n
                b.wait("ve", c_te, sumsq_done)
                b.op("ve", lambda h: h.tensor_scalar(
                    SMb[:], PA[0:1, 512:1024], 1.0 / D, None, AluOpType.mult), [(c_ve, 1)])
                b.wait("se", c_ve, mean_done)
                b.op("se", lambda h: h.activation(SMc[:], SMa[:], AF.Square), [(c_se, 1)])
                b.wait("ve", c_se)
                b.wait("ve", c_ve)
                b.op("ve", lambda h: h.tensor_tensor(
                    SMb[:], SMb[:], SMc[:], AluOpType.subtract), [(c_ve, 1)])
                b.wait("se", c_ve)
                b.op("se", lambda h: h.activation(
                    SMb[:], SMb[:], AF.Sqrt, bias=EPS[0:1, :]), [(c_se, 1)])
                b.wait("ve", c_se)
                b.op("ve", lambda h: h.reciprocal(SMb[:], SMb[:]), [(c_ve, 1)])
                rstd_done = c_ve.n
                b.wait("ve", c_ve, rstd_done)
                b.op("ve", lambda h: h.tensor_tensor(
                    SMc[:], SMa[:], SMb[:], AluOpType.mult), [(c_ve, 1)])
                brow_done = c_ve.n
                b.wait("te", c_ve, brow_done)
                b.op("te", lambda h: h.matmul(PB[:, 0:512], ONFR[:], SMb[:], start=True, stop=True),
                     [(c_te, 1)])
                b.op("te", lambda h: h.matmul(PB[:, 512:1024], ONFR[:], SMc[:], start=True, stop=True),
                     [(c_te, 1)])
                rep_done = c_te.n
                b.wait("ve", c_te, rep_done)
                for c in range(DC):
                    b.op("ve", lambda h, c=c, t0=t0: h.tensor_tensor(
                        LNC[:, c * 512:(c + 1) * 512],
                        XSB[:, c * T + t0: c * T + t0 + 512],
                        PB[:, 0:512], AluOpType.mult), [(c_ve, 1)])
                mul_done = c_ve.n
                b.wait("ve", c_ve, mul_done)
                for c in range(DC):
                    b.op("ve", lambda h, c=c, t0=t0: h.tensor_tensor(
                        xn_sl(c, t0, 512), LNC[:, c * 512:(c + 1) * 512],
                        PB[:, 512:1024], AluOpType.subtract), [(c_ve, 1)])
                prev_affine = c_ve.n
            return c_ve.n

        def emit_l0_ffn(tag):
            ar = Arena()
            KEY = ar.alloc(16 * 1024)
            VAL = ar.alloc(16 * 1024)
            EST = ar.alloc(128)
            ESN = ar.alloc(1024)
            SC = ar.alloc(16 * 512)
            SBC0 = ar.alloc(512)
            SBC1 = ar.alloc(512)
            SELF = ar.alloc(2048, F32)

            entry_te, entry_ve, entry_se = c_te.n, c_ve.n, c_se.n
            xn_done = emit_layernorm()

            for _e in ("ge", "sy"):
                b.wait(_e, c_te, entry_te)
                b.wait(_e, c_ve, entry_ve)
                b.wait(_e, c_se, entry_se)
            b.dma_barrier()
            keys_r = w["l0_keys"].rearrange("e (c p) hh -> e p c hh", p=128)
            vals_r = w["l0_vals"]
            for e in range(16):
                b.op("ge", lambda h, e=e: h.dma_start(
                    KEY(0, 128, e * 1024, (e + 1) * 1024).rearrange("p (c hh) -> p c hh", c=DC),
                    keys_r[e]), [(c_gw, 16)])
                b.op("ge", lambda h, e=e: h.dma_start(
                    VAL(0, 128, e * 1024, (e + 1) * 1024), vals_r[e]), [(c_gw, 16)])
            b.op("ge", lambda h: h.dma_start(
                ESN(0, 16, 0, 1024), w["l0_sel"]), [(c_gw, 16)])
            wload = c_gw.n

            b.wait("te", c_gw, wload)
            pa_b = PA[:].bitcast(BF16)
            for c in range(DC):
                b.op("te", lambda h, c=c: h.transpose(
                    pa_b[:, c * 16:(c + 1) * 16],
                    ESN(0, 16, c * 128, (c + 1) * 128), IDB[0:16, 0:16]),
                    [(c_te, 1)] if c == DC - 1 else ())
            est_tr = c_te.n
            b.wait("ve", c_te, est_tr)
            b.op("ve", lambda h: h.tensor_copy(EST(0, 128, 0, 128), pa_b[:, 0:128]),
                 [(c_ve, 1)])
            est_done = c_ve.n

            b.wait("te", c_ve, xn_done)
            b.wait("te", c_ve, est_done)
            sig_mark = {0: 0, 1: 0}
            for t in range(NT):
                t0 = t * 512
                if sig_mark[t % 2]:
                    b.wait("te", c_se, sig_mark[t % 2])
                for c in range(DC):
                    b.op("te", lambda h, c=c, t0=t0, t=t: h.matmul(
                        PB[0:16, (t % 2) * 512:(t % 2) * 512 + 512],
                        EST(0, 128, c * 16, (c + 1) * 16),
                        xn_sl(c, t0, 512),
                        start=(c == 0), stop=(c == DC - 1)),
                        [(c_te, 1)] if c == DC - 1 else ())
                lg = c_te.n
                b.wait("se", c_te, lg)
                b.op("se", lambda h, t=t, t0=t0: h.activation(
                    SELF(0, 16, t0, t0 + 512), PB[0:16, (t % 2) * 512:(t % 2) * 512 + 512],
                    AF.Sigmoid), [(c_se, 1)])
                sig_mark[t % 2] = c_se.n
            sel_sig = c_se.n
            b.wait("ve", c_se, sel_sig)
            b.op("ve", lambda h: h.tensor_scalar(
                SELF(0, 16, 0, T), SELF(0, 16, 0, T), 0.5, None, AluOpType.mult),
                [(c_ve, 1)])
            sel_done = c_ve.n
            b.wait("ge", c_ve, sel_done)
            b.op("ge", lambda h: h.dma_start(scr[0:16, :], SELF(0, 16, 0, T)), [(c_scw, 16)])
            b.wait("ge", c_scw)

            se_relu_mark = {0: 0, 1: 0}
            sbc_lastread = {0: 0, 1: 0}
            bc_target = {}
            vdone_mark = {}
            val_te_mark = 0
            for t in range(NT):
                t0 = t * 512
                for e in range(16):
                    idx = t * 16 + e
                    half = e % 2
                    bc = (c_bc0, c_bc1)[idx % 2]
                    sbc = (SBC0, SBC1)[idx % 2]
                    if sbc_lastread[idx % 2]:
                        b.wait("ge", c_ve, sbc_lastread[idx % 2])
                    b.op("ge", lambda h, e=e, t0=t0, sbc=sbc: h.dma_start(
                        sbc(0, 128, 0, 512),
                        scr[e:e + 1, t0:t0 + 512].broadcast_to((128, 512))), [(bc, 16)])
                    bc_target[idx] = bc.n
                    if se_relu_mark[half]:
                        b.wait("te", c_se, se_relu_mark[half])
                    if e == 0 and val_te_mark:
                        b.wait("se", c_te, val_te_mark)
                    for c in range(DC):
                        b.op("te", lambda h, c=c, e=e, t0=t0, half=half: h.matmul(
                            PA[:, half * 512: half * 512 + 512],
                            KEY(0, 128, e * 1024 + c * 128, e * 1024 + (c + 1) * 128),
                            xn_sl(c, t0, 512),
                            start=(c == 0), stop=(c == DC - 1)),
                            [(c_te, 1)] if c == DC - 1 else ())
                    key_done = c_te.n
                    b.wait("se", c_te, key_done)
                    b.op("se", lambda h, e=e, half=half: h.activation(
                        SC(0, 128, e * 512, (e + 1) * 512),
                        PA[:, half * 512: half * 512 + 512], AF.Relu), [(c_se, 1)])
                    se_relu_mark[half] = c_se.n
                    b.wait("ve", bc, bc_target[idx])
                    b.wait("ve", c_se, se_relu_mark[half])
                    b.op("ve", lambda h, e=e, sbc=sbc: h.tensor_tensor(
                        SC(0, 128, e * 512, (e + 1) * 512),
                        SC(0, 128, e * 512, (e + 1) * 512),
                        sbc(0, 128, 0, 512), AluOpType.mult), [(c_ve, 1)])
                    sbc_lastread[idx % 2] = c_ve.n
                sc_ready = c_ve.n
                b.wait("te", c_ve, sc_ready)
                for v in range(DC):
                    vidx = t * DC + v
                    pv = (P4, P5, P6)[vidx % 3]
                    if vidx >= 3:
                        b.wait("te", c_ve, vdone_mark[vidx - 3])
                    for e in range(16):
                        b.op("te", lambda h, pv=pv, v=v, e=e: h.matmul(
                            pv[:, 0:512],
                            VAL(0, 128, e * 1024 + v * 128, e * 1024 + (v + 1) * 128),
                            SC(0, 128, e * 512, (e + 1) * 512),
                            start=(e == 0), stop=(e == 15)),
                            [(c_te, 1)] if e == 15 else ())
                    val_done = c_te.n
                    b.wait("ve", c_te, val_done)
                    b.op("ve", lambda h, pv=pv, v=v, t0=t0: h.tensor_tensor(
                        xsb_sl(v, t0, 512), xsb_sl(v, t0, 512), pv[:, 0:512],
                        AluOpType.add), [(c_ve, 1)])
                    vdone_mark[vidx] = c_ve.n
                val_te_mark = c_te.n
            return c_ve.n

        def emit_l0_attn():
            ar = Arena()
            WQT = ar.alloc(8192)
            WKT = ar.alloc(8192)
            WVT = ar.alloc(8192)
            WOT = ar.alloc(8192)
            WSTG = ar.alloc(2048)
            QTM = ar.alloc(1024)
            KTM = ar.alloc(1024)
            QTBe = ar.alloc(1024)
            QTBo = ar.alloc(1024)
            KTBe = ar.alloc(1024)
            KTBo = ar.alloc(1024)
            VTM = ar.alloc(1040)
            PT0 = ar.alloc(1024)
            PT1 = ar.alloc(1024)
            AOTh = ar.alloc(2048)
            RBC = ar.alloc(1024, F32)

            P6b = P6[:].bitcast(BF16)
            P7b = P7[:].bitcast(BF16)

            entry_te, entry_ve, entry_se = c_te.n, c_ve.n, c_se.n
            xn_done = emit_layernorm()
            for _e in ("ge", "sy"):
                b.wait(_e, c_te, entry_te)
                b.wait(_e, c_ve, entry_ve)
                b.wait(_e, c_se, entry_se)
            b.dma_barrier()

            wsrc = [w["l0_wq"], w["l0_wk"], w["l0_wv"], w["l0_wo"]]
            wdst = [WQT, WKT, WVT, WOT]
            c_ws = [b.ch("c_ws0", dma=True), b.ch("c_ws1", dma=True)]
            stg_te_mark = {}
            evac_ve_mark = {}
            b.wait("te", c_ve, entry_ve)
            b.wait("te", c_se, entry_se)
            for idx in range(32):
                wi, oc = idx // 8, idx % 8
                slot = idx % 2
                if idx >= 2:
                    b.wait("ge", c_te, stg_te_mark[idx - 2])
                b.op("ge", lambda h, wi=wi, oc=oc, slot=slot: h.dma_start(
                    WSTG(0, 128, slot * 1024, slot * 1024 + 1024),
                    wsrc[wi][oc * 128:(oc + 1) * 128, :]), [(c_ws[slot], 16)])
                b.wait("te", c_ws[slot], 16 * (idx // 2 + 1))
                if idx >= 2:
                    b.wait("te", c_ve, evac_ve_mark[idx - 2])
                pb_t = (P6b, P7b)[slot]
                for cc in range(DC):
                    b.op("te", lambda h, cc=cc, slot=slot, pb_t=pb_t: h.transpose(
                        pb_t[:, cc * 128:(cc + 1) * 128],
                        WSTG(0, 128, slot * 1024 + cc * 128, slot * 1024 + (cc + 1) * 128),
                        IDB[:]), [(c_te, 1)] if cc == DC - 1 else ())
                stg_te_mark[idx] = c_te.n
                b.wait("ve", c_te, stg_te_mark[idx])
                dst = wdst[wi]
                b.op("ve", lambda h, dst=dst, oc=oc, pb_t=pb_t: h.tensor_copy(
                    dst(0, 128, 0, 8192).rearrange("p (cc o) -> p cc o", cc=DC)[
                        :, :, oc * 128:(oc + 1) * 128],
                    pb_t.rearrange("p (cc s) -> p cc s", cc=DC)), [(c_ve, 1)])
                evac_ve_mark[idx] = c_ve.n
            wprep_ve = c_ve.n

            b.op("ge", lambda h: h.memset(
                VTM(0, 128, 0, 1040).rearrange("p (g dd) -> p g dd", dd=65)[:, :, 64:65],
                1.0), [(c_ge, 1)])
            vones = c_ge.n
            b.wait("te", c_ge, vones)
            b.wait("te", c_ve, wprep_ve)
            b.wait("te", c_ve, xn_done)

            NEG8l = NEG8
            exp_mark = {}
            pt_av_mark = {}
            for h16 in range(16):
                hcol = 128 * h16
                aoth0 = (h16 % 2) * 1024
                for half in range(2):
                    for cc in range(DC):
                        b.op("te", lambda h, cc=cc, half=half, hcol=hcol: h.matmul(
                            PA[:, half * 512:(half + 1) * 512],
                            xn_sl(cc, hcol, 128),
                            WQT(0, 128, cc * 1024 + half * 512, cc * 1024 + (half + 1) * 512),
                            start=(cc == 0), stop=(cc == DC - 1)),
                            [(c_te, 1)] if cc == DC - 1 else ())
                q_done = c_te.n
                for half in range(2):
                    for cc in range(DC):
                        b.op("te", lambda h, cc=cc, half=half, hcol=hcol: h.matmul(
                            PB[:, half * 512:(half + 1) * 512],
                            xn_sl(cc, hcol, 128),
                            WKT(0, 128, cc * 1024 + half * 512, cc * 1024 + (half + 1) * 512),
                            start=(cc == 0), stop=(cc == DC - 1)),
                            [(c_te, 1)] if cc == DC - 1 else ())
                k_done = c_te.n
                b.wait("ve", c_te, q_done)
                b.op("ve", lambda h: h.tensor_copy(QTM(0, 128, 0, 1024), PA[:]), [(c_ve, 1)])
                q_ev = c_ve.n
                b.wait("ve", c_te, k_done)
                b.op("ve", lambda h: h.tensor_copy(KTM(0, 128, 0, 1024), PB[:]), [(c_ve, 1)])
                k_ev = c_ve.n
                b.wait("te", c_ve, q_ev)
                for half in range(2):
                    for cc in range(DC):
                        b.op("te", lambda h, cc=cc, half=half, hcol=hcol: h.matmul(
                            PA[:, half * 512:(half + 1) * 512],
                            xn_sl(cc, hcol, 128),
                            WVT(0, 128, cc * 1024 + half * 512, cc * 1024 + (half + 1) * 512),
                            start=(cc == 0), stop=(cc == DC - 1)),
                            [(c_te, 1)] if cc == DC - 1 else ())
                v_done = c_te.n
                b.wait("ve", c_te, v_done)
                for half in range(2):
                    b.op("ve", lambda h, half=half: h.tensor_copy(
                        VTM(0, 128, half * 520, half * 520 + 520).rearrange(
                            "p (g dd) -> p g dd", dd=65)[:, :, 0:64],
                        PA[:, half * 512:(half + 1) * 512].rearrange(
                            "p (g dd) -> p g dd", dd=64)), [(c_ve, 1)])
                v_ev = c_ve.n
                b.wait("te", c_ve, q_ev)
                for bb in range(DC):
                    b.op("te", lambda h, bb=bb: h.transpose(
                        P6b[:, bb * 128:(bb + 1) * 128],
                        QTM(0, 128, bb * 128, (bb + 1) * 128), IDB[:]),
                        [(c_te, 1)] if bb == DC - 1 else ())
                qtr = c_te.n
                b.wait("ve", c_te, qtr)
                b.op("ve", lambda h: h.tensor_copy(QTBe(0, 64, 0, 1024), P6b[0:64, :]), [(c_ve, 1)])
                b.op("ve", lambda h: h.tensor_copy(QTBo(0, 64, 0, 1024), P6b[64:128, :]), [(c_ve, 1)])
                qtb_ev = c_ve.n
                b.wait("te", c_ve, k_ev)
                for bb in range(DC):
                    b.op("te", lambda h, bb=bb: h.transpose(
                        P7b[:, bb * 128:(bb + 1) * 128],
                        KTM(0, 128, bb * 128, (bb + 1) * 128), IDB[:]),
                        [(c_te, 1)] if bb == DC - 1 else ())
                ktr = c_te.n
                b.wait("ve", c_te, ktr)
                b.op("ve", lambda h: h.tensor_copy(KTBe(0, 64, 0, 1024), P7b[0:64, :]), [(c_ve, 1)])
                b.op("ve", lambda h: h.tensor_copy(KTBo(0, 64, 0, 1024), P7b[64:128, :]), [(c_ve, 1)])
                ktb_ev = c_ve.n
                b.wait("te", c_ve, ktb_ev)
                b.wait("se", c_ve, qtb_ev)
                b.wait("se", c_ve, v_ev)
                for pi in range(2):
                    QTB = (QTBe, QTBo)[pi]
                    avp = [(PB, PB), (P4, P5)][pi]
                    avsl = [((0, 512), (512, 1024)), ((0, 512), (0, 512))][pi]
                    for jg in range(16):
                        KTB = (KTBe, KTBo)[jg % 2]
                        jb = jg // 2
                        if exp_mark.get("cur"):
                            b.wait("te", c_se, exp_mark["cur"])
                        for ih in range(2):
                            b.op("te", lambda h, KTB=KTB, jb=jb, QTB=QTB, ih=ih: h.matmul(
                                PA[:, ih * 512:(ih + 1) * 512],
                                KTB(0, 64, jb * 128, (jb + 1) * 128),
                                QTB(0, 64, ih * 512, (ih + 1) * 512),
                                start=True, stop=True),
                                [(c_te, 1)] if ih == 1 else ())
                        qk_done = c_te.n
                        pt = (PT0, PT1)[jg % 2]
                        b.wait("se", c_te, qk_done)
                        if pt_av_mark.get(jg % 2):
                            b.wait("se", c_te, pt_av_mark[jg % 2])
                        b.op("se", lambda h, pt=pt: h.activation(
                            pt(0, 128, 0, 1024), PA[:], AF.Exp,
                            bias=NEG8l[:], scale=0.125), [(c_se, 1)])
                        exp_mark["cur"] = c_se.n
                        b.wait("te", c_se, exp_mark["cur"])
                        for ih in range(2):
                            pvt, (s0, s1) = avp[ih], avsl[ih]
                            b.op("te", lambda h, pvt=pvt, s0=s0, s1=s1, jg=jg, pt=pt, ih=ih: h.matmul(
                                pvt[0:65, s0:s1],
                                VTM(0, 128, jg * 65, (jg + 1) * 65),
                                pt(0, 128, ih * 512, (ih + 1) * 512),
                                start=(jg == 0), stop=(jg == 15)),
                                [(c_te, 1)] if ih == 1 else ())
                        pt_av_mark[jg % 2] = c_te.n
                    av_done = c_te.n
                    b.wait("ve", c_te, av_done)
                    b.op("ve", lambda h, avp=avp, avsl=avsl: h.reciprocal(
                        SMa[:], avp[0][64:65, avsl[0][0]:avsl[0][1]]), [(c_ve, 1)])
                    b.op("ve", lambda h, avp=avp, avsl=avsl: h.reciprocal(
                        SMb[:], avp[1][64:65, avsl[1][0]:avsl[1][1]]), [(c_ve, 1)])
                    rec_done = c_ve.n
                    b.wait("te", c_ve, rec_done)
                    b.op("te", lambda h: h.matmul(P6[0:64, :], ONFR[0:1, 0:64], SMa[:],
                                                  start=True, stop=True), [(c_te, 1)])
                    b.op("te", lambda h: h.matmul(P7[0:64, :], ONFR[0:1, 0:64], SMb[:],
                                                  start=True, stop=True), [(c_te, 1)])
                    rep_done = c_te.n
                    b.wait("ve", c_te, rep_done)
                    b.op("ve", lambda h: h.tensor_copy(RBC(0, 64, 0, 512), P6[0:64, :]), [(c_ve, 1)])
                    b.op("ve", lambda h: h.tensor_copy(RBC(0, 64, 512, 1024), P7[0:64, :]), [(c_ve, 1)])
                    rbc_done = c_ve.n
                    b.wait("ve", c_ve, rbc_done)
                    for ih in range(2):
                        pvt, (s0, s1) = avp[ih], avsl[ih]
                        b.op("ve", lambda h, pvt=pvt, s0=s0, s1=s1, pi=pi, ih=ih, aoth0=aoth0: h.tensor_tensor(
                            AOTh(64 * pi, 64 * pi + 64, aoth0 + ih * 512, aoth0 + (ih + 1) * 512),
                            pvt[64 * 0:64, s0:s1] if False else pvt[0:64, s0:s1],
                            RBC(0, 64, ih * 512, (ih + 1) * 512),
                            AluOpType.mult), [(c_ve, 1)])
                    evac_done = c_ve.n
                    if pi == 0:
                        b.wait("te", c_ve, evac_done)
                aoth_done = c_ve.n
                b.wait("te", c_ve, aoth_done)
                opev = {}
                for ccp in range(DC):
                    bk = (ccp % 2) * 512
                    if ccp >= 2:
                        b.wait("te", c_ve, opev[ccp - 2])
                    for oc in range(DC):
                        b.op("te", lambda h, ccp=ccp, oc=oc, aoth0=aoth0, bk=bk: h.matmul(
                            PB[:, bk:bk + 128],
                            WOT(0, 128, oc * 1024 + ccp * 128, oc * 1024 + (ccp + 1) * 128),
                            AOTh(0, 128, aoth0 + oc * 128, aoth0 + (oc + 1) * 128),
                            start=(oc == 0), stop=(oc == DC - 1)),
                            [(c_te, 1)] if oc == DC - 1 else ())
                    op_done = c_te.n
                    b.wait("ve", c_te, op_done)
                    b.op("ve", lambda h, ccp=ccp, hcol=hcol, bk=bk: h.tensor_tensor(
                        xsb_sl(ccp, hcol, 128), xsb_sl(ccp, hcol, 128),
                        PB[:, bk:bk + 128],
                        AluOpType.add), [(c_ve, 1)])
                    opev[ccp] = c_ve.n
                oproj_ve = c_ve.n
                b.wait("te", c_ve, oproj_ve)
            return c_ve.n

        def emit_l1_moe(tag, scrow):
            ar = Arena()
            CBC0 = ar.alloc(2048)
            CBC1 = ar.alloc(2048)
            RWT = ar.alloc(128)
            RWN = ar.alloc(1024)
            LGT = ar.alloc(2048, F32)
            LTM = ar.alloc(256, F32)
            CTM = ar.alloc(256, F32)
            CBT = ar.alloc(2048, F32)
            RB16 = ar.alloc(1, F32)
            SCL = ar.alloc(8, F32)
            HS = ar.alloc(1024)
            HID0 = ar.alloc(1024)
            HID1 = ar.alloc(1024)
            SHHC = ar.alloc(2048)
            ovl = ar.top
            SHGT = ar.alloc(8192)
            SHUT = ar.alloc(8192)
            SHDT = ar.alloc(8192)
            WSTG2 = ar.alloc(2048)
            ar.reset(ovl)
            ESL0 = ar.alloc(12288)
            ESL1 = ar.alloc(12288)

            P6b = P6[:].bitcast(BF16)
            P7b = P7[:].bitcast(BF16)
            m1c = SCL(0, 128, 0, 1)
            m2c = SCL(0, 128, 1, 2)
            nm1c = SCL(0, 128, 2, 3)
            recc = SCL(0, 128, 3, 4)

            entry_te, entry_ve, entry_se = c_te.n, c_ve.n, c_se.n
            xn_done = emit_layernorm()
            b.wait("ge", c_te, entry_te)
            b.wait("ge", c_ve, entry_ve)
            b.wait("ge", c_se, entry_se)

            c_rt = b.ch("c_rt_" + tag, dma=True)
            b.op("ge", lambda h: h.dma_start(RWN(0, 16, 0, 1024), w["l1_rw"]), [(c_rt, 16)])
            b.op("ge", lambda h: h.dma_start(
                RB16(0, 16, 0, 1), w["l1_rb"].rearrange("(e o) -> e o", o=1)), [(c_rt, 16)])

            c_sh = [b.ch("c_sh0_" + tag, dma=True), b.ch("c_sh1_" + tag, dma=True)]
            stg_te = {}
            evac_ve = {}
            shsrc = [w["l1_shg"], w["l1_shu"], w["l1_shd"]]
            shdst = [SHGT, SHUT, SHDT]
            b.wait("te", c_ve, entry_ve)
            b.wait("te", c_se, entry_se)
            for idx in range(24):
                wi, oc = idx // 8, idx % 8
                slot = idx % 2
                if idx >= 2:
                    b.wait("ge", c_te, stg_te[idx - 2])
                b.op("ge", lambda h, wi=wi, oc=oc, slot=slot: h.dma_start(
                    WSTG2(0, 128, slot * 1024, slot * 1024 + 1024),
                    shsrc[wi][oc * 128:(oc + 1) * 128, :]), [(c_sh[slot], 16)])
                b.wait("te", c_sh[slot], 16 * (idx // 2 + 1))
                if idx >= 2:
                    b.wait("te", c_ve, evac_ve[idx - 2])
                pb_t = (P6b, P7b)[slot]
                for cc in range(DC):
                    b.op("te", lambda h, cc=cc, slot=slot, pb_t=pb_t: h.transpose(
                        pb_t[:, cc * 128:(cc + 1) * 128],
                        WSTG2(0, 128, slot * 1024 + cc * 128, slot * 1024 + (cc + 1) * 128),
                        IDB[:]), [(c_te, 1)] if cc == DC - 1 else ())
                stg_te[idx] = c_te.n
                b.wait("ve", c_te, stg_te[idx])
                b.op("ve", lambda h, wi=wi, oc=oc, pb_t=pb_t: h.tensor_copy(
                    shdst[wi](0, 128, 0, 8192).rearrange("p (cc o) -> p cc o", cc=DC)[
                        :, :, oc * 128:(oc + 1) * 128],
                    pb_t.rearrange("p (cc s) -> p cc s", cc=DC)), [(c_ve, 1)])
                evac_ve[idx] = c_ve.n
            shw_ve = c_ve.n

            b.wait("te", c_rt)
            b.wait("te", c_ve, xn_done)
            b.wait("te", c_ve, shw_ve)
            for cc in range(DC):
                b.op("te", lambda h, cc=cc: h.transpose(
                    P6b[:, cc * 16:(cc + 1) * 16],
                    RWN(0, 16, cc * 128, (cc + 1) * 128), IDB[0:16, 0:16]),
                    [(c_te, 1)] if cc == DC - 1 else ())
            rwt_tr = c_te.n
            b.wait("ve", c_te, rwt_tr)
            b.op("ve", lambda h: h.tensor_copy(RWT(0, 128, 0, 128), P6b[:, 0:128]), [(c_ve, 1)])
            rwt_done = c_ve.n
            b.wait("te", c_ve, rwt_done)
            lg_se = {0: 0, 1: 0}
            for t in range(NT):
                t0 = t * 512
                if lg_se[t % 2]:
                    b.wait("te", c_ve, lg_se[t % 2])
                for cc in range(DC):
                    b.op("te", lambda h, cc=cc, t0=t0, t=t: h.matmul(
                        PB[0:16, (t % 2) * 512:(t % 2) * 512 + 512],
                        RWT(0, 128, cc * 16, (cc + 1) * 16),
                        xn_sl(cc, t0, 512),
                        start=(cc == 0), stop=(cc == DC - 1)),
                        [(c_te, 1)] if cc == DC - 1 else ())
                lgm = c_te.n
                b.wait("ve", c_te, lgm)
                b.wait("ve", c_rt)
                b.op("ve", lambda h, t=t, t0=t0: h.tensor_scalar(
                    LGT(0, 16, t0, t0 + 512), PB[0:16, (t % 2) * 512:(t % 2) * 512 + 512],
                    RB16(0, 16, 0, 1), None, AluOpType.add), [(c_ve, 1)])
                lg_se[t % 2] = c_ve.n
            lgt_done = c_ve.n
            b.wait("te", c_ve, lgt_done)
            for tt in range(16):
                pvt = (P4, P5)[tt // 8]
                b.op("te", lambda h, tt=tt, pvt=pvt: h.transpose(
                    pvt[:, (tt % 8) * 16:(tt % 8) * 16 + 16],
                    LGT(0, 16, tt * 128, (tt + 1) * 128), IDF[0:16, 0:16]),
                    [(c_te, 1)] if tt in (7, 15) else ())
            ltm_tr = c_te.n
            b.wait("ve", c_te, ltm_tr)
            b.op("ve", lambda h: h.tensor_copy(LTM(0, 128, 0, 128), P4[:, 0:128]), [(c_ve, 1)])
            b.op("ve", lambda h: h.tensor_copy(LTM(0, 128, 128, 256), P5[:, 0:128]), [(c_ve, 1)])
            ltm_done = c_ve.n
            b.wait("ve", c_ve, ltm_done)
            for tt in range(16):
                lg = LTM(0, 128, tt * 16, (tt + 1) * 16)
                cm = CTM(0, 128, tt * 16, (tt + 1) * 16)
                b.op("ve", lambda h, lg=lg: h.reduce_max(m1c, lg, axis=AX.X), [(c_ve, 1)])
                b.wait("ve", c_ve)
                b.op("ve", lambda h, lg=lg, cm=cm: h.tensor_scalar(
                    cm, lg, m1c, -1e30, AluOpType.is_ge, AluOpType.mult), [(c_ve, 1)])
                b.wait("ve", c_ve)
                b.op("ve", lambda h, lg=lg, cm=cm: h.tensor_tensor(
                    cm, cm, lg, AluOpType.add), [(c_ve, 1)])
                b.wait("ve", c_ve)
                b.op("ve", lambda h, cm=cm: h.reduce_max(m2c, cm, axis=AX.X), [(c_ve, 1)])
                b.op("ve", lambda h: h.tensor_scalar(
                    nm1c, m1c, -1.0, None, AluOpType.mult), [(c_ve, 1)])
                nm1_done = c_ve.n
                b.wait("se", c_ve, nm1_done)
                b.op("se", lambda h, lg=lg, cm=cm: h.activation(
                    cm, lg, AF.Exp, bias=nm1c), [(c_se, 1)])
                eb_done = c_se.n
                b.wait("ve", c_se, eb_done)
                b.op("ve", lambda h, lg=lg: h.tensor_scalar(
                    lg, lg, m2c, None, AluOpType.is_ge), [(c_ve, 1)])
                b.wait("ve", c_ve)
                b.op("ve", lambda h, lg=lg, cm=cm: h.tensor_tensor(
                    cm, cm, lg, AluOpType.mult), [(c_ve, 1)])
                b.wait("ve", c_ve)
                b.op("ve", lambda h, cm=cm: h.reduce_sum(recc, cm, axis=AX.X), [(c_ve, 1)])
                b.wait("ve", c_ve)
                b.op("ve", lambda h: h.reciprocal(recc, recc), [(c_ve, 1)])
                b.wait("ve", c_ve)
                b.op("ve", lambda h, cm=cm: h.tensor_scalar(
                    cm, cm, recc, 0.5, AluOpType.mult, AluOpType.mult), [(c_ve, 1)])
                b.wait("ve", c_ve)
            ctm_done = c_ve.n
            b.wait("te", c_ve, ctm_done)
            cb_ev = {}
            for tt in range(16):
                pvt = (P4, P5)[tt % 2]
                if tt >= 2:
                    b.wait("te", c_ve, cb_ev[tt - 2])
                b.op("te", lambda h, tt=tt, pvt=pvt: h.transpose(
                    pvt[0:16, 0:128],
                    CTM(0, 128, tt * 16, (tt + 1) * 16), IDF[:]), [(c_te, 1)])
                ctr = c_te.n
                b.wait("ve", c_te, ctr)
                b.op("ve", lambda h, tt=tt, pvt=pvt: h.tensor_copy(
                    CBT(0, 16, tt * 128, (tt + 1) * 128), pvt[0:16, 0:128]), [(c_ve, 1)])
                cb_ev[tt] = c_ve.n
            cbt_done = c_ve.n
            b.wait("ge", c_ve, cbt_done)
            c_scw2 = b.ch("c_scw_" + tag, dma=True)
            b.op("ge", lambda h: h.dma_start(
                scr[scrow:scrow + 16, :], CBT(0, 16, 0, T)), [(c_scw2, 16)])
            b.wait("ge", c_scw2)

            b.wait("te", c_ve, shw_ve)
            b.wait("te", c_ve, lgt_done)
            sh_se = 0
            sh_hid_ve = 0
            sh_x_ve = {}
            sh_dn_te = {}
            for t in range(8):
                t0 = t * 256
                for grp in range(2):
                    if sh_se:
                        b.wait("te", c_se, sh_se)
                        b.wait("te", c_ve, sh_hid_ve)
                    for scn4 in range(4):
                        scn = grp * 4 + scn4
                        for cc in range(DC):
                            b.op("te", lambda h, scn=scn, scn4=scn4, cc=cc, t0=t0: h.matmul(
                                PA[:, scn4 * 256:(scn4 + 1) * 256],
                                SHGT(0, 128, cc * 1024 + scn * 128, cc * 1024 + (scn + 1) * 128),
                                xn_sl(cc, t0, 256),
                                start=(cc == 0), stop=(cc == DC - 1)),
                                [(c_te, 1)] if cc == DC - 1 else ())
                    for scn4 in range(4):
                        scn = grp * 4 + scn4
                        for cc in range(DC):
                            b.op("te", lambda h, scn=scn, scn4=scn4, cc=cc, t0=t0: h.matmul(
                                PB[:, scn4 * 256:(scn4 + 1) * 256],
                                SHUT(0, 128, cc * 1024 + scn * 128, cc * 1024 + (scn + 1) * 128),
                                xn_sl(cc, t0, 256),
                                start=(cc == 0), stop=(cc == DC - 1)),
                                [(c_te, 1)] if cc == DC - 1 else ())
                    gu = c_te.n
                    b.wait("se", c_te, gu)
                    if sh_hid_ve:
                        b.wait("se", c_ve, sh_hid_ve)
                    b.op("se", lambda h: h.activation(
                        HS(0, 128, 0, 1024), PA[:], AF.Silu), [(c_se, 1)])
                    sh_se = c_se.n
                    b.wait("ve", c_se, sh_se)
                    b.op("ve", lambda h: h.tensor_tensor(
                        HS(0, 128, 0, 1024), HS(0, 128, 0, 1024), PB[:],
                        AluOpType.mult), [(c_ve, 1)])
                    b.wait("ve", c_ve)
                    b.op("ve", lambda h, grp=grp: h.tensor_scalar(
                        SHHC(0, 128, grp * 1024, (grp + 1) * 1024),
                        HS(0, 128, 0, 1024), 0.5, None, AluOpType.mult), [(c_ve, 1)])
                    sh_hid_ve = c_ve.n
                b.wait("te", c_ve, sh_hid_ve)
                if t >= 1:
                    b.wait("te", c_ve, sh_x_ve[t - 1])
                for v in range(DC):
                    pvt = (P4, P5, P6, P7)[v // 2]
                    off = (v % 2) * 256
                    for scn in range(DC):
                        b.op("te", lambda h, pvt=pvt, off=off, scn=scn, v=v: h.matmul(
                            pvt[:, off:off + 256],
                            SHDT(0, 128, scn * 1024 + v * 128, scn * 1024 + (v + 1) * 128),
                            SHHC(0, 128, scn * 256, (scn + 1) * 256),
                            start=(scn == 0 and v % 2 == 0),
                            stop=(scn == DC - 1 and v % 2 == 1)),
                            [(c_te, 1)] if (scn == DC - 1 and v == DC - 1) else ())
                sh_dn_te[t] = c_te.n
                b.wait("ve", c_te, sh_dn_te[t])
                for bk in range(4):
                    b.op("ve", lambda h, bk=bk, t0=t0: h.tensor_tensor(
                        XSB[:].rearrange("p (c t) -> p c t", c=DC)[:, bk * 2: bk * 2 + 2, t0:t0 + 256],
                        XSB[:].rearrange("p (c t) -> p c t", c=DC)[:, bk * 2: bk * 2 + 2, t0:t0 + 256],
                        (P4, P5, P6, P7)[bk][:].rearrange("p (u t) -> p u t", u=2),
                        AluOpType.add), [(c_ve, 1)])
                sh_x_ve[t] = c_ve.n
            sh_done_te, sh_done_ve, sh_done_se = c_te.n, c_ve.n, c_se.n

            b.wait("ge", c_te, sh_done_te)
            b.wait("ge", c_ve, sh_done_ve)

            c_ew = [b.ch("c_ew0_" + tag, dma=True), b.ch("c_ew1_" + tag, dma=True)]
            c_cb = [b.ch("c_cb0_" + tag, dma=True), b.ch("c_cb1_" + tag, dma=True)]
            gate_r = w["l1_gate"].rearrange("e (cc p) hh -> e p cc hh", p=128)
            up_r = w["l1_up"].rearrange("e (cc p) hh -> e p cc hh", p=128)
            down_r = w["l1_down"].rearrange("e (hc p) v -> e p hc v", p=128)
            eslot = (ESL0, ESL1)
            b.wait("te", c_se, sh_done_se)
            b.wait("te", c_ve, sh_done_ve)
            b.wait("se", c_ve, sh_done_ve)
            e_te_done = {}
            e_ve_done = {}
            silu_se = {}
            hid_ve = {}
            dn_te = {}
            xadd_ve = {}
            def emit_gu(g):
                e, t = g // 8, g % 8
                ES = eslot[e % 2]
                t0 = t * 256
                if g >= 1:
                    b.wait("te", c_se, silu_se[g - 1])
                    b.wait("te", c_ve, hid_ve[g - 1])
                for hc in range(4):
                    for cc in range(DC):
                        b.op("te", lambda h, hc=hc, cc=cc, t0=t0, ES=ES: h.matmul(
                            PA[:, hc * 256:(hc + 1) * 256],
                            ES(0, 128, cc * 512 + hc * 128, cc * 512 + (hc + 1) * 128),
                            xn_sl(cc, t0, 256),
                            start=(cc == 0), stop=(cc == DC - 1)),
                            [(c_te, 1)] if cc == DC - 1 else ())
                for hc in range(4):
                    for cc in range(DC):
                        b.op("te", lambda h, hc=hc, cc=cc, t0=t0, ES=ES: h.matmul(
                            PB[:, hc * 256:(hc + 1) * 256],
                            ES(0, 128, 4096 + cc * 512 + hc * 128, 4096 + cc * 512 + (hc + 1) * 128),
                            xn_sl(cc, t0, 256),
                            start=(cc == 0), stop=(cc == DC - 1)),
                            [(c_te, 1)] if cc == DC - 1 else ())
                gu_done = c_te.n
                b.wait("se", c_te, gu_done)
                if g >= 1:
                    b.wait("se", c_ve, hid_ve[g - 1])
                b.op("se", lambda h: h.activation(
                    HS(0, 128, 0, 1024), PA[:], AF.Silu), [(c_se, 1)])
                silu_se[g] = c_se.n
                HD = (HID0, HID1)[g % 2]
                b.wait("ve", c_se, silu_se[g])
                if g >= 2:
                    b.wait("ve", c_te, dn_te[g - 2])
                sl = e % 2
                b.wait("ve", c_cb[sl], cb_target[e])
                b.op("ve", lambda h, HD=HD: h.tensor_tensor(
                    HD(0, 128, 0, 1024), HS(0, 128, 0, 1024), PB[:],
                    AluOpType.mult), [(c_ve, 1)])
                b.wait("ve", c_ve)
                CBCe = (CBC0, CBC1)[sl]
                for hc in range(4):
                    b.op("ve", lambda h, HD=HD, hc=hc, t0=t0, CBCe=CBCe: h.tensor_tensor(
                        HD(0, 128, hc * 256, (hc + 1) * 256),
                        HD(0, 128, hc * 256, (hc + 1) * 256),
                        CBCe(0, 128, t0, t0 + 256), AluOpType.mult), [(c_ve, 1)])
                hid_ve[g] = c_ve.n

            def emit_down(g):
                e, t = g // 8, g % 8
                ES = eslot[e % 2]
                HD = (HID0, HID1)[g % 2]
                t0 = t * 256
                b.wait("te", c_ve, hid_ve[g])
                if g >= 1:
                    b.wait("te", c_ve, xadd_ve[g - 1])
                for v in range(DC):
                    pvt = (P4, P5, P6, P7)[v // 2]
                    off = (v % 2) * 256
                    for hc in range(4):
                        b.op("te", lambda h, pvt=pvt, off=off, hc=hc, v=v, ES=ES, HD=HD: h.matmul(
                            pvt[:, off:off + 256],
                            ES(0, 128, 8192 + hc * 1024 + v * 128, 8192 + hc * 1024 + (v + 1) * 128),
                            HD(0, 128, hc * 256, (hc + 1) * 256),
                            start=(hc == 0 and v % 2 == 0),
                            stop=(hc == 3 and v % 2 == 1)),
                            [(c_te, 1)] if (hc == 3 and v == DC - 1) else ())
                dn_te[g] = c_te.n
                b.wait("ve", c_te, dn_te[g])
                for bk in range(4):
                    b.op("ve", lambda h, bk=bk, t0=t0: h.tensor_tensor(
                        XSB[:].rearrange("p (c t) -> p c t", c=DC)[:, bk * 2: bk * 2 + 2, t0:t0 + 256],
                        XSB[:].rearrange("p (c t) -> p c t", c=DC)[:, bk * 2: bk * 2 + 2, t0:t0 + 256],
                        (P4, P5, P6, P7)[bk][:].rearrange("p (u t) -> p u t", u=2),
                        AluOpType.add), [(c_ve, 1)])
                xadd_ve[g] = c_ve.n

            cb_target = {}
            for e in range(16):
                sl = e % 2
                ES = eslot[sl]
                CBCe = (CBC0, CBC1)[sl]
                if e >= 2:
                    b.wait("ge", c_te, dn_te[(e - 2) * 8 + 7])
                    b.wait("ge", c_ve, xadd_ve[(e - 2) * 8 + 7])
                b.op("ge", lambda h, e=e, ES=ES: h.dma_start(
                    ES(0, 128, 0, 4096).rearrange("p (cc hh) -> p cc hh", cc=DC),
                    gate_r[e]), [(c_ew[sl], 16)])
                b.op("ge", lambda h, e=e, ES=ES: h.dma_start(
                    ES(0, 128, 4096, 8192).rearrange("p (cc hh) -> p cc hh", cc=DC),
                    up_r[e]), [(c_ew[sl], 16)])
                b.op("ge", lambda h, e=e, ES=ES: h.dma_start(
                    ES(0, 128, 8192, 12288).rearrange("p (hc v) -> p hc v", hc=4),
                    down_r[e]), [(c_ew[sl], 16)])
                b.op("ge", lambda h, e=e, CBCe=CBCe, scrow=scrow: h.dma_start(
                    CBCe(0, 128, 0, T),
                    scr[scrow + e:scrow + e + 1, :].broadcast_to((128, T))), [(c_cb[sl], 16)])
                cb_target[e] = c_cb[sl].n
                b.wait("te", c_ew[sl], 16 * 3 * (e // 2 + 1))
                for t in range(8):
                    g = e * 8 + t
                    emit_gu(g)
                    if g >= 1:
                        emit_down(g - 1)
            emit_down(127)
            return c_ve.n

        def emit_l1_conv():
            ar = Arena()
            W1B = ar.alloc(8192)
            W2B = ar.alloc(8192)
            HB = ar.alloc(16384)
            ETN = ar.alloc(1024, F32)
            ETT = ar.alloc(256, F32)
            DGT = ar.alloc(2560)
            MASK = ar.alloc(128)
            IOT = ar.alloc(512)
            BIASR = ar.alloc(1024)
            ONR = ar.alloc(512)

            entry_te, entry_ve, entry_se = c_te.n, c_ve.n, c_se.n
            xn_done = emit_layernorm()
            for _e in ("ge", "sy"):
                b.wait(_e, c_te, entry_te)
                b.wait(_e, c_ve, entry_ve)
                b.wait(_e, c_se, entry_se)
            b.dma_barrier()

            c_cw = b.ch("c_cw", dma=True)
            c_cwh = b.ch("c_cwh", dma=True)
            w1_r = w["l1_W1"].rearrange("(dc p) o -> p dc o", p=128)
            w2_r = w["l1_W2"].rearrange("(dc p) o -> p dc o", p=128)
            b.op("ge", lambda h: h.dma_start(
                W1B(0, 128, 0, 8192).rearrange("p (dc o) -> p dc o", dc=DC), w1_r),
                [(c_cw, 16)])
            b.op("ge", lambda h: h.dma_start(
                W2B(0, 128, 0, 8192).rearrange("p (dc o) -> p dc o", dc=DC), w2_r),
                [(c_cw, 16)])
            b.op("sy", lambda h: h.dma_start(ETN(0, 21, 0, 1024), w["l1_et"]), [(c_cwh, 16)])
            b.op("ge", lambda h: h.dma_start(
                BIASR(0, 1, 0, 1024), w["l1_bias"].rearrange("(o d) -> o d", o=1)),
                [(c_cw, 16)])
            b.op("ge", lambda h: h.memset(ONR(0, 1, 0, 512), 1.0), [(c_ge, 1)])
            pi32 = IOT(0, 128, 0, 256).bitcast(mybir.dt.int32)
            fi32 = IOT(0, 128, 256, 512).bitcast(mybir.dt.int32)
            b.op("ge", lambda h: h.iota(pi32, pattern=[[0, 128]], channel_multiplier=1),
                 [(c_ge, 1)])
            b.op("ge", lambda h: h.iota(fi32, pattern=[[1, 128]], channel_multiplier=0),
                 [(c_ge, 1)])
            init_ge2 = c_ge.n
            b.wait("ve", c_ge, init_ge2)
            b.op("ve", lambda h: h.tensor_tensor(
                MASK(0, 128, 0, 128), pi32, fi32, AluOpType.is_equal), [(c_ve, 1)])
            mask_done = c_ve.n

            b.wait("te", c_cw)
            b.wait("te", c_cwh)
            b.wait("te", c_ve, entry_ve)
            b.wait("te", c_se, entry_se)
            for dc in range(DC):
                b.op("te", lambda h, dc=dc: h.transpose(
                    P6[:, dc * 32: dc * 32 + 21],
                    ETN(0, 21, dc * 128, (dc + 1) * 128), IDF[0:21, 0:21]),
                    [(c_te, 1)] if dc == DC - 1 else ())
            ett_tr = c_te.n
            b.wait("ve", c_te, ett_tr)
            b.op("ve", lambda h: h.tensor_copy(ETT(0, 128, 0, 256), P6[:, 0:256]),
                 [(c_ve, 1)])
            ett_done = c_ve.n

            b.wait("te", c_ve, xn_done)
            hb_ev = {}
            cv_te = {}
            for dc in range(DC):
                b.wait("ve", c_ve, max(ett_done, mask_done))
                if dc >= 1:
                    b.wait("ve", c_te, cv_te[dc - 1])
                for m in range(1, 21):
                    b.op("ve", lambda h, dc=dc, m=m: h.tensor_scalar(
                        DGT(0, 128, (m - 1) * 128, m * 128),
                        MASK(0, 128, 0, 128),
                        ETT(0, 128, dc * 32 + m, dc * 32 + m + 1), None,
                        AluOpType.mult), [(c_ve, 1)])
                dg_done = c_ve.n
                b.wait("te", c_ve, dg_done)
                for t in range(NT):
                    t0 = t * 512
                    half = (dc * NT + t) % 2
                    if dc * NT + t >= 2:
                        b.wait("te", c_ve, hb_ev[dc * NT + t - 2])
                    for m in range(1, 21):
                        b.op("te", lambda h, dc=dc, m=m, t0=t0, half=half: h.matmul(
                            PA[:, half * 512:(half + 1) * 512],
                            DGT(0, 128, (m - 1) * 128, m * 128),
                            XN[:, dc * PADT + PADW + t0 - m: dc * PADT + PADW + t0 - m + 512],
                            start=(m == 1), stop=(m == 20)),
                            [(c_te, 1)] if m == 20 else ())
                    cv_te[dc] = c_te.n
                    b.wait("ve", c_te, cv_te[dc])
                    b.op("ve", lambda h, dc=dc, t0=t0, half=half: h.tensor_copy(
                        HB(0, 128, dc * T + t0, dc * T + t0 + 512),
                        PA[:, half * 512:(half + 1) * 512]), [(c_ve, 1)])
                    hb_ev[dc * NT + t] = c_ve.n
            conv_ve = c_ve.n

            b.wait("te", c_ve, conv_ve)
            wp_ev = {}
            for t in range(NT):
                t0 = t * 512
                for oc in range(DC):
                    gi = t * DC + oc
                    pvt = (P4, P5, P6)[gi % 3]
                    if gi >= 3:
                        b.wait("te", c_ve, wp_ev[gi - 3])
                    for dc in range(DC):
                        b.op("te", lambda h, dc=dc, oc=oc, t0=t0, pvt=pvt: h.matmul(
                            pvt[:, 0:512],
                            W1B(0, 128, dc * 1024 + oc * 128, dc * 1024 + (oc + 1) * 128),
                            xn_sl(dc, t0, 512),
                            start=(dc == 0), stop=False),
                            ())
                    for dc in range(DC):
                        b.op("te", lambda h, dc=dc, oc=oc, t0=t0, pvt=pvt: h.matmul(
                            pvt[:, 0:512],
                            W2B(0, 128, dc * 1024 + oc * 128, dc * 1024 + (oc + 1) * 128),
                            HB(0, 128, dc * T + t0, dc * T + t0 + 512),
                            start=False, stop=False),
                            ())
                    b.op("te", lambda h, oc=oc, pvt=pvt: h.matmul(
                        pvt[:, 0:512],
                        BIASR(0, 1, oc * 128, (oc + 1) * 128),
                        ONR(0, 1, 0, 512),
                        start=False, stop=True), [(c_te, 1)])
                    wpm = c_te.n
                    b.wait("ve", c_te, wpm)
                    b.op("ve", lambda h, oc=oc, t0=t0, pvt=pvt: h.tensor_tensor(
                        xsb_sl(oc, t0, 512), xsb_sl(oc, t0, 512), pvt[:, 0:512],
                        AluOpType.add), [(c_ve, 1)])
                    wp_ev[gi] = c_ve.n
            return c_ve.n

        res_marks = {}
        for ph in phases:
            if ph in ("l0ffn1", "l0ffn2"):
                emit_l0_ffn(ph)
            elif ph == "l0attn":
                emit_l0_attn()
            elif ph == "l1moe1":
                emit_l1_moe(ph, 16)
            elif ph == "l1moe2":
                emit_l1_moe(ph, 32)
            elif ph == "l1conv":
                emit_l1_conv()

        last_ve = c_ve.n
        b.wait("sy", c_ve, last_ve)
        yTr = yT.rearrange("(c p) t -> p c t", p=128)
        b.op("sy", lambda h: h.dma_start(
            yTr, XSB[:].rearrange("p (c t) -> p c t", c=DC)), [(c_out, 16)])
        b.wait("sy", c_out)

        blk = es.enter_context(nc.Block())
        b.emit(blk)
    return nc


def _mk_inmaps(inputs):
    import ml_dtypes
    x = inputs["L0_attn_x"]
    consts = {
        "identb": np.eye(128, dtype=ml_dtypes.bfloat16),
        "identf": np.eye(128, dtype=np.float32),
        "onesb": np.ones((128, 1), dtype=ml_dtypes.bfloat16),
        "onesfr": np.ones((1, 128), dtype=np.float32),
    }
    wkeys = [k for k in inputs if k != "L0_attn_x"]
    maps = []
    for core in range(8):
        m = {"xT": np.ascontiguousarray(x[core].T)}
        for k in wkeys:
            m[k] = np.ascontiguousarray(inputs[k])
        m.update(consts)
        maps.append(m)
    return maps


def kernel(**inputs):
    nc = build_program(["l0ffn1", "l0attn", "l0ffn2", "l1moe1", "l1conv", "l1moe2"])
    maps = _mk_inmaps(inputs)
    res = run_bass_kernel_spmd(nc, maps, core_ids=list(range(8)))
    out = np.stack([np.asarray(r["yT"]).T for r in res.results], axis=0)
    return np.ascontiguousarray(out.astype(np.float32))


# version 03
# speedup vs baseline: 1.4211x; 1.0017x over previous
import contextlib
import numpy as np
import concourse.bass as bass
import concourse.mybir as mybir
from concourse.bass_utils import run_bass_kernel_spmd
from concourse.alu_op_type import AluOpType

F32 = mybir.dt.float32
BF16 = mybir.dt.bfloat16
AF = mybir.ActivationFunctionType
AX = mybir.AxisListType

D = 1024
T = 2048
DC = D // 128
NT = T // 512
PADW = 24
PADT = T + PADW

ENGINES = ("te", "ve", "se", "sy", "ge")


class Chan:
    __slots__ = ("sem", "n", "name")

    def __init__(self, sem, name):
        self.sem = sem
        self.n = 0
        self.name = name


class Builder:
    def __init__(self, nc, es):
        self.nc = nc
        self.es = es
        self.ops = {e: [] for e in ENGINES}
        self.chans = {}
        self.dma_chans = []

    def ch(self, name, dma=False):
        if name not in self.chans:
            sem = self.es.enter_context(self.nc.semaphore(name))
            self.chans[name] = Chan(sem, name)
            if dma:
                self.dma_chans.append(self.chans[name])
        return self.chans[name]

    def dma_barrier(self, engines=("ge", "sy")):
        for c in self.dma_chans:
            if c.n:
                for e in engines:
                    self.wait(e, c)

    def op(self, eng, f, incs=()):
        for c, a in incs:
            c.n += a
        semincs = tuple((c.sem, a) for c, a in incs)

        def g(h, f=f, semincs=semincs):
            ins = f(h)
            for s, a in semincs:
                ins = ins.then_inc(s, a)
        self.ops[eng].append(g)

    def wait(self, eng, c, tgt=None):
        t = c.n if tgt is None else tgt
        assert t <= c.n, (c.name, t, c.n)
        self.ops[eng].append(lambda h, s=c.sem, t=t: h.wait_ge(s, t))

    def emit(self, blk):
        for name, attr in (("te", "tensor"), ("ve", "vector"), ("se", "scalar"),
                           ("sy", "sync"), ("ge", "gpsimd")):
            ops = self.ops[name]

            def section(h, ops=ops):
                for f in ops:
                    f(h)
            getattr(blk, attr)(section)


def ceil_div(a, b):
    return (a + b - 1) // b


def build_program(phases):
    nc = bass.Bass("TRN2", target_bir_lowering=False)

    def din(name, shape, dtype=F32):
        return nc.dram_tensor(name, list(shape), dtype, kind="ExternalInput").ap()

    xT = din("xT", (D, T))
    w = {}
    w["l0_wq"] = din("L0_attn_wq", (D, D))
    w["l0_wk"] = din("L0_attn_wk", (D, D))
    w["l0_wv"] = din("L0_attn_wv", (D, D))
    w["l0_wo"] = din("L0_attn_wo", (D, D))
    w["l0_sel"] = din("L0_ffn_expert_sel", (16, D))
    w["l0_keys"] = din("L0_ffn_keys", (16, D, 128))
    w["l0_vals"] = din("L0_ffn_values", (16, 128, D))
    w["l1_W1"] = din("L1_attn_W1", (D, D))
    w["l1_W2"] = din("L1_attn_W2", (D, D))
    w["l1_bias"] = din("L1_attn_bias", (D,))
    w["l1_et"] = din("L1_attn_embeddings_table", (21, D))
    w["l1_rw"] = din("L1_ffn_router_w", (16, D))
    w["l1_rb"] = din("L1_ffn_router_bias", (16,))
    w["l1_up"] = din("L1_ffn_up_proj", (16, D, 512))
    w["l1_gate"] = din("L1_ffn_gate_proj", (16, D, 512))
    w["l1_down"] = din("L1_ffn_down_proj", (16, 512, D))
    w["l1_shg"] = din("L1_ffn_sh_gate", (D, D))
    w["l1_shu"] = din("L1_ffn_sh_up", (D, D))
    w["l1_shd"] = din("L1_ffn_sh_down", (D, D))
    identb = din("identb", (128, 128), BF16)
    identf = din("identf", (128, 128), F32)
    onesb = din("onesb", (128, 1), BF16)
    onesfr = din("onesfr", (1, 128), F32)
    onesbr = din("onesbr", (1, 128), BF16)

    yT = nc.dram_tensor("yT", [D, T], F32, kind="ExternalOutput").ap()
    scr = nc.dram_tensor("scr", [64, T], F32, kind="Internal").ap()

    es = contextlib.ExitStack()
    with es:
        b = Builder(nc, es)
        sb = lambda name, shape, dtype: es.enter_context(nc.sbuf_tensor(name, list(shape), dtype))
        ps = lambda name, shape, dtype=F32: es.enter_context(nc.psum_tensor(name, list(shape), dtype))

        XSB = sb("XSB", (128, DC * T), F32)
        XN = sb("XN", (128, DC * PADT), BF16)
        AR = sb("AR", (128, 49088), BF16)
        IDB = sb("IDB", (128, 128), BF16)
        IDF = sb("IDF", (128, 128), F32)
        ONB = sb("ONB", (128, 1), BF16)
        ONFR = sb("ONFR", (1, 128), F32)
        ONBR = sb("ONBR", (1, 128), BF16)
        EPS = sb("EPS", (128, 1), F32)
        NEG8 = sb("NEG8", (128, 1), F32)
        SMa = sb("SMa", (1, 512), F32)
        SMb = sb("SMb", (1, 512), F32)
        SMc = sb("SMc", (1, 512), F32)
        LNC = sb("LNC", (128, 4096), BF16)

        PA = ps("PA", (128, 1024))
        PB = ps("PB", (128, 1024))
        P4 = ps("P4", (128, 512))
        P5 = ps("P5", (128, 512))
        P6 = ps("P6", (128, 512))
        P7 = ps("P7", (128, 512))

        c_te = b.ch("c_te")
        c_ve = b.ch("c_ve")
        c_se = b.ch("c_se")
        c_ge = b.ch("c_ge")
        c_sy = b.ch("c_sy", dma=True)
        c_init = b.ch("c_init", dma=True)
        c_xl = b.ch("c_xl", dma=True)
        c_gw = b.ch("c_gw", dma=True)
        c_bc0 = b.ch("c_bc0", dma=True)
        c_bc1 = b.ch("c_bc1", dma=True)
        c_scw = b.ch("c_scw", dma=True)
        c_out = b.ch("c_out", dma=True)

        def xn_sl(c, t0, tw):
            return XN[:, c * PADT + PADW + t0: c * PADT + PADW + t0 + tw]

        def xsb_sl(c, t0, tw):
            return XSB[:, c * T + t0: c * T + t0 + tw]

        class Arena:
            def __init__(self):
                self.top = 0

            def alloc(self, cols, dtype=BF16):
                bcols = cols * (2 if dtype == F32 else 1)
                bcols = (bcols + 63) // 64 * 64
                c0 = self.top
                self.top += bcols
                assert self.top <= 49088, f"arena overflow {self.top}"
                if dtype == BF16:
                    return lambda p0, p1, f0, f1: AR[p0:p1, c0 + f0: c0 + f1]
                else:
                    arf = AR[:].bitcast(F32)
                    c0f = c0 // 2
                    return lambda p0, p1, f0, f1: arf[p0:p1, c0f + f0: c0f + f1]

            def reset(self, top=0):
                self.top = top

        b.op("sy", lambda h: h.dma_start(IDB[:], identb), [(c_init, 16)])
        b.op("sy", lambda h: h.dma_start(IDF[:], identf), [(c_init, 16)])
        b.op("sy", lambda h: h.dma_start(ONB[:], onesb), [(c_init, 16)])
        b.op("sy", lambda h: h.dma_start(ONFR[:], onesfr), [(c_init, 16)])
        b.op("sy", lambda h: h.dma_start(ONBR[:], onesbr), [(c_init, 16)])
        b.op("ge", lambda h: h.memset(EPS[:], 1e-5), [(c_ge, 1)])
        b.op("ge", lambda h: h.memset(NEG8[:], -8.0), [(c_ge, 1)])
        for c in range(DC):
            b.op("ge", lambda h, c=c: h.memset(XN[:, c * PADT: c * PADT + PADW], 0.0),
                 [(c_ge, 1)])
        init_ge = c_ge.n

        xTr = xT.rearrange("(c p) t -> p c t", p=128)
        b.op("sy", lambda h: h.dma_start(
            XSB[:].rearrange("p (c t) -> p c t", c=DC), xTr), [(c_xl, 16)])

        b.wait("ve", c_xl)
        b.wait("te", c_init)
        b.wait("te", c_ge, init_ge)
        b.wait("se", c_ge, init_ge)

        def emit_layernorm():
            b.wait("ve", c_ve)
            b.wait("te", c_se)
            prev_sumsq = 0
            prev_affine = 0
            for t in range(NT):
                t0 = t * 512
                xv = XSB[:].rearrange("p (c t) -> p c t", c=DC)[:, :, t0:t0 + 512]
                lnc3 = LNC[:].rearrange("p (c t) -> p c t", c=DC)[:, :, 0:512]
                if t > 0:
                    b.wait("ve", c_te, prev_sumsq)
                    b.wait("ve", c_ve, prev_affine)
                b.op("ve", lambda h, xv=xv, lnc3=lnc3: h.tensor_copy(lnc3, xv), [(c_ve, 1)])
                cast_done = c_ve.n
                b.wait("te", c_ve, cast_done)
                for c in range(DC):
                    b.op("te", lambda h, c=c: h.matmul(
                        PA[0:1, 0:512], ONB[:], LNC[:, c * 512:(c + 1) * 512],
                        start=(c == 0), stop=(c == DC - 1)),
                        [(c_te, 1)] if c == DC - 1 else ())
                sum_done = c_te.n
                b.wait("se", c_ve, cast_done)
                b.wait("se", c_te, sum_done)
                b.op("se", lambda h, lnc3=lnc3: h.activation(
                    lnc3, lnc3, AF.Square), [(c_se, 1)])
                sq_done = c_se.n
                b.wait("te", c_se, sq_done)
                for c in range(DC):
                    b.op("te", lambda h, c=c: h.matmul(
                        PA[0:1, 512:1024], ONB[:], LNC[:, c * 512:(c + 1) * 512],
                        start=(c == 0), stop=(c == DC - 1)),
                        [(c_te, 1)] if c == DC - 1 else ())
                sumsq_done = c_te.n
                prev_sumsq = sumsq_done
                b.wait("ve", c_te, sum_done)
                b.op("ve", lambda h: h.tensor_scalar(
                    SMa[:], PA[0:1, 0:512], 1.0 / D, None, AluOpType.mult), [(c_ve, 1)])
                mean_done = c_ve.n
                b.wait("ve", c_te, sumsq_done)
                b.op("ve", lambda h: h.tensor_scalar(
                    SMb[:], PA[0:1, 512:1024], 1.0 / D, None, AluOpType.mult), [(c_ve, 1)])
                b.wait("se", c_ve, mean_done)
                b.op("se", lambda h: h.activation(SMc[:], SMa[:], AF.Square), [(c_se, 1)])
                b.wait("ve", c_se)
                b.wait("ve", c_ve)
                b.op("ve", lambda h: h.tensor_tensor(
                    SMb[:], SMb[:], SMc[:], AluOpType.subtract), [(c_ve, 1)])
                b.wait("se", c_ve)
                b.op("se", lambda h: h.activation(
                    SMb[:], SMb[:], AF.Sqrt, bias=EPS[0:1, :]), [(c_se, 1)])
                b.wait("ve", c_se)
                b.op("ve", lambda h: h.reciprocal(SMb[:], SMb[:]), [(c_ve, 1)])
                rstd_done = c_ve.n
                b.wait("ve", c_ve, rstd_done)
                b.op("ve", lambda h: h.tensor_tensor(
                    SMc[:], SMa[:], SMb[:], AluOpType.mult), [(c_ve, 1)])
                brow_done = c_ve.n
                b.wait("te", c_ve, brow_done)
                b.op("te", lambda h: h.matmul(PB[:, 0:512], ONFR[:], SMb[:], start=True, stop=True),
                     [(c_te, 1)])
                b.op("te", lambda h: h.matmul(PB[:, 512:1024], ONFR[:], SMc[:], start=True, stop=True),
                     [(c_te, 1)])
                rep_done = c_te.n
                b.wait("ve", c_te, rep_done)
                for c in range(DC):
                    b.op("ve", lambda h, c=c, t0=t0: h.tensor_tensor(
                        LNC[:, c * 512:(c + 1) * 512],
                        XSB[:, c * T + t0: c * T + t0 + 512],
                        PB[:, 0:512], AluOpType.mult), [(c_ve, 1)])
                mul_done = c_ve.n
                b.wait("ve", c_ve, mul_done)
                for c in range(DC):
                    b.op("ve", lambda h, c=c, t0=t0: h.tensor_tensor(
                        xn_sl(c, t0, 512), LNC[:, c * 512:(c + 1) * 512],
                        PB[:, 512:1024], AluOpType.subtract), [(c_ve, 1)])
                prev_affine = c_ve.n
            return c_ve.n

        def emit_l0_ffn(tag):
            ar = Arena()
            KEY = ar.alloc(16 * 1024)
            VAL = ar.alloc(16 * 1024)
            EST = ar.alloc(128)
            ESN = ar.alloc(1024)
            SC = ar.alloc(16 * 512)
            SBC0 = ar.alloc(512)
            SBC1 = ar.alloc(512)
            SELF = ar.alloc(2048, F32)

            entry_te, entry_ve, entry_se = c_te.n, c_ve.n, c_se.n
            xn_done = emit_layernorm()

            for _e in ("ge", "sy"):
                b.wait(_e, c_te, entry_te)
                b.wait(_e, c_ve, entry_ve)
                b.wait(_e, c_se, entry_se)
            b.dma_barrier()
            keys_r = w["l0_keys"].rearrange("e (c p) hh -> e p c hh", p=128)
            vals_r = w["l0_vals"]
            for e in range(16):
                b.op("ge", lambda h, e=e: h.dma_start(
                    KEY(0, 128, e * 1024, (e + 1) * 1024).rearrange("p (c hh) -> p c hh", c=DC),
                    keys_r[e]), [(c_gw, 16)])
                b.op("ge", lambda h, e=e: h.dma_start(
                    VAL(0, 128, e * 1024, (e + 1) * 1024), vals_r[e]), [(c_gw, 16)])
            b.op("ge", lambda h: h.dma_start(
                ESN(0, 16, 0, 1024), w["l0_sel"]), [(c_gw, 16)])
            wload = c_gw.n

            b.wait("te", c_gw, wload)
            pa_b = PA[:].bitcast(BF16)
            for c in range(DC):
                b.op("te", lambda h, c=c: h.transpose(
                    pa_b[:, c * 16:(c + 1) * 16],
                    ESN(0, 16, c * 128, (c + 1) * 128), IDB[0:16, 0:16]),
                    [(c_te, 1)] if c == DC - 1 else ())
            est_tr = c_te.n
            b.wait("ve", c_te, est_tr)
            b.op("ve", lambda h: h.tensor_copy(EST(0, 128, 0, 128), pa_b[:, 0:128]),
                 [(c_ve, 1)])
            est_done = c_ve.n

            b.wait("te", c_ve, xn_done)
            b.wait("te", c_ve, est_done)
            sig_mark = {0: 0, 1: 0}
            for t in range(NT):
                t0 = t * 512
                if sig_mark[t % 2]:
                    b.wait("te", c_se, sig_mark[t % 2])
                for c in range(DC):
                    b.op("te", lambda h, c=c, t0=t0, t=t: h.matmul(
                        PB[0:16, (t % 2) * 512:(t % 2) * 512 + 512],
                        EST(0, 128, c * 16, (c + 1) * 16),
                        xn_sl(c, t0, 512),
                        start=(c == 0), stop=(c == DC - 1)),
                        [(c_te, 1)] if c == DC - 1 else ())
                lg = c_te.n
                b.wait("se", c_te, lg)
                b.op("se", lambda h, t=t, t0=t0: h.activation(
                    SELF(0, 16, t0, t0 + 512), PB[0:16, (t % 2) * 512:(t % 2) * 512 + 512],
                    AF.Sigmoid), [(c_se, 1)])
                sig_mark[t % 2] = c_se.n
            sel_sig = c_se.n
            b.wait("ve", c_se, sel_sig)
            b.op("ve", lambda h: h.tensor_scalar(
                SELF(0, 16, 0, T), SELF(0, 16, 0, T), 0.5, None, AluOpType.mult),
                [(c_ve, 1)])
            sel_done = c_ve.n
            b.wait("ge", c_ve, sel_done)
            b.op("ge", lambda h: h.dma_start(scr[0:16, :], SELF(0, 16, 0, T)), [(c_scw, 16)])
            b.wait("ge", c_scw)

            se_relu_mark = {0: 0, 1: 0}
            sbc_lastread = {0: 0, 1: 0}
            bc_target = {}
            vdone_mark = {}
            val_te_mark = 0
            for t in range(NT):
                t0 = t * 512
                for e in range(16):
                    idx = t * 16 + e
                    half = e % 2
                    bc = (c_bc0, c_bc1)[idx % 2]
                    sbc = (SBC0, SBC1)[idx % 2]
                    if sbc_lastread[idx % 2]:
                        b.wait("ge", c_ve, sbc_lastread[idx % 2])
                    b.op("ge", lambda h, e=e, t0=t0, sbc=sbc: h.dma_start(
                        sbc(0, 128, 0, 512),
                        scr[e:e + 1, t0:t0 + 512].broadcast_to((128, 512))), [(bc, 16)])
                    bc_target[idx] = bc.n
                    if se_relu_mark[half]:
                        b.wait("te", c_se, se_relu_mark[half])
                    if e == 0 and val_te_mark:
                        b.wait("se", c_te, val_te_mark)
                    for c in range(DC):
                        b.op("te", lambda h, c=c, e=e, t0=t0, half=half: h.matmul(
                            PA[:, half * 512: half * 512 + 512],
                            KEY(0, 128, e * 1024 + c * 128, e * 1024 + (c + 1) * 128),
                            xn_sl(c, t0, 512),
                            start=(c == 0), stop=(c == DC - 1)),
                            [(c_te, 1)] if c == DC - 1 else ())
                    key_done = c_te.n
                    b.wait("se", c_te, key_done)
                    b.op("se", lambda h, e=e, half=half: h.activation(
                        SC(0, 128, e * 512, (e + 1) * 512),
                        PA[:, half * 512: half * 512 + 512], AF.Relu), [(c_se, 1)])
                    se_relu_mark[half] = c_se.n
                    b.wait("ve", bc, bc_target[idx])
                    b.wait("ve", c_se, se_relu_mark[half])
                    b.op("ve", lambda h, e=e, sbc=sbc: h.tensor_tensor(
                        SC(0, 128, e * 512, (e + 1) * 512),
                        SC(0, 128, e * 512, (e + 1) * 512),
                        sbc(0, 128, 0, 512), AluOpType.mult), [(c_ve, 1)])
                    sbc_lastread[idx % 2] = c_ve.n
                sc_ready = c_ve.n
                b.wait("te", c_ve, sc_ready)
                for v in range(DC):
                    vidx = t * DC + v
                    pv = (P4, P5, P6)[vidx % 3]
                    if vidx >= 3:
                        b.wait("te", c_ve, vdone_mark[vidx - 3])
                    for e in range(16):
                        b.op("te", lambda h, pv=pv, v=v, e=e: h.matmul(
                            pv[:, 0:512],
                            VAL(0, 128, e * 1024 + v * 128, e * 1024 + (v + 1) * 128),
                            SC(0, 128, e * 512, (e + 1) * 512),
                            start=(e == 0), stop=(e == 15)),
                            [(c_te, 1)] if e == 15 else ())
                    val_done = c_te.n
                    b.wait("ve", c_te, val_done)
                    b.op("ve", lambda h, pv=pv, v=v, t0=t0: h.tensor_tensor(
                        xsb_sl(v, t0, 512), xsb_sl(v, t0, 512), pv[:, 0:512],
                        AluOpType.add), [(c_ve, 1)])
                    vdone_mark[vidx] = c_ve.n
                val_te_mark = c_te.n
            return c_ve.n

        def emit_l0_attn():
            ar = Arena()
            WQT = ar.alloc(8192)
            WKT = ar.alloc(8192)
            WVT = ar.alloc(8192)
            WOT = ar.alloc(8192)
            WSTG = ar.alloc(1024)
            QKTM = ar.alloc(1024)
            QTBe = [ar.alloc(1024), ar.alloc(1024)]
            QTBo = [ar.alloc(1024), ar.alloc(1024)]
            KTBe = [ar.alloc(1024), ar.alloc(1024)]
            KTBo = [ar.alloc(1024), ar.alloc(1024)]
            VTM = ar.alloc(1040)
            PT0 = ar.alloc(1024)
            PT1 = ar.alloc(1024)
            AOTh = ar.alloc(1024)
            RBC = ar.alloc(1024)
            SMR = WSTG

            P6b = P6[:].bitcast(BF16)
            P7b = P7[:].bitcast(BF16)

            entry_te, entry_ve, entry_se = c_te.n, c_ve.n, c_se.n
            xn_done = emit_layernorm()
            for _e in ("ge", "sy"):
                b.wait(_e, c_te, entry_te)
                b.wait(_e, c_ve, entry_ve)
                b.wait(_e, c_se, entry_se)
            b.dma_barrier()

            wsrc = [w["l0_wq"], w["l0_wk"], w["l0_wv"], w["l0_wo"]]
            wdst = [WQT, WKT, WVT, WOT]
            c_ws = [b.ch("c_ws0", dma=True), b.ch("c_ws1", dma=True)]
            stg_te_mark = {}
            evac_ve_mark = {}
            b.wait("te", c_ve, entry_ve)
            b.wait("te", c_se, entry_se)
            for idx in range(32):
                wi, oc = idx // 8, idx % 8
                slot = idx % 2
                if idx >= 1:
                    b.wait("ge", c_te, stg_te_mark[idx - 1])
                b.op("ge", lambda h, wi=wi, oc=oc: h.dma_start(
                    WSTG(0, 128, 0, 1024),
                    wsrc[wi][oc * 128:(oc + 1) * 128, :]), [(c_ws[slot], 16)])
                b.wait("te", c_ws[slot], 16 * (idx // 2 + 1))
                if idx >= 2:
                    b.wait("te", c_ve, evac_ve_mark[idx - 2])
                pb_t = (P6b, P7b)[slot]
                for cc in range(DC):
                    b.op("te", lambda h, cc=cc, pb_t=pb_t: h.transpose(
                        pb_t[:, cc * 128:(cc + 1) * 128],
                        WSTG(0, 128, cc * 128, (cc + 1) * 128),
                        IDB[:]), [(c_te, 1)] if cc == DC - 1 else ())
                stg_te_mark[idx] = c_te.n
                b.wait("ve", c_te, stg_te_mark[idx])
                dst = wdst[wi]
                b.op("ve", lambda h, dst=dst, oc=oc, pb_t=pb_t: h.tensor_copy(
                    dst(0, 128, 0, 8192).rearrange("p (cc o) -> p cc o", cc=DC)[
                        :, :, oc * 128:(oc + 1) * 128],
                    pb_t.rearrange("p (cc s) -> p cc s", cc=DC)), [(c_ve, 1)])
                evac_ve_mark[idx] = c_ve.n
            wprep_ve = c_ve.n

            b.op("ge", lambda h: h.memset(
                VTM(0, 128, 0, 1040).rearrange("p (g dd) -> p g dd", dd=65)[:, :, 64:65],
                1.0), [(c_ge, 1)])
            vones = c_ge.n
            b.wait("te", c_ge, vones)
            b.wait("te", c_ve, wprep_ve)
            b.wait("te", c_ve, xn_done)

            marks = {
                "exp": 0,
                "pt_av": {0: 0, 1: 0},
                "qtm_ev": {}, "v_ev": {},
                "qtb_ev": {}, "ktb_ev": {},
                "rbc_ve": 0,
                "aoth_ev": {},
                "oproj_te": {}, "oproj_ve": {},
                "projlast_ve": 0,
                "smr": 0,
                "p67_ve": 0,
            }

            def emit_proj(h16, which):
                hcol = 128 * h16
                wt = {"q": WQT, "k": WKT, "v": WVT}[which]
                b.wait("te", c_ve, marks["projlast_ve"])
                for half in range(2):
                    for cc in range(DC):
                        b.op("te", lambda h, cc=cc, half=half, hcol=hcol, wt=wt: h.matmul(
                            PB[:, half * 512:(half + 1) * 512],
                            xn_sl(cc, hcol, 128),
                            wt(0, 128, cc * 1024 + half * 512, cc * 1024 + (half + 1) * 512),
                            start=(cc == 0), stop=(cc == DC - 1)),
                            [(c_te, 1)] if cc == DC - 1 else ())
                p_done = c_te.n
                b.wait("ve", c_te, p_done)
                if which == "v":
                    b.wait("ve", c_ge, vones)
                    for half in range(2):
                        b.op("ve", lambda h, half=half: h.tensor_copy(
                            VTM(0, 128, half * 520, half * 520 + 520).rearrange(
                                "p (g dd) -> p g dd", dd=65)[:, :, 0:64],
                            PB[:, half * 512:(half + 1) * 512].rearrange(
                                "p (g dd) -> p g dd", dd=64)), [(c_ve, 1)])
                    marks["v_ev"][h16] = c_ve.n
                else:
                    if marks["qtm_ev"].get("last_tr"):
                        b.wait("ve", c_te, marks["qtm_ev"]["last_tr"])
                    b.op("ve", lambda h: h.tensor_copy(QKTM(0, 128, 0, 1024), PB[:]),
                         [(c_ve, 1)])
                    marks["qtm_ev"][(h16, which)] = c_ve.n
                marks["projlast_ve"] = c_ve.n

            def emit_tr(h16, which):
                par = h16 % 2
                dste = (QTBe, KTBe)[which == "k"][par]
                dsto = (QTBo, KTBo)[which == "k"][par]
                b.wait("te", c_ve, marks["qtm_ev"][(h16, which)])
                b.wait("te", c_ve, marks["rbc_ve"])
                for bb in range(DC):
                    b.op("te", lambda h, bb=bb: h.transpose(
                        P6b[:, bb * 128:(bb + 1) * 128],
                        QKTM(0, 128, bb * 128, (bb + 1) * 128), IDB[:]),
                        [(c_te, 1)] if bb == DC - 1 else ())
                marks["qtm_ev"]["last_tr"] = c_te.n
                b.wait("ve", c_te, marks["qtm_ev"]["last_tr"])
                b.op("ve", lambda h, dste=dste: h.tensor_copy(
                    dste(0, 64, 0, 1024), P6b[0:64, :]), [(c_ve, 1)])
                b.op("ve", lambda h, dsto=dsto: h.tensor_copy(
                    dsto(0, 64, 0, 1024), P6b[64:128, :]), [(c_ve, 1)])
                marks[("qtb_ev", "ktb_ev")[which == "k"]][h16] = c_ve.n
                marks["p67_ve"] = c_ve.n

            def emit_round(h16, pi):
                par = h16 % 2
                QTB = (QTBe, QTBo)[pi][par]
                KTBe_, KTBo_ = KTBe[par], KTBo[par]
                b.wait("te", c_ve, marks["qtb_ev"][h16])
                b.wait("te", c_ve, marks["ktb_ev"][h16])
                b.wait("te", c_ve, marks["v_ev"][h16])
                if pi == 1:
                    b.wait("te", c_ve, marks["aoth_ev"][(h16, 0)])
                elif marks["oproj_ve"].get(h16 - 1) is not None:
                    b.wait("te", c_ve, marks["oproj_ve"][h16 - 1])
                for jg in range(16):
                    KTB = (KTBe_, KTBo_)[jg % 2]
                    jb = jg // 2
                    if marks["exp"]:
                        b.wait("te", c_se, marks["exp"])
                    for ih in range(2):
                        b.op("te", lambda h, KTB=KTB, jb=jb, QTB=QTB, ih=ih: h.matmul(
                            PA[:, ih * 512:(ih + 1) * 512],
                            KTB(0, 64, jb * 128, (jb + 1) * 128),
                            QTB(0, 64, ih * 512, (ih + 1) * 512),
                            start=True, stop=True),
                            [(c_te, 1)] if ih == 1 else ())
                    qk_done = c_te.n
                    pt = (PT0, PT1)[jg % 2]
                    b.wait("se", c_te, qk_done)
                    if marks["pt_av"][jg % 2]:
                        b.wait("se", c_te, marks["pt_av"][jg % 2])
                    b.op("se", lambda h, pt=pt: h.activation(
                        pt(0, 128, 0, 1024), PA[:], AF.Exp,
                        bias=NEG8[:], scale=0.125), [(c_se, 1)])
                    marks["exp"] = c_se.n
                    b.wait("te", c_se, marks["exp"])
                    for ih in range(2):
                        pvt = (P4, P5)[ih]
                        b.op("te", lambda h, pvt=pvt, jg=jg, pt=pt, ih=ih: h.matmul(
                            pvt[0:65, 0:512],
                            VTM(0, 128, jg * 65, (jg + 1) * 65),
                            pt(0, 128, ih * 512, (ih + 1) * 512),
                            start=(jg == 0), stop=(jg == 15)),
                            [(c_te, 1)] if ih == 1 else ())
                    marks["pt_av"][jg % 2] = c_te.n
                av_done = c_te.n
                b.wait("ve", c_te, av_done)
                b.wait("ve", c_ve, marks["smr"])
                b.op("ve", lambda h: h.reciprocal(SMa[:], P4[64:65, 0:512]), [(c_ve, 1)])
                b.op("ve", lambda h: h.reciprocal(SMb[:], P5[64:65, 0:512]), [(c_ve, 1)])
                b.wait("ve", c_ve)
                b.op("ve", lambda h: h.tensor_copy(SMR(0, 1, 0, 512), SMa[:]), [(c_ve, 1)])
                b.op("ve", lambda h: h.tensor_copy(SMR(0, 1, 512, 1024), SMb[:]), [(c_ve, 1)])
                rec_done = c_ve.n
                marks["smr"] = rec_done
                b.wait("te", c_ve, rec_done)
                b.wait("te", c_ve, marks["p67_ve"])
                b.op("te", lambda h: h.matmul(P6[0:64, :], ONBR[0:1, 0:64], SMR(0, 1, 0, 512),
                                              start=True, stop=True), [(c_te, 1)])
                b.op("te", lambda h: h.matmul(P7[0:64, :], ONBR[0:1, 0:64], SMR(0, 1, 512, 1024),
                                              start=True, stop=True), [(c_te, 1)])
                rep_done = c_te.n
                b.wait("ve", c_te, rep_done)
                b.op("ve", lambda h: h.tensor_copy(RBC(0, 64, 0, 512), P6[0:64, :]), [(c_ve, 1)])
                b.op("ve", lambda h: h.tensor_copy(RBC(0, 64, 512, 1024), P7[0:64, :]), [(c_ve, 1)])
                marks["rbc_ve"] = c_ve.n
                marks["p67_ve"] = c_ve.n
                b.wait("ve", c_ve, marks["rbc_ve"])
                if marks["oproj_te"].get(h16 - 1) is not None and pi == 0:
                    b.wait("ve", c_te, marks["oproj_te"][h16 - 1])
                for ih in range(2):
                    pvt = (P4, P5)[ih]
                    b.op("ve", lambda h, pvt=pvt, pi=pi, ih=ih: h.tensor_tensor(
                        AOTh(64 * pi, 64 * pi + 64, ih * 512, (ih + 1) * 512),
                        pvt[0:64, 0:512],
                        RBC(0, 64, ih * 512, (ih + 1) * 512),
                        AluOpType.mult), [(c_ve, 1)])
                marks["aoth_ev"][(h16, pi)] = c_ve.n

            def emit_oproj(h16):
                hcol = 128 * h16
                opev = {}
                b.wait("te", c_ve, marks["aoth_ev"][(h16, 1)])
                for ccp in range(DC):
                    pvt = (P4, P5)[ccp % 2]
                    if ccp >= 2:
                        b.wait("te", c_ve, opev[ccp - 2])
                    for oc in range(DC):
                        b.op("te", lambda h, ccp=ccp, oc=oc, pvt=pvt: h.matmul(
                            pvt[:, 0:128],
                            WOT(0, 128, oc * 1024 + ccp * 128, oc * 1024 + (ccp + 1) * 128),
                            AOTh(0, 128, oc * 128, (oc + 1) * 128),
                            start=(oc == 0), stop=(oc == DC - 1)),
                            [(c_te, 1)] if oc == DC - 1 else ())
                    op_done = c_te.n
                    b.wait("ve", c_te, op_done)
                    b.op("ve", lambda h, ccp=ccp, hcol=hcol, pvt=pvt: h.tensor_tensor(
                        xsb_sl(ccp, hcol, 128), xsb_sl(ccp, hcol, 128),
                        pvt[:, 0:128], AluOpType.add), [(c_ve, 1)])
                    opev[ccp] = c_ve.n
                marks["oproj_te"][h16] = c_te.n
                marks["oproj_ve"][h16] = c_ve.n

            emit_proj(0, "q")
            emit_tr(0, "q")
            emit_proj(0, "k")
            emit_tr(0, "k")
            emit_proj(0, "v")
            for h16 in range(16):
                emit_round(h16, 0)
                if h16 < 15:
                    emit_proj(h16 + 1, "q")
                    emit_tr(h16 + 1, "q")
                emit_round(h16, 1)
                if h16 < 15:
                    emit_proj(h16 + 1, "k")
                    emit_tr(h16 + 1, "k")
                    emit_proj(h16 + 1, "v")
                emit_oproj(h16)
            return c_ve.n

        def emit_l1_moe(tag, scrow):
            ar = Arena()
            CBC0 = ar.alloc(2048)
            CBC1 = ar.alloc(2048)
            RWT = ar.alloc(128)
            RWN = ar.alloc(1024)
            LGT = ar.alloc(2048, F32)
            LTM = ar.alloc(256, F32)
            CTM = ar.alloc(256, F32)
            CBT = ar.alloc(2048, F32)
            RB16 = ar.alloc(1, F32)
            SCL = ar.alloc(8, F32)
            HS = ar.alloc(1024)
            HID0 = ar.alloc(1024)
            HID1 = ar.alloc(1024)
            SHHC = ar.alloc(2048)
            ovl = ar.top
            SHGT = ar.alloc(8192)
            SHUT = ar.alloc(8192)
            SHDT = ar.alloc(8192)
            WSTG2 = ar.alloc(2048)
            ar.reset(ovl)
            ESL0 = ar.alloc(12288)
            ESL1 = ar.alloc(12288)

            P6b = P6[:].bitcast(BF16)
            P7b = P7[:].bitcast(BF16)
            m1c = SCL(0, 128, 0, 1)
            m2c = SCL(0, 128, 1, 2)
            nm1c = SCL(0, 128, 2, 3)
            recc = SCL(0, 128, 3, 4)

            entry_te, entry_ve, entry_se = c_te.n, c_ve.n, c_se.n
            xn_done = emit_layernorm()
            b.wait("ge", c_te, entry_te)
            b.wait("ge", c_ve, entry_ve)
            b.wait("ge", c_se, entry_se)

            c_rt = b.ch("c_rt_" + tag, dma=True)
            b.op("ge", lambda h: h.dma_start(RWN(0, 16, 0, 1024), w["l1_rw"]), [(c_rt, 16)])
            b.op("ge", lambda h: h.dma_start(
                RB16(0, 16, 0, 1), w["l1_rb"].rearrange("(e o) -> e o", o=1)), [(c_rt, 16)])

            c_sh = [b.ch("c_sh0_" + tag, dma=True), b.ch("c_sh1_" + tag, dma=True)]
            stg_te = {}
            evac_ve = {}
            shsrc = [w["l1_shg"], w["l1_shu"], w["l1_shd"]]
            shdst = [SHGT, SHUT, SHDT]
            b.wait("te", c_ve, entry_ve)
            b.wait("te", c_se, entry_se)
            for idx in range(24):
                wi, oc = idx // 8, idx % 8
                slot = idx % 2
                if idx >= 2:
                    b.wait("ge", c_te, stg_te[idx - 2])
                b.op("ge", lambda h, wi=wi, oc=oc, slot=slot: h.dma_start(
                    WSTG2(0, 128, slot * 1024, slot * 1024 + 1024),
                    shsrc[wi][oc * 128:(oc + 1) * 128, :]), [(c_sh[slot], 16)])
                b.wait("te", c_sh[slot], 16 * (idx // 2 + 1))
                if idx >= 2:
                    b.wait("te", c_ve, evac_ve[idx - 2])
                pb_t = (P6b, P7b)[slot]
                for cc in range(DC):
                    b.op("te", lambda h, cc=cc, slot=slot, pb_t=pb_t: h.transpose(
                        pb_t[:, cc * 128:(cc + 1) * 128],
                        WSTG2(0, 128, slot * 1024 + cc * 128, slot * 1024 + (cc + 1) * 128),
                        IDB[:]), [(c_te, 1)] if cc == DC - 1 else ())
                stg_te[idx] = c_te.n
                b.wait("ve", c_te, stg_te[idx])
                b.op("ve", lambda h, wi=wi, oc=oc, pb_t=pb_t: h.tensor_copy(
                    shdst[wi](0, 128, 0, 8192).rearrange("p (cc o) -> p cc o", cc=DC)[
                        :, :, oc * 128:(oc + 1) * 128],
                    pb_t.rearrange("p (cc s) -> p cc s", cc=DC)), [(c_ve, 1)])
                evac_ve[idx] = c_ve.n
            shw_ve = c_ve.n

            b.wait("te", c_rt)
            b.wait("te", c_ve, xn_done)
            b.wait("te", c_ve, shw_ve)
            for cc in range(DC):
                b.op("te", lambda h, cc=cc: h.transpose(
                    P6b[:, cc * 16:(cc + 1) * 16],
                    RWN(0, 16, cc * 128, (cc + 1) * 128), IDB[0:16, 0:16]),
                    [(c_te, 1)] if cc == DC - 1 else ())
            rwt_tr = c_te.n
            b.wait("ve", c_te, rwt_tr)
            b.op("ve", lambda h: h.tensor_copy(RWT(0, 128, 0, 128), P6b[:, 0:128]), [(c_ve, 1)])
            rwt_done = c_ve.n
            b.wait("te", c_ve, rwt_done)
            lg_se = {0: 0, 1: 0}
            for t in range(NT):
                t0 = t * 512
                if lg_se[t % 2]:
                    b.wait("te", c_ve, lg_se[t % 2])
                for cc in range(DC):
                    b.op("te", lambda h, cc=cc, t0=t0, t=t: h.matmul(
                        PB[0:16, (t % 2) * 512:(t % 2) * 512 + 512],
                        RWT(0, 128, cc * 16, (cc + 1) * 16),
                        xn_sl(cc, t0, 512),
                        start=(cc == 0), stop=(cc == DC - 1)),
                        [(c_te, 1)] if cc == DC - 1 else ())
                lgm = c_te.n
                b.wait("ve", c_te, lgm)
                b.wait("ve", c_rt)
                b.op("ve", lambda h, t=t, t0=t0: h.tensor_scalar(
                    LGT(0, 16, t0, t0 + 512), PB[0:16, (t % 2) * 512:(t % 2) * 512 + 512],
                    RB16(0, 16, 0, 1), None, AluOpType.add), [(c_ve, 1)])
                lg_se[t % 2] = c_ve.n
            lgt_done = c_ve.n
            b.wait("te", c_ve, lgt_done)
            for tt in range(16):
                pvt = (P4, P5)[tt // 8]
                b.op("te", lambda h, tt=tt, pvt=pvt: h.transpose(
                    pvt[:, (tt % 8) * 16:(tt % 8) * 16 + 16],
                    LGT(0, 16, tt * 128, (tt + 1) * 128), IDF[0:16, 0:16]),
                    [(c_te, 1)] if tt in (7, 15) else ())
            ltm_tr = c_te.n
            b.wait("ve", c_te, ltm_tr)
            b.op("ve", lambda h: h.tensor_copy(LTM(0, 128, 0, 128), P4[:, 0:128]), [(c_ve, 1)])
            b.op("ve", lambda h: h.tensor_copy(LTM(0, 128, 128, 256), P5[:, 0:128]), [(c_ve, 1)])
            ltm_done = c_ve.n
            b.wait("ve", c_ve, ltm_done)
            for tt in range(16):
                lg = LTM(0, 128, tt * 16, (tt + 1) * 16)
                cm = CTM(0, 128, tt * 16, (tt + 1) * 16)
                b.op("ve", lambda h, lg=lg: h.reduce_max(m1c, lg, axis=AX.X), [(c_ve, 1)])
                b.wait("ve", c_ve)
                b.op("ve", lambda h, lg=lg, cm=cm: h.tensor_scalar(
                    cm, lg, m1c, -1e30, AluOpType.is_ge, AluOpType.mult), [(c_ve, 1)])
                b.wait("ve", c_ve)
                b.op("ve", lambda h, lg=lg, cm=cm: h.tensor_tensor(
                    cm, cm, lg, AluOpType.add), [(c_ve, 1)])
                b.wait("ve", c_ve)
                b.op("ve", lambda h, cm=cm: h.reduce_max(m2c, cm, axis=AX.X), [(c_ve, 1)])
                b.op("ve", lambda h: h.tensor_scalar(
                    nm1c, m1c, -1.0, None, AluOpType.mult), [(c_ve, 1)])
                nm1_done = c_ve.n
                b.wait("se", c_ve, nm1_done)
                b.op("se", lambda h, lg=lg, cm=cm: h.activation(
                    cm, lg, AF.Exp, bias=nm1c), [(c_se, 1)])
                eb_done = c_se.n
                b.wait("ve", c_se, eb_done)
                b.op("ve", lambda h, lg=lg: h.tensor_scalar(
                    lg, lg, m2c, None, AluOpType.is_ge), [(c_ve, 1)])
                b.wait("ve", c_ve)
                b.op("ve", lambda h, lg=lg, cm=cm: h.tensor_tensor(
                    cm, cm, lg, AluOpType.mult), [(c_ve, 1)])
                b.wait("ve", c_ve)
                b.op("ve", lambda h, cm=cm: h.reduce_sum(recc, cm, axis=AX.X), [(c_ve, 1)])
                b.wait("ve", c_ve)
                b.op("ve", lambda h: h.reciprocal(recc, recc), [(c_ve, 1)])
                b.wait("ve", c_ve)
                b.op("ve", lambda h, cm=cm: h.tensor_scalar(
                    cm, cm, recc, 0.5, AluOpType.mult, AluOpType.mult), [(c_ve, 1)])
                b.wait("ve", c_ve)
            ctm_done = c_ve.n
            b.wait("te", c_ve, ctm_done)
            cb_ev = {}
            for tt in range(16):
                pvt = (P4, P5)[tt % 2]
                if tt >= 2:
                    b.wait("te", c_ve, cb_ev[tt - 2])
                b.op("te", lambda h, tt=tt, pvt=pvt: h.transpose(
                    pvt[0:16, 0:128],
                    CTM(0, 128, tt * 16, (tt + 1) * 16), IDF[:]), [(c_te, 1)])
                ctr = c_te.n
                b.wait("ve", c_te, ctr)
                b.op("ve", lambda h, tt=tt, pvt=pvt: h.tensor_copy(
                    CBT(0, 16, tt * 128, (tt + 1) * 128), pvt[0:16, 0:128]), [(c_ve, 1)])
                cb_ev[tt] = c_ve.n
            cbt_done = c_ve.n
            b.wait("ge", c_ve, cbt_done)
            c_scw2 = b.ch("c_scw_" + tag, dma=True)
            b.op("ge", lambda h: h.dma_start(
                scr[scrow:scrow + 16, :], CBT(0, 16, 0, T)), [(c_scw2, 16)])
            b.wait("ge", c_scw2)

            b.wait("te", c_ve, shw_ve)
            b.wait("te", c_ve, lgt_done)
            sh_se = 0
            sh_hid_ve = 0
            sh_x_ve = {}
            sh_dn_te = {}
            for t in range(8):
                t0 = t * 256
                for grp in range(2):
                    if sh_se:
                        b.wait("te", c_se, sh_se)
                        b.wait("te", c_ve, sh_hid_ve)
                    for scn4 in range(4):
                        scn = grp * 4 + scn4
                        for cc in range(DC):
                            b.op("te", lambda h, scn=scn, scn4=scn4, cc=cc, t0=t0: h.matmul(
                                PA[:, scn4 * 256:(scn4 + 1) * 256],
                                SHGT(0, 128, cc * 1024 + scn * 128, cc * 1024 + (scn + 1) * 128),
                                xn_sl(cc, t0, 256),
                                start=(cc == 0), stop=(cc == DC - 1)),
                                [(c_te, 1)] if cc == DC - 1 else ())
                    for scn4 in range(4):
                        scn = grp * 4 + scn4
                        for cc in range(DC):
                            b.op("te", lambda h, scn=scn, scn4=scn4, cc=cc, t0=t0: h.matmul(
                                PB[:, scn4 * 256:(scn4 + 1) * 256],
                                SHUT(0, 128, cc * 1024 + scn * 128, cc * 1024 + (scn + 1) * 128),
                                xn_sl(cc, t0, 256),
                                start=(cc == 0), stop=(cc == DC - 1)),
                                [(c_te, 1)] if cc == DC - 1 else ())
                    gu = c_te.n
                    b.wait("se", c_te, gu)
                    if sh_hid_ve:
                        b.wait("se", c_ve, sh_hid_ve)
                    b.op("se", lambda h: h.activation(
                        HS(0, 128, 0, 1024), PA[:], AF.Silu), [(c_se, 1)])
                    sh_se = c_se.n
                    b.wait("ve", c_se, sh_se)
                    b.op("ve", lambda h: h.tensor_tensor(
                        HS(0, 128, 0, 1024), HS(0, 128, 0, 1024), PB[:],
                        AluOpType.mult), [(c_ve, 1)])
                    b.wait("ve", c_ve)
                    b.op("ve", lambda h, grp=grp: h.tensor_scalar(
                        SHHC(0, 128, grp * 1024, (grp + 1) * 1024),
                        HS(0, 128, 0, 1024), 0.5, None, AluOpType.mult), [(c_ve, 1)])
                    sh_hid_ve = c_ve.n
                b.wait("te", c_ve, sh_hid_ve)
                if t >= 1:
                    b.wait("te", c_ve, sh_x_ve[t - 1])
                for v in range(DC):
                    pvt = (P4, P5, P6, P7)[v // 2]
                    off = (v % 2) * 256
                    for scn in range(DC):
                        b.op("te", lambda h, pvt=pvt, off=off, scn=scn, v=v: h.matmul(
                            pvt[:, off:off + 256],
                            SHDT(0, 128, scn * 1024 + v * 128, scn * 1024 + (v + 1) * 128),
                            SHHC(0, 128, scn * 256, (scn + 1) * 256),
                            start=(scn == 0 and v % 2 == 0),
                            stop=(scn == DC - 1 and v % 2 == 1)),
                            [(c_te, 1)] if (scn == DC - 1 and v == DC - 1) else ())
                sh_dn_te[t] = c_te.n
                b.wait("ve", c_te, sh_dn_te[t])
                for bk in range(4):
                    b.op("ve", lambda h, bk=bk, t0=t0: h.tensor_tensor(
                        XSB[:].rearrange("p (c t) -> p c t", c=DC)[:, bk * 2: bk * 2 + 2, t0:t0 + 256],
                        XSB[:].rearrange("p (c t) -> p c t", c=DC)[:, bk * 2: bk * 2 + 2, t0:t0 + 256],
                        (P4, P5, P6, P7)[bk][:].rearrange("p (u t) -> p u t", u=2),
                        AluOpType.add), [(c_ve, 1)])
                sh_x_ve[t] = c_ve.n
            sh_done_te, sh_done_ve, sh_done_se = c_te.n, c_ve.n, c_se.n

            b.wait("ge", c_te, sh_done_te)
            b.wait("ge", c_ve, sh_done_ve)

            c_ew = [b.ch("c_ew0_" + tag, dma=True), b.ch("c_ew1_" + tag, dma=True)]
            c_cb = [b.ch("c_cb0_" + tag, dma=True), b.ch("c_cb1_" + tag, dma=True)]
            gate_r = w["l1_gate"].rearrange("e (cc p) hh -> e p cc hh", p=128)
            up_r = w["l1_up"].rearrange("e (cc p) hh -> e p cc hh", p=128)
            down_r = w["l1_down"].rearrange("e (hc p) v -> e p hc v", p=128)
            eslot = (ESL0, ESL1)
            b.wait("te", c_se, sh_done_se)
            b.wait("te", c_ve, sh_done_ve)
            b.wait("se", c_ve, sh_done_ve)
            e_te_done = {}
            e_ve_done = {}
            silu_se = {}
            hid_ve = {}
            dn_te = {}
            xadd_ve = {}
            def emit_gu(g):
                e, t = g // 8, g % 8
                ES = eslot[e % 2]
                t0 = t * 256
                if g >= 1:
                    b.wait("te", c_se, silu_se[g - 1])
                    b.wait("te", c_ve, hid_ve[g - 1])
                for hc in range(4):
                    for cc in range(DC):
                        b.op("te", lambda h, hc=hc, cc=cc, t0=t0, ES=ES: h.matmul(
                            PA[:, hc * 256:(hc + 1) * 256],
                            ES(0, 128, cc * 512 + hc * 128, cc * 512 + (hc + 1) * 128),
                            xn_sl(cc, t0, 256),
                            start=(cc == 0), stop=(cc == DC - 1)),
                            [(c_te, 1)] if cc == DC - 1 else ())
                for hc in range(4):
                    for cc in range(DC):
                        b.op("te", lambda h, hc=hc, cc=cc, t0=t0, ES=ES: h.matmul(
                            PB[:, hc * 256:(hc + 1) * 256],
                            ES(0, 128, 4096 + cc * 512 + hc * 128, 4096 + cc * 512 + (hc + 1) * 128),
                            xn_sl(cc, t0, 256),
                            start=(cc == 0), stop=(cc == DC - 1)),
                            [(c_te, 1)] if cc == DC - 1 else ())
                gu_done = c_te.n
                b.wait("se", c_te, gu_done)
                if g >= 1:
                    b.wait("se", c_ve, hid_ve[g - 1])
                b.op("se", lambda h: h.activation(
                    HS(0, 128, 0, 1024), PA[:], AF.Silu), [(c_se, 1)])
                silu_se[g] = c_se.n
                HD = (HID0, HID1)[g % 2]
                b.wait("ve", c_se, silu_se[g])
                if g >= 2:
                    b.wait("ve", c_te, dn_te[g - 2])
                sl = e % 2
                b.wait("ve", c_cb[sl], cb_target[e])
                b.op("ve", lambda h, HD=HD: h.tensor_tensor(
                    HD(0, 128, 0, 1024), HS(0, 128, 0, 1024), PB[:],
                    AluOpType.mult), [(c_ve, 1)])
                b.wait("ve", c_ve)
                CBCe = (CBC0, CBC1)[sl]
                for hc in range(4):
                    b.op("ve", lambda h, HD=HD, hc=hc, t0=t0, CBCe=CBCe: h.tensor_tensor(
                        HD(0, 128, hc * 256, (hc + 1) * 256),
                        HD(0, 128, hc * 256, (hc + 1) * 256),
                        CBCe(0, 128, t0, t0 + 256), AluOpType.mult), [(c_ve, 1)])
                hid_ve[g] = c_ve.n

            def emit_down(g):
                e, t = g // 8, g % 8
                ES = eslot[e % 2]
                HD = (HID0, HID1)[g % 2]
                t0 = t * 256
                b.wait("te", c_ve, hid_ve[g])
                if g >= 1:
                    b.wait("te", c_ve, xadd_ve[g - 1])
                for v in range(DC):
                    pvt = (P4, P5, P6, P7)[v // 2]
                    off = (v % 2) * 256
                    for hc in range(4):
                        b.op("te", lambda h, pvt=pvt, off=off, hc=hc, v=v, ES=ES, HD=HD: h.matmul(
                            pvt[:, off:off + 256],
                            ES(0, 128, 8192 + hc * 1024 + v * 128, 8192 + hc * 1024 + (v + 1) * 128),
                            HD(0, 128, hc * 256, (hc + 1) * 256),
                            start=(hc == 0 and v % 2 == 0),
                            stop=(hc == 3 and v % 2 == 1)),
                            [(c_te, 1)] if (hc == 3 and v == DC - 1) else ())
                dn_te[g] = c_te.n
                b.wait("ve", c_te, dn_te[g])
                for bk in range(4):
                    b.op("ve", lambda h, bk=bk, t0=t0: h.tensor_tensor(
                        XSB[:].rearrange("p (c t) -> p c t", c=DC)[:, bk * 2: bk * 2 + 2, t0:t0 + 256],
                        XSB[:].rearrange("p (c t) -> p c t", c=DC)[:, bk * 2: bk * 2 + 2, t0:t0 + 256],
                        (P4, P5, P6, P7)[bk][:].rearrange("p (u t) -> p u t", u=2),
                        AluOpType.add), [(c_ve, 1)])
                xadd_ve[g] = c_ve.n

            cb_target = {}
            for e in range(16):
                sl = e % 2
                ES = eslot[sl]
                CBCe = (CBC0, CBC1)[sl]
                if e >= 2:
                    b.wait("ge", c_te, dn_te[(e - 2) * 8 + 7])
                    b.wait("ge", c_ve, xadd_ve[(e - 2) * 8 + 7])
                b.op("ge", lambda h, e=e, ES=ES: h.dma_start(
                    ES(0, 128, 0, 4096).rearrange("p (cc hh) -> p cc hh", cc=DC),
                    gate_r[e]), [(c_ew[sl], 16)])
                b.op("ge", lambda h, e=e, ES=ES: h.dma_start(
                    ES(0, 128, 4096, 8192).rearrange("p (cc hh) -> p cc hh", cc=DC),
                    up_r[e]), [(c_ew[sl], 16)])
                b.op("ge", lambda h, e=e, ES=ES: h.dma_start(
                    ES(0, 128, 8192, 12288).rearrange("p (hc v) -> p hc v", hc=4),
                    down_r[e]), [(c_ew[sl], 16)])
                b.op("ge", lambda h, e=e, CBCe=CBCe, scrow=scrow: h.dma_start(
                    CBCe(0, 128, 0, T),
                    scr[scrow + e:scrow + e + 1, :].broadcast_to((128, T))), [(c_cb[sl], 16)])
                cb_target[e] = c_cb[sl].n
                b.wait("te", c_ew[sl], 16 * 3 * (e // 2 + 1))
                for t in range(8):
                    g = e * 8 + t
                    emit_gu(g)
                    if g >= 1:
                        emit_down(g - 1)
            emit_down(127)
            return c_ve.n

        def emit_l1_conv():
            ar = Arena()
            W1B = ar.alloc(8192)
            W2B = ar.alloc(8192)
            HB = ar.alloc(16384)
            ETN = ar.alloc(1024, F32)
            ETT = ar.alloc(256, F32)
            DGT = ar.alloc(2560)
            MASK = ar.alloc(128)
            IOT = ar.alloc(512)
            BIASR = ar.alloc(1024)
            ONR = ar.alloc(512)

            entry_te, entry_ve, entry_se = c_te.n, c_ve.n, c_se.n
            xn_done = emit_layernorm()
            for _e in ("ge", "sy"):
                b.wait(_e, c_te, entry_te)
                b.wait(_e, c_ve, entry_ve)
                b.wait(_e, c_se, entry_se)
            b.dma_barrier()

            c_cw = b.ch("c_cw", dma=True)
            c_cwh = b.ch("c_cwh", dma=True)
            w1_r = w["l1_W1"].rearrange("(dc p) o -> p dc o", p=128)
            w2_r = w["l1_W2"].rearrange("(dc p) o -> p dc o", p=128)
            b.op("ge", lambda h: h.dma_start(
                W1B(0, 128, 0, 8192).rearrange("p (dc o) -> p dc o", dc=DC), w1_r),
                [(c_cw, 16)])
            b.op("ge", lambda h: h.dma_start(
                W2B(0, 128, 0, 8192).rearrange("p (dc o) -> p dc o", dc=DC), w2_r),
                [(c_cw, 16)])
            b.op("sy", lambda h: h.dma_start(ETN(0, 21, 0, 1024), w["l1_et"]), [(c_cwh, 16)])
            b.op("ge", lambda h: h.dma_start(
                BIASR(0, 1, 0, 1024), w["l1_bias"].rearrange("(o d) -> o d", o=1)),
                [(c_cw, 16)])
            b.op("ge", lambda h: h.memset(ONR(0, 1, 0, 512), 1.0), [(c_ge, 1)])
            pi32 = IOT(0, 128, 0, 256).bitcast(mybir.dt.int32)
            fi32 = IOT(0, 128, 256, 512).bitcast(mybir.dt.int32)
            b.op("ge", lambda h: h.iota(pi32, pattern=[[0, 128]], channel_multiplier=1),
                 [(c_ge, 1)])
            b.op("ge", lambda h: h.iota(fi32, pattern=[[1, 128]], channel_multiplier=0),
                 [(c_ge, 1)])
            init_ge2 = c_ge.n
            b.wait("ve", c_ge, init_ge2)
            b.op("ve", lambda h: h.tensor_tensor(
                MASK(0, 128, 0, 128), pi32, fi32, AluOpType.is_equal), [(c_ve, 1)])
            mask_done = c_ve.n

            b.wait("te", c_cw)
            b.wait("te", c_cwh)
            b.wait("te", c_ve, entry_ve)
            b.wait("te", c_se, entry_se)
            for dc in range(DC):
                b.op("te", lambda h, dc=dc: h.transpose(
                    P6[:, dc * 32: dc * 32 + 21],
                    ETN(0, 21, dc * 128, (dc + 1) * 128), IDF[0:21, 0:21]),
                    [(c_te, 1)] if dc == DC - 1 else ())
            ett_tr = c_te.n
            b.wait("ve", c_te, ett_tr)
            b.op("ve", lambda h: h.tensor_copy(ETT(0, 128, 0, 256), P6[:, 0:256]),
                 [(c_ve, 1)])
            ett_done = c_ve.n

            b.wait("te", c_ve, xn_done)
            hb_ev = {}
            cv_te = {}
            for dc in range(DC):
                b.wait("ve", c_ve, max(ett_done, mask_done))
                if dc >= 1:
                    b.wait("ve", c_te, cv_te[dc - 1])
                for m in range(1, 21):
                    b.op("ve", lambda h, dc=dc, m=m: h.tensor_scalar(
                        DGT(0, 128, (m - 1) * 128, m * 128),
                        MASK(0, 128, 0, 128),
                        ETT(0, 128, dc * 32 + m, dc * 32 + m + 1), None,
                        AluOpType.mult), [(c_ve, 1)])
                dg_done = c_ve.n
                b.wait("te", c_ve, dg_done)
                for t in range(NT):
                    t0 = t * 512
                    half = (dc * NT + t) % 2
                    if dc * NT + t >= 2:
                        b.wait("te", c_ve, hb_ev[dc * NT + t - 2])
                    for m in range(1, 21):
                        b.op("te", lambda h, dc=dc, m=m, t0=t0, half=half: h.matmul(
                            PA[:, half * 512:(half + 1) * 512],
                            DGT(0, 128, (m - 1) * 128, m * 128),
                            XN[:, dc * PADT + PADW + t0 - m: dc * PADT + PADW + t0 - m + 512],
                            start=(m == 1), stop=(m == 20)),
                            [(c_te, 1)] if m == 20 else ())
                    cv_te[dc] = c_te.n
                    b.wait("ve", c_te, cv_te[dc])
                    b.op("ve", lambda h, dc=dc, t0=t0, half=half: h.tensor_copy(
                        HB(0, 128, dc * T + t0, dc * T + t0 + 512),
                        PA[:, half * 512:(half + 1) * 512]), [(c_ve, 1)])
                    hb_ev[dc * NT + t] = c_ve.n
            conv_ve = c_ve.n

            b.wait("te", c_ve, conv_ve)
            wp_ev = {}
            for t in range(NT):
                t0 = t * 512
                for oc in range(DC):
                    gi = t * DC + oc
                    pvt = (P4, P5, P6)[gi % 3]
                    if gi >= 3:
                        b.wait("te", c_ve, wp_ev[gi - 3])
                    for dc in range(DC):
                        b.op("te", lambda h, dc=dc, oc=oc, t0=t0, pvt=pvt: h.matmul(
                            pvt[:, 0:512],
                            W1B(0, 128, dc * 1024 + oc * 128, dc * 1024 + (oc + 1) * 128),
                            xn_sl(dc, t0, 512),
                            start=(dc == 0), stop=False),
                            ())
                    for dc in range(DC):
                        b.op("te", lambda h, dc=dc, oc=oc, t0=t0, pvt=pvt: h.matmul(
                            pvt[:, 0:512],
                            W2B(0, 128, dc * 1024 + oc * 128, dc * 1024 + (oc + 1) * 128),
                            HB(0, 128, dc * T + t0, dc * T + t0 + 512),
                            start=False, stop=False),
                            ())
                    b.op("te", lambda h, oc=oc, pvt=pvt: h.matmul(
                        pvt[:, 0:512],
                        BIASR(0, 1, oc * 128, (oc + 1) * 128),
                        ONR(0, 1, 0, 512),
                        start=False, stop=True), [(c_te, 1)])
                    wpm = c_te.n
                    b.wait("ve", c_te, wpm)
                    b.op("ve", lambda h, oc=oc, t0=t0, pvt=pvt: h.tensor_tensor(
                        xsb_sl(oc, t0, 512), xsb_sl(oc, t0, 512), pvt[:, 0:512],
                        AluOpType.add), [(c_ve, 1)])
                    wp_ev[gi] = c_ve.n
            return c_ve.n

        res_marks = {}
        for ph in phases:
            if ph in ("l0ffn1", "l0ffn2"):
                emit_l0_ffn(ph)
            elif ph == "l0attn":
                emit_l0_attn()
            elif ph == "l1moe1":
                emit_l1_moe(ph, 16)
            elif ph == "l1moe2":
                emit_l1_moe(ph, 32)
            elif ph == "l1conv":
                emit_l1_conv()

        last_ve = c_ve.n
        b.wait("sy", c_ve, last_ve)
        yTr = yT.rearrange("(c p) t -> p c t", p=128)
        b.op("sy", lambda h: h.dma_start(
            yTr, XSB[:].rearrange("p (c t) -> p c t", c=DC)), [(c_out, 16)])
        b.wait("sy", c_out)

        blk = es.enter_context(nc.Block())
        b.emit(blk)
    return nc


def _mk_inmaps(inputs):
    import ml_dtypes
    x = inputs["L0_attn_x"]
    consts = {
        "identb": np.eye(128, dtype=ml_dtypes.bfloat16),
        "identf": np.eye(128, dtype=np.float32),
        "onesb": np.ones((128, 1), dtype=ml_dtypes.bfloat16),
        "onesfr": np.ones((1, 128), dtype=np.float32),
        "onesbr": np.ones((1, 128), dtype=ml_dtypes.bfloat16),
    }
    wkeys = [k for k in inputs if k != "L0_attn_x"]
    maps = []
    for core in range(8):
        m = {"xT": np.ascontiguousarray(x[core].T)}
        for k in wkeys:
            m[k] = np.ascontiguousarray(inputs[k])
        m.update(consts)
        maps.append(m)
    return maps


def kernel(**inputs):
    nc = build_program(["l0ffn1", "l0attn", "l0ffn2", "l1moe1", "l1conv", "l1moe2"])
    maps = _mk_inmaps(inputs)
    res = run_bass_kernel_spmd(nc, maps, core_ids=list(range(8)))
    out = np.stack([np.asarray(r["yT"]).T for r in res.results], axis=0)
    return np.ascontiguousarray(out.astype(np.float32))


# version 04
# speedup vs baseline: 1.4264x; 1.0037x over previous
import contextlib
import numpy as np
import concourse.bass as bass
import concourse.mybir as mybir
from concourse.bass_utils import run_bass_kernel_spmd
from concourse.alu_op_type import AluOpType

F32 = mybir.dt.float32
BF16 = mybir.dt.bfloat16
AF = mybir.ActivationFunctionType
AX = mybir.AxisListType

D = 1024
T = 2048
DC = D // 128
NT = T // 512
PADW = 24
PADT = T + PADW

ENGINES = ("te", "ve", "se", "sy", "ge")


class Chan:
    __slots__ = ("sem", "n", "name")

    def __init__(self, sem, name):
        self.sem = sem
        self.n = 0
        self.name = name


class Builder:
    def __init__(self, nc, es):
        self.nc = nc
        self.es = es
        self.ops = {e: [] for e in ENGINES}
        self.chans = {}
        self.dma_chans = []

    def ch(self, name, dma=False):
        if name not in self.chans:
            sem = self.es.enter_context(self.nc.semaphore(name))
            self.chans[name] = Chan(sem, name)
            if dma:
                self.dma_chans.append(self.chans[name])
        return self.chans[name]

    def dma_barrier(self, engines=("ge", "sy")):
        for c in self.dma_chans:
            if c.n:
                for e in engines:
                    self.wait(e, c)

    def op(self, eng, f, incs=()):
        for c, a in incs:
            c.n += a
        semincs = tuple((c.sem, a) for c, a in incs)

        def g(h, f=f, semincs=semincs):
            ins = f(h)
            for s, a in semincs:
                ins = ins.then_inc(s, a)
        self.ops[eng].append(g)

    def wait(self, eng, c, tgt=None):
        t = c.n if tgt is None else tgt
        assert t <= c.n, (c.name, t, c.n)
        self.ops[eng].append(lambda h, s=c.sem, t=t: h.wait_ge(s, t))

    def emit(self, blk):
        for name, attr in (("te", "tensor"), ("ve", "vector"), ("se", "scalar"),
                           ("sy", "sync"), ("ge", "gpsimd")):
            ops = self.ops[name]

            def section(h, ops=ops):
                for f in ops:
                    f(h)
            getattr(blk, attr)(section)


def ceil_div(a, b):
    return (a + b - 1) // b


def build_program(phases):
    nc = bass.Bass("TRN2", target_bir_lowering=False)

    def din(name, shape, dtype=F32):
        return nc.dram_tensor(name, list(shape), dtype, kind="ExternalInput").ap()

    xT = din("xT", (D, T))
    w = {}
    w["l0_wq"] = din("L0_attn_wq", (D, D))
    w["l0_wk"] = din("L0_attn_wk", (D, D))
    w["l0_wv"] = din("L0_attn_wv", (D, D))
    w["l0_wo"] = din("L0_attn_wo", (D, D))
    w["l0_sel"] = din("L0_ffn_expert_sel", (16, D))
    w["l0_keys"] = din("L0_ffn_keys", (16, D, 128))
    w["l0_vals"] = din("L0_ffn_values", (16, 128, D))
    w["l1_W1"] = din("L1_attn_W1", (D, D))
    w["l1_W2"] = din("L1_attn_W2", (D, D))
    w["l1_bias"] = din("L1_attn_bias", (D,))
    w["l1_et"] = din("L1_attn_embeddings_table", (21, D))
    w["l1_rw"] = din("L1_ffn_router_w", (16, D))
    w["l1_rb"] = din("L1_ffn_router_bias", (16,))
    w["l1_up"] = din("L1_ffn_up_proj", (16, D, 512))
    w["l1_gate"] = din("L1_ffn_gate_proj", (16, D, 512))
    w["l1_down"] = din("L1_ffn_down_proj", (16, 512, D))
    w["l1_shg"] = din("L1_ffn_sh_gate", (D, D))
    w["l1_shu"] = din("L1_ffn_sh_up", (D, D))
    w["l1_shd"] = din("L1_ffn_sh_down", (D, D))
    identb = din("identb", (128, 128), BF16)
    identf = din("identf", (128, 128), F32)
    onesb = din("onesb", (128, 1), BF16)
    onesfr = din("onesfr", (1, 128), F32)
    onesbr = din("onesbr", (1, 128), BF16)

    yT = nc.dram_tensor("yT", [D, T], F32, kind="ExternalOutput").ap()
    scr = nc.dram_tensor("scr", [64, T], F32, kind="Internal").ap()

    es = contextlib.ExitStack()
    with es:
        b = Builder(nc, es)
        sb = lambda name, shape, dtype: es.enter_context(nc.sbuf_tensor(name, list(shape), dtype))
        ps = lambda name, shape, dtype=F32: es.enter_context(nc.psum_tensor(name, list(shape), dtype))

        XSB = sb("XSB", (128, DC * T), F32)
        XN = sb("XN", (128, DC * PADT), BF16)
        AR = sb("AR", (128, 49088), BF16)
        IDB = sb("IDB", (128, 128), BF16)
        IDF = sb("IDF", (128, 128), F32)
        ONB = sb("ONB", (128, 1), BF16)
        ONFR = sb("ONFR", (1, 128), F32)
        ONBR = sb("ONBR", (1, 128), BF16)
        EPS = sb("EPS", (128, 1), F32)
        NEG8 = sb("NEG8", (128, 1), F32)
        SMa = sb("SMa", (1, 512), F32)
        SMb = sb("SMb", (1, 512), F32)
        SMc = sb("SMc", (1, 512), F32)
        LNC = sb("LNC", (128, 4096), BF16)

        PA = ps("PA", (128, 1024))
        PB = ps("PB", (128, 1024))
        P4 = ps("P4", (128, 512))
        P5 = ps("P5", (128, 512))
        P6 = ps("P6", (128, 512))
        P7 = ps("P7", (128, 512))

        c_te = b.ch("c_te")
        c_ve = b.ch("c_ve")
        c_se = b.ch("c_se")
        c_ge = b.ch("c_ge")
        c_sy = b.ch("c_sy", dma=True)
        c_init = b.ch("c_init", dma=True)
        c_xl = b.ch("c_xl", dma=True)
        c_gw = b.ch("c_gw", dma=True)
        c_bc0 = b.ch("c_bc0", dma=True)
        c_bc1 = b.ch("c_bc1", dma=True)
        c_scw = b.ch("c_scw", dma=True)
        c_out = b.ch("c_out", dma=True)

        def xn_sl(c, t0, tw):
            return XN[:, c * PADT + PADW + t0: c * PADT + PADW + t0 + tw]

        def xsb_sl(c, t0, tw):
            return XSB[:, c * T + t0: c * T + t0 + tw]

        class Arena:
            def __init__(self):
                self.top = 0

            def alloc(self, cols, dtype=BF16):
                bcols = cols * (2 if dtype == F32 else 1)
                bcols = (bcols + 63) // 64 * 64
                c0 = self.top
                self.top += bcols
                assert self.top <= 49088, f"arena overflow {self.top}"
                if dtype == BF16:
                    return lambda p0, p1, f0, f1: AR[p0:p1, c0 + f0: c0 + f1]
                else:
                    arf = AR[:].bitcast(F32)
                    c0f = c0 // 2
                    return lambda p0, p1, f0, f1: arf[p0:p1, c0f + f0: c0f + f1]

            def reset(self, top=0):
                self.top = top

        b.op("sy", lambda h: h.dma_start(IDB[:], identb), [(c_init, 16)])
        b.op("sy", lambda h: h.dma_start(IDF[:], identf), [(c_init, 16)])
        b.op("sy", lambda h: h.dma_start(ONB[:], onesb), [(c_init, 16)])
        b.op("sy", lambda h: h.dma_start(ONFR[:], onesfr), [(c_init, 16)])
        b.op("sy", lambda h: h.dma_start(ONBR[:], onesbr), [(c_init, 16)])
        b.op("ge", lambda h: h.memset(EPS[:], 1e-5), [(c_ge, 1)])
        b.op("ge", lambda h: h.memset(NEG8[:], -8.0), [(c_ge, 1)])
        for c in range(DC):
            b.op("ge", lambda h, c=c: h.memset(XN[:, c * PADT: c * PADT + PADW], 0.0),
                 [(c_ge, 1)])
        init_ge = c_ge.n

        xTr = xT.rearrange("(c p) t -> p c t", p=128)
        b.op("sy", lambda h: h.dma_start(
            XSB[:].rearrange("p (c t) -> p c t", c=DC), xTr), [(c_xl, 16)])

        b.wait("ve", c_xl)
        b.wait("te", c_init)
        b.wait("te", c_ge, init_ge)
        b.wait("se", c_ge, init_ge)

        def emit_layernorm():
            b.wait("ve", c_ve)
            b.wait("te", c_se)
            prev_sumsq = 0
            prev_affine = 0
            for t in range(NT):
                t0 = t * 512
                xv = XSB[:].rearrange("p (c t) -> p c t", c=DC)[:, :, t0:t0 + 512]
                lnc3 = LNC[:].rearrange("p (c t) -> p c t", c=DC)[:, :, 0:512]
                if t > 0:
                    b.wait("ve", c_te, prev_sumsq)
                    b.wait("ve", c_ve, prev_affine)
                b.op("ve", lambda h, xv=xv, lnc3=lnc3: h.tensor_copy(lnc3, xv), [(c_ve, 1)])
                cast_done = c_ve.n
                b.wait("te", c_ve, cast_done)
                for c in range(DC):
                    b.op("te", lambda h, c=c: h.matmul(
                        PA[0:1, 0:512], ONB[:], LNC[:, c * 512:(c + 1) * 512],
                        start=(c == 0), stop=(c == DC - 1)),
                        [(c_te, 1)] if c == DC - 1 else ())
                sum_done = c_te.n
                b.wait("se", c_ve, cast_done)
                b.wait("se", c_te, sum_done)
                b.op("se", lambda h, lnc3=lnc3: h.activation(
                    lnc3, lnc3, AF.Square), [(c_se, 1)])
                sq_done = c_se.n
                b.wait("te", c_se, sq_done)
                for c in range(DC):
                    b.op("te", lambda h, c=c: h.matmul(
                        PA[0:1, 512:1024], ONB[:], LNC[:, c * 512:(c + 1) * 512],
                        start=(c == 0), stop=(c == DC - 1)),
                        [(c_te, 1)] if c == DC - 1 else ())
                sumsq_done = c_te.n
                prev_sumsq = sumsq_done
                b.wait("ve", c_te, sum_done)
                b.op("ve", lambda h: h.tensor_scalar(
                    SMa[:], PA[0:1, 0:512], 1.0 / D, None, AluOpType.mult), [(c_ve, 1)])
                mean_done = c_ve.n
                b.wait("ve", c_te, sumsq_done)
                b.op("ve", lambda h: h.tensor_scalar(
                    SMb[:], PA[0:1, 512:1024], 1.0 / D, None, AluOpType.mult), [(c_ve, 1)])
                b.wait("se", c_ve, mean_done)
                b.op("se", lambda h: h.activation(SMc[:], SMa[:], AF.Square), [(c_se, 1)])
                b.wait("ve", c_se)
                b.wait("ve", c_ve)
                b.op("ve", lambda h: h.tensor_tensor(
                    SMb[:], SMb[:], SMc[:], AluOpType.subtract), [(c_ve, 1)])
                b.wait("se", c_ve)
                b.op("se", lambda h: h.activation(
                    SMb[:], SMb[:], AF.Sqrt, bias=EPS[0:1, :]), [(c_se, 1)])
                b.wait("ve", c_se)
                b.op("ve", lambda h: h.reciprocal(SMb[:], SMb[:]), [(c_ve, 1)])
                rstd_done = c_ve.n
                b.wait("ve", c_ve, rstd_done)
                b.op("ve", lambda h: h.tensor_tensor(
                    SMc[:], SMa[:], SMb[:], AluOpType.mult), [(c_ve, 1)])
                brow_done = c_ve.n
                b.wait("te", c_ve, brow_done)
                b.op("te", lambda h: h.matmul(PB[:, 0:512], ONFR[:], SMb[:], start=True, stop=True),
                     [(c_te, 1)])
                b.op("te", lambda h: h.matmul(PB[:, 512:1024], ONFR[:], SMc[:], start=True, stop=True),
                     [(c_te, 1)])
                rep_done = c_te.n
                b.wait("ve", c_te, rep_done)
                for c in range(DC):
                    b.op("ve", lambda h, c=c, t0=t0: h.tensor_tensor(
                        LNC[:, c * 512:(c + 1) * 512],
                        XSB[:, c * T + t0: c * T + t0 + 512],
                        PB[:, 0:512], AluOpType.mult), [(c_ve, 1)])
                mul_done = c_ve.n
                b.wait("ve", c_ve, mul_done)
                for c in range(DC):
                    b.op("ve", lambda h, c=c, t0=t0: h.tensor_tensor(
                        xn_sl(c, t0, 512), LNC[:, c * 512:(c + 1) * 512],
                        PB[:, 512:1024], AluOpType.subtract), [(c_ve, 1)])
                prev_affine = c_ve.n
            return c_ve.n

        def emit_l0_ffn(tag):
            ar = Arena()
            KEY = ar.alloc(16 * 1024)
            VAL = ar.alloc(16 * 1024)
            EST = ar.alloc(128)
            ESN = ar.alloc(1024)
            SC = ar.alloc(16 * 512)
            SBC0 = ar.alloc(512)
            SBC1 = ar.alloc(512)
            SELF = ar.alloc(2048, F32)

            entry_te, entry_ve, entry_se = c_te.n, c_ve.n, c_se.n
            xn_done = emit_layernorm()

            for _e in ("ge", "sy"):
                b.wait(_e, c_te, entry_te)
                b.wait(_e, c_ve, entry_ve)
                b.wait(_e, c_se, entry_se)
            b.dma_barrier()
            keys_r = w["l0_keys"].rearrange("e (c p) hh -> e p c hh", p=128)
            vals_r = w["l0_vals"]
            for e in range(16):
                b.op("ge", lambda h, e=e: h.dma_start(
                    KEY(0, 128, e * 1024, (e + 1) * 1024).rearrange("p (c hh) -> p c hh", c=DC),
                    keys_r[e]), [(c_gw, 16)])
                b.op("ge", lambda h, e=e: h.dma_start(
                    VAL(0, 128, e * 1024, (e + 1) * 1024), vals_r[e]), [(c_gw, 16)])
            b.op("ge", lambda h: h.dma_start(
                ESN(0, 16, 0, 1024), w["l0_sel"]), [(c_gw, 16)])
            wload = c_gw.n

            b.wait("te", c_gw, wload)
            pa_b = PA[:].bitcast(BF16)
            for c in range(DC):
                b.op("te", lambda h, c=c: h.transpose(
                    pa_b[:, c * 16:(c + 1) * 16],
                    ESN(0, 16, c * 128, (c + 1) * 128), IDB[0:16, 0:16]),
                    [(c_te, 1)] if c == DC - 1 else ())
            est_tr = c_te.n
            b.wait("ve", c_te, est_tr)
            b.op("ve", lambda h: h.tensor_copy(EST(0, 128, 0, 128), pa_b[:, 0:128]),
                 [(c_ve, 1)])
            est_done = c_ve.n

            b.wait("te", c_ve, xn_done)
            b.wait("te", c_ve, est_done)
            sig_mark = {0: 0, 1: 0}
            for t in range(NT):
                t0 = t * 512
                if sig_mark[t % 2]:
                    b.wait("te", c_se, sig_mark[t % 2])
                for c in range(DC):
                    b.op("te", lambda h, c=c, t0=t0, t=t: h.matmul(
                        PB[0:16, (t % 2) * 512:(t % 2) * 512 + 512],
                        EST(0, 128, c * 16, (c + 1) * 16),
                        xn_sl(c, t0, 512),
                        start=(c == 0), stop=(c == DC - 1)),
                        [(c_te, 1)] if c == DC - 1 else ())
                lg = c_te.n
                b.wait("se", c_te, lg)
                b.op("se", lambda h, t=t, t0=t0: h.activation(
                    SELF(0, 16, t0, t0 + 512), PB[0:16, (t % 2) * 512:(t % 2) * 512 + 512],
                    AF.Sigmoid), [(c_se, 1)])
                sig_mark[t % 2] = c_se.n
            sel_sig = c_se.n
            b.wait("ve", c_se, sel_sig)
            b.op("ve", lambda h: h.tensor_scalar(
                SELF(0, 16, 0, T), SELF(0, 16, 0, T), 0.5, None, AluOpType.mult),
                [(c_ve, 1)])
            sel_done = c_ve.n
            b.wait("ge", c_ve, sel_done)
            b.op("ge", lambda h: h.dma_start(scr[0:16, :], SELF(0, 16, 0, T)), [(c_scw, 16)])
            b.wait("ge", c_scw)

            se_relu_mark = {0: 0, 1: 0}
            sbc_lastread = {0: 0, 1: 0}
            bc_target = {}
            vdone_mark = {}
            val_te_mark = 0
            for t in range(NT):
                t0 = t * 512
                for e in range(16):
                    idx = t * 16 + e
                    half = e % 2
                    bc = (c_bc0, c_bc1)[idx % 2]
                    sbc = (SBC0, SBC1)[idx % 2]
                    if sbc_lastread[idx % 2]:
                        b.wait("ge", c_ve, sbc_lastread[idx % 2])
                    b.op("ge", lambda h, e=e, t0=t0, sbc=sbc: h.dma_start(
                        sbc(0, 128, 0, 512),
                        scr[e:e + 1, t0:t0 + 512].broadcast_to((128, 512))), [(bc, 16)])
                    bc_target[idx] = bc.n
                    if se_relu_mark[half]:
                        b.wait("te", c_se, se_relu_mark[half])
                    if e == 0 and val_te_mark:
                        b.wait("se", c_te, val_te_mark)
                    for c in range(DC):
                        b.op("te", lambda h, c=c, e=e, t0=t0, half=half: h.matmul(
                            PA[:, half * 512: half * 512 + 512],
                            KEY(0, 128, e * 1024 + c * 128, e * 1024 + (c + 1) * 128),
                            xn_sl(c, t0, 512),
                            start=(c == 0), stop=(c == DC - 1)),
                            [(c_te, 1)] if c == DC - 1 else ())
                    key_done = c_te.n
                    b.wait("se", c_te, key_done)
                    b.op("se", lambda h, e=e, half=half: h.activation(
                        SC(0, 128, e * 512, (e + 1) * 512),
                        PA[:, half * 512: half * 512 + 512], AF.Relu), [(c_se, 1)])
                    se_relu_mark[half] = c_se.n
                    b.wait("ve", bc, bc_target[idx])
                    b.wait("ve", c_se, se_relu_mark[half])
                    b.op("ve", lambda h, e=e, sbc=sbc: h.tensor_tensor(
                        SC(0, 128, e * 512, (e + 1) * 512),
                        SC(0, 128, e * 512, (e + 1) * 512),
                        sbc(0, 128, 0, 512), AluOpType.mult), [(c_ve, 1)])
                    sbc_lastread[idx % 2] = c_ve.n
                sc_ready = c_ve.n
                b.wait("te", c_ve, sc_ready)
                for v in range(DC):
                    vidx = t * DC + v
                    pv = (P4, P5, P6)[vidx % 3]
                    if vidx >= 3:
                        b.wait("te", c_ve, vdone_mark[vidx - 3])
                    for e in range(16):
                        b.op("te", lambda h, pv=pv, v=v, e=e: h.matmul(
                            pv[:, 0:512],
                            VAL(0, 128, e * 1024 + v * 128, e * 1024 + (v + 1) * 128),
                            SC(0, 128, e * 512, (e + 1) * 512),
                            start=(e == 0), stop=(e == 15)),
                            [(c_te, 1)] if e == 15 else ())
                    val_done = c_te.n
                    b.wait("ve", c_te, val_done)
                    b.op("ve", lambda h, pv=pv, v=v, t0=t0: h.tensor_tensor(
                        xsb_sl(v, t0, 512), xsb_sl(v, t0, 512), pv[:, 0:512],
                        AluOpType.add), [(c_ve, 1)])
                    vdone_mark[vidx] = c_ve.n
                val_te_mark = c_te.n
            return c_ve.n

        def emit_l0_attn():
            ar = Arena()
            WQT = ar.alloc(8192)
            WKT = ar.alloc(8192)
            WVT = ar.alloc(8192)
            WOT = ar.alloc(8192)
            WSTG = ar.alloc(1024)
            QKTM = ar.alloc(1024)
            QTBe = [ar.alloc(1024), ar.alloc(1024)]
            QTBo = [ar.alloc(1024), ar.alloc(1024)]
            KTBe = [ar.alloc(1024), ar.alloc(1024)]
            KTBo = [ar.alloc(1024), ar.alloc(1024)]
            VTM = ar.alloc(1040)
            PT0 = ar.alloc(1024)
            PT1 = ar.alloc(1024)
            AOTh = ar.alloc(1024)
            RBC = ar.alloc(1024)
            SMR = WSTG

            P6b = P6[:].bitcast(BF16)
            P7b = P7[:].bitcast(BF16)

            entry_te, entry_ve, entry_se = c_te.n, c_ve.n, c_se.n
            xn_done = emit_layernorm()
            for _e in ("ge", "sy"):
                b.wait(_e, c_te, entry_te)
                b.wait(_e, c_ve, entry_ve)
                b.wait(_e, c_se, entry_se)
            b.dma_barrier()

            wsrc = [w["l0_wq"], w["l0_wk"], w["l0_wv"], w["l0_wo"]]
            wdst = [WQT, WKT, WVT, WOT]
            c_ws = [b.ch("c_ws0", dma=True), b.ch("c_ws1", dma=True)]
            stg_te_mark = {}
            evac_ve_mark = {}
            b.wait("te", c_ve, entry_ve)
            b.wait("te", c_se, entry_se)
            for idx in range(32):
                wi, oc = idx // 8, idx % 8
                slot = idx % 2
                if idx >= 1:
                    b.wait("ge", c_te, stg_te_mark[idx - 1])
                b.op("ge", lambda h, wi=wi, oc=oc: h.dma_start(
                    WSTG(0, 128, 0, 1024),
                    wsrc[wi][oc * 128:(oc + 1) * 128, :]), [(c_ws[slot], 16)])
                b.wait("te", c_ws[slot], 16 * (idx // 2 + 1))
                if idx >= 2:
                    b.wait("te", c_ve, evac_ve_mark[idx - 2])
                pb_t = (P6b, P7b)[slot]
                for cc in range(DC):
                    b.op("te", lambda h, cc=cc, pb_t=pb_t: h.transpose(
                        pb_t[:, cc * 128:(cc + 1) * 128],
                        WSTG(0, 128, cc * 128, (cc + 1) * 128),
                        IDB[:]), [(c_te, 1)] if cc == DC - 1 else ())
                stg_te_mark[idx] = c_te.n
                b.wait("ve", c_te, stg_te_mark[idx])
                dst = wdst[wi]
                b.op("ve", lambda h, dst=dst, oc=oc, pb_t=pb_t: h.tensor_copy(
                    dst(0, 128, 0, 8192).rearrange("p (cc o) -> p cc o", cc=DC)[
                        :, :, oc * 128:(oc + 1) * 128],
                    pb_t.rearrange("p (cc s) -> p cc s", cc=DC)), [(c_ve, 1)])
                evac_ve_mark[idx] = c_ve.n
            wprep_ve = c_ve.n

            b.op("ge", lambda h: h.memset(
                VTM(0, 128, 0, 1040).rearrange("p (g dd) -> p g dd", dd=65)[:, :, 64:65],
                1.0), [(c_ge, 1)])
            vones = c_ge.n
            b.wait("te", c_ge, vones)
            b.wait("te", c_ve, wprep_ve)
            b.wait("te", c_ve, xn_done)

            marks = {
                "exp2": {0: 0, 1: 0},
                "pt_av": {0: 0, 1: 0},
                "qtm_ev": {}, "v_ev": {},
                "qtb_ev": {}, "ktb_ev": {},
                "rbc_ve": 0,
                "aoth_ev": {},
                "oproj_te": {}, "oproj_ve": {},
                "projlast_ve": 0,
                "smr": 0,
                "p67_ve": 0,
            }

            def emit_proj(h16, which):
                hcol = 128 * h16
                wt = {"q": WQT, "k": WKT, "v": WVT}[which]
                b.wait("te", c_ve, marks["p67_ve"])
                for half in range(2):
                    pvt = (P6, P7)[half]
                    for cc in range(DC):
                        b.op("te", lambda h, cc=cc, half=half, hcol=hcol, wt=wt, pvt=pvt: h.matmul(
                            pvt[:, 0:512],
                            xn_sl(cc, hcol, 128),
                            wt(0, 128, cc * 1024 + half * 512, cc * 1024 + (half + 1) * 512),
                            start=(cc == 0), stop=(cc == DC - 1)),
                            [(c_te, 1)] if cc == DC - 1 else ())
                p_done = c_te.n
                b.wait("ve", c_te, p_done)
                if which == "v":
                    b.wait("ve", c_ge, vones)
                    for half in range(2):
                        pvt = (P6, P7)[half]
                        b.op("ve", lambda h, half=half, pvt=pvt: h.tensor_copy(
                            VTM(0, 128, half * 520, half * 520 + 520).rearrange(
                                "p (g dd) -> p g dd", dd=65)[:, :, 0:64],
                            pvt[:, 0:512].rearrange(
                                "p (g dd) -> p g dd", dd=64)), [(c_ve, 1)])
                    marks["v_ev"][h16] = c_ve.n
                else:
                    if marks["qtm_ev"].get("last_tr"):
                        b.wait("ve", c_te, marks["qtm_ev"]["last_tr"])
                    b.op("ve", lambda h: h.tensor_copy(
                        QKTM(0, 128, 0, 512), P6[:, 0:512]), [(c_ve, 1)])
                    b.op("ve", lambda h: h.tensor_copy(
                        QKTM(0, 128, 512, 1024), P7[:, 0:512]), [(c_ve, 1)])
                    marks["qtm_ev"][(h16, which)] = c_ve.n
                marks["p67_ve"] = c_ve.n

            def emit_tr(h16, which):
                par = h16 % 2
                dste = (QTBe, KTBe)[which == "k"][par]
                dsto = (QTBo, KTBo)[which == "k"][par]
                b.wait("te", c_ve, marks["qtm_ev"][(h16, which)])
                b.wait("te", c_ve, marks["p67_ve"])
                for bb in range(DC):
                    b.op("te", lambda h, bb=bb: h.transpose(
                        P6b[:, bb * 128:(bb + 1) * 128],
                        QKTM(0, 128, bb * 128, (bb + 1) * 128), IDB[:]),
                        [(c_te, 1)] if bb == DC - 1 else ())
                marks["qtm_ev"]["last_tr"] = c_te.n
                b.wait("ve", c_te, marks["qtm_ev"]["last_tr"])
                b.op("ve", lambda h, dste=dste: h.tensor_copy(
                    dste(0, 64, 0, 1024), P6b[0:64, :]), [(c_ve, 1)])
                b.op("ve", lambda h, dsto=dsto: h.tensor_copy(
                    dsto(0, 64, 0, 1024), P6b[64:128, :]), [(c_ve, 1)])
                marks[("qtb_ev", "ktb_ev")[which == "k"]][h16] = c_ve.n
                marks["p67_ve"] = c_ve.n

            def emit_round(h16, pi, inserts=()):
                par = h16 % 2
                QTB = (QTBe, QTBo)[pi][par]
                KTBe_, KTBo_ = KTBe[par], KTBo[par]
                ins = dict(inserts)
                b.wait("te", c_ve, marks["qtb_ev"][h16])
                b.wait("te", c_ve, marks["ktb_ev"][h16])
                b.wait("te", c_ve, marks["v_ev"][h16])
                if pi == 1:
                    b.wait("te", c_ve, marks["aoth_ev"][(h16, 0)])
                elif marks["oproj_ve"].get(h16 - 1) is not None:
                    b.wait("te", c_ve, marks["oproj_ve"][h16 - 1])
                for jg in range(16):
                    if jg in ins:
                        ins[jg]()
                    KTB = (KTBe_, KTBo_)[jg % 2]
                    jb = jg // 2
                    sc = (PA, PB)[jg % 2]
                    if marks["exp2"][jg % 2]:
                        b.wait("te", c_se, marks["exp2"][jg % 2])
                    for ih in range(2):
                        b.op("te", lambda h, KTB=KTB, jb=jb, QTB=QTB, ih=ih, sc=sc: h.matmul(
                            sc[:, ih * 512:(ih + 1) * 512],
                            KTB(0, 64, jb * 128, (jb + 1) * 128),
                            QTB(0, 64, ih * 512, (ih + 1) * 512),
                            start=True, stop=True),
                            [(c_te, 1)] if ih == 1 else ())
                    qk_done = c_te.n
                    pt = (PT0, PT1)[jg % 2]
                    b.wait("se", c_te, qk_done)
                    if marks["pt_av"][jg % 2]:
                        b.wait("se", c_te, marks["pt_av"][jg % 2])
                    b.op("se", lambda h, pt=pt, sc=sc: h.activation(
                        pt(0, 128, 0, 1024), sc[:], AF.Exp,
                        bias=NEG8[:], scale=0.125), [(c_se, 1)])
                    marks["exp2"][jg % 2] = c_se.n
                    b.wait("te", c_se, marks["exp2"][jg % 2])
                    for ih in range(2):
                        pvt = (P4, P5)[ih]
                        b.op("te", lambda h, pvt=pvt, jg=jg, pt=pt, ih=ih: h.matmul(
                            pvt[0:65, 0:512],
                            VTM(0, 128, jg * 65, (jg + 1) * 65),
                            pt(0, 128, ih * 512, (ih + 1) * 512),
                            start=(jg == 0), stop=(jg == 15)),
                            [(c_te, 1)] if ih == 1 else ())
                    marks["pt_av"][jg % 2] = c_te.n
                av_done = c_te.n
                b.wait("ve", c_te, av_done)
                b.wait("ve", c_ve, marks["smr"])
                b.op("ve", lambda h: h.reciprocal(SMa[:], P4[64:65, 0:512]), [(c_ve, 1)])
                b.op("ve", lambda h: h.reciprocal(SMb[:], P5[64:65, 0:512]), [(c_ve, 1)])
                b.wait("ve", c_ve)
                b.op("ve", lambda h: h.tensor_copy(SMR(0, 1, 0, 512), SMa[:]), [(c_ve, 1)])
                b.op("ve", lambda h: h.tensor_copy(SMR(0, 1, 512, 1024), SMb[:]), [(c_ve, 1)])
                rec_done = c_ve.n
                marks["smr"] = rec_done
                b.wait("te", c_ve, rec_done)
                b.wait("te", c_ve, marks["p67_ve"])
                b.op("te", lambda h: h.matmul(P6[0:64, :], ONBR[0:1, 0:64], SMR(0, 1, 0, 512),
                                              start=True, stop=True), [(c_te, 1)])
                b.op("te", lambda h: h.matmul(P7[0:64, :], ONBR[0:1, 0:64], SMR(0, 1, 512, 1024),
                                              start=True, stop=True), [(c_te, 1)])
                rep_done = c_te.n
                b.wait("ve", c_te, rep_done)
                b.op("ve", lambda h: h.tensor_copy(RBC(0, 64, 0, 512), P6[0:64, :]), [(c_ve, 1)])
                b.op("ve", lambda h: h.tensor_copy(RBC(0, 64, 512, 1024), P7[0:64, :]), [(c_ve, 1)])
                marks["rbc_ve"] = c_ve.n
                marks["p67_ve"] = c_ve.n
                b.wait("ve", c_ve, marks["rbc_ve"])
                if marks["oproj_te"].get(h16 - 1) is not None and pi == 0:
                    b.wait("ve", c_te, marks["oproj_te"][h16 - 1])
                for ih in range(2):
                    pvt = (P4, P5)[ih]
                    b.op("ve", lambda h, pvt=pvt, pi=pi, ih=ih: h.tensor_tensor(
                        AOTh(64 * pi, 64 * pi + 64, ih * 512, (ih + 1) * 512),
                        pvt[0:64, 0:512],
                        RBC(0, 64, ih * 512, (ih + 1) * 512),
                        AluOpType.mult), [(c_ve, 1)])
                marks["aoth_ev"][(h16, pi)] = c_ve.n

            def emit_oproj(h16):
                hcol = 128 * h16
                opev = {}
                b.wait("te", c_ve, marks["aoth_ev"][(h16, 1)])
                for ccp in range(DC):
                    pvt = (P4, P5)[ccp % 2]
                    if ccp >= 2:
                        b.wait("te", c_ve, opev[ccp - 2])
                    for oc in range(DC):
                        b.op("te", lambda h, ccp=ccp, oc=oc, pvt=pvt: h.matmul(
                            pvt[:, 0:128],
                            WOT(0, 128, oc * 1024 + ccp * 128, oc * 1024 + (ccp + 1) * 128),
                            AOTh(0, 128, oc * 128, (oc + 1) * 128),
                            start=(oc == 0), stop=(oc == DC - 1)),
                            [(c_te, 1)] if oc == DC - 1 else ())
                    op_done = c_te.n
                    b.wait("ve", c_te, op_done)
                    b.op("ve", lambda h, ccp=ccp, hcol=hcol, pvt=pvt: h.tensor_tensor(
                        xsb_sl(ccp, hcol, 128), xsb_sl(ccp, hcol, 128),
                        pvt[:, 0:128], AluOpType.add), [(c_ve, 1)])
                    opev[ccp] = c_ve.n
                marks["oproj_te"][h16] = c_te.n
                marks["oproj_ve"][h16] = c_ve.n

            emit_proj(0, "q")
            emit_tr(0, "q")
            emit_proj(0, "k")
            emit_tr(0, "k")
            emit_proj(0, "v")
            for h16 in range(16):
                if h16 < 15:
                    emit_round(h16, 0, inserts=[
                        (4, lambda: emit_proj(h16 + 1, "q")),
                        (10, lambda: emit_tr(h16 + 1, "q"))])
                    emit_round(h16, 1, inserts=[
                        (3, lambda: emit_proj(h16 + 1, "k")),
                        (8, lambda: emit_tr(h16 + 1, "k"))])
                    emit_proj(h16 + 1, "v")
                else:
                    emit_round(h16, 0)
                    emit_round(h16, 1)
                emit_oproj(h16)
            return c_ve.n

        def emit_l1_moe(tag, scrow):
            ar = Arena()
            CBC0 = ar.alloc(2048)
            CBC1 = ar.alloc(2048)
            RWT = ar.alloc(128)
            RWN = ar.alloc(1024)
            LGT = ar.alloc(2048, F32)
            LTM = ar.alloc(256, F32)
            CTM = ar.alloc(256, F32)
            CBT = ar.alloc(2048, F32)
            RB16 = ar.alloc(1, F32)
            SCL = ar.alloc(8, F32)
            HS = ar.alloc(1024)
            HID0 = ar.alloc(1024)
            HID1 = ar.alloc(1024)
            SHHC = ar.alloc(2048)
            ovl = ar.top
            SHGT = ar.alloc(8192)
            SHUT = ar.alloc(8192)
            SHDT = ar.alloc(8192)
            WSTG2 = ar.alloc(2048)
            ar.reset(ovl)
            ESL0 = ar.alloc(12288)
            ESL1 = ar.alloc(12288)

            P6b = P6[:].bitcast(BF16)
            P7b = P7[:].bitcast(BF16)
            m1c = SCL(0, 128, 0, 1)
            m2c = SCL(0, 128, 1, 2)
            nm1c = SCL(0, 128, 2, 3)
            recc = SCL(0, 128, 3, 4)

            entry_te, entry_ve, entry_se = c_te.n, c_ve.n, c_se.n
            xn_done = emit_layernorm()
            b.wait("ge", c_te, entry_te)
            b.wait("ge", c_ve, entry_ve)
            b.wait("ge", c_se, entry_se)

            c_rt = b.ch("c_rt_" + tag, dma=True)
            b.op("ge", lambda h: h.dma_start(RWN(0, 16, 0, 1024), w["l1_rw"]), [(c_rt, 16)])
            b.op("ge", lambda h: h.dma_start(
                RB16(0, 16, 0, 1), w["l1_rb"].rearrange("(e o) -> e o", o=1)), [(c_rt, 16)])

            c_sh = [b.ch("c_sh0_" + tag, dma=True), b.ch("c_sh1_" + tag, dma=True)]
            stg_te = {}
            evac_ve = {}
            shsrc = [w["l1_shg"], w["l1_shu"], w["l1_shd"]]
            shdst = [SHGT, SHUT, SHDT]
            b.wait("te", c_ve, entry_ve)
            b.wait("te", c_se, entry_se)
            for idx in range(24):
                wi, oc = idx // 8, idx % 8
                slot = idx % 2
                if idx >= 2:
                    b.wait("ge", c_te, stg_te[idx - 2])
                b.op("ge", lambda h, wi=wi, oc=oc, slot=slot: h.dma_start(
                    WSTG2(0, 128, slot * 1024, slot * 1024 + 1024),
                    shsrc[wi][oc * 128:(oc + 1) * 128, :]), [(c_sh[slot], 16)])
                b.wait("te", c_sh[slot], 16 * (idx // 2 + 1))
                if idx >= 2:
                    b.wait("te", c_ve, evac_ve[idx - 2])
                pb_t = (P6b, P7b)[slot]
                for cc in range(DC):
                    b.op("te", lambda h, cc=cc, slot=slot, pb_t=pb_t: h.transpose(
                        pb_t[:, cc * 128:(cc + 1) * 128],
                        WSTG2(0, 128, slot * 1024 + cc * 128, slot * 1024 + (cc + 1) * 128),
                        IDB[:]), [(c_te, 1)] if cc == DC - 1 else ())
                stg_te[idx] = c_te.n
                b.wait("ve", c_te, stg_te[idx])
                b.op("ve", lambda h, wi=wi, oc=oc, pb_t=pb_t: h.tensor_copy(
                    shdst[wi](0, 128, 0, 8192).rearrange("p (cc o) -> p cc o", cc=DC)[
                        :, :, oc * 128:(oc + 1) * 128],
                    pb_t.rearrange("p (cc s) -> p cc s", cc=DC)), [(c_ve, 1)])
                evac_ve[idx] = c_ve.n
            shw_ve = c_ve.n

            b.wait("te", c_rt)
            b.wait("te", c_ve, xn_done)
            b.wait("te", c_ve, shw_ve)
            for cc in range(DC):
                b.op("te", lambda h, cc=cc: h.transpose(
                    P6b[:, cc * 16:(cc + 1) * 16],
                    RWN(0, 16, cc * 128, (cc + 1) * 128), IDB[0:16, 0:16]),
                    [(c_te, 1)] if cc == DC - 1 else ())
            rwt_tr = c_te.n
            b.wait("ve", c_te, rwt_tr)
            b.op("ve", lambda h: h.tensor_copy(RWT(0, 128, 0, 128), P6b[:, 0:128]), [(c_ve, 1)])
            rwt_done = c_ve.n
            b.wait("te", c_ve, rwt_done)
            lg_se = {0: 0, 1: 0}
            for t in range(NT):
                t0 = t * 512
                if lg_se[t % 2]:
                    b.wait("te", c_ve, lg_se[t % 2])
                for cc in range(DC):
                    b.op("te", lambda h, cc=cc, t0=t0, t=t: h.matmul(
                        PB[0:16, (t % 2) * 512:(t % 2) * 512 + 512],
                        RWT(0, 128, cc * 16, (cc + 1) * 16),
                        xn_sl(cc, t0, 512),
                        start=(cc == 0), stop=(cc == DC - 1)),
                        [(c_te, 1)] if cc == DC - 1 else ())
                lgm = c_te.n
                b.wait("ve", c_te, lgm)
                b.wait("ve", c_rt)
                b.op("ve", lambda h, t=t, t0=t0: h.tensor_scalar(
                    LGT(0, 16, t0, t0 + 512), PB[0:16, (t % 2) * 512:(t % 2) * 512 + 512],
                    RB16(0, 16, 0, 1), None, AluOpType.add), [(c_ve, 1)])
                lg_se[t % 2] = c_ve.n
            lgt_done = c_ve.n
            b.wait("te", c_ve, lgt_done)
            for tt in range(16):
                pvt = (P4, P5)[tt // 8]
                b.op("te", lambda h, tt=tt, pvt=pvt: h.transpose(
                    pvt[:, (tt % 8) * 16:(tt % 8) * 16 + 16],
                    LGT(0, 16, tt * 128, (tt + 1) * 128), IDF[0:16, 0:16]),
                    [(c_te, 1)] if tt in (7, 15) else ())
            ltm_tr = c_te.n
            b.wait("ve", c_te, ltm_tr)
            b.op("ve", lambda h: h.tensor_copy(LTM(0, 128, 0, 128), P4[:, 0:128]), [(c_ve, 1)])
            b.op("ve", lambda h: h.tensor_copy(LTM(0, 128, 128, 256), P5[:, 0:128]), [(c_ve, 1)])
            ltm_done = c_ve.n
            b.wait("ve", c_ve, ltm_done)
            for tt in range(16):
                lg = LTM(0, 128, tt * 16, (tt + 1) * 16)
                cm = CTM(0, 128, tt * 16, (tt + 1) * 16)
                b.op("ve", lambda h, lg=lg: h.reduce_max(m1c, lg, axis=AX.X), [(c_ve, 1)])
                b.wait("ve", c_ve)
                b.op("ve", lambda h, lg=lg, cm=cm: h.tensor_scalar(
                    cm, lg, m1c, -1e30, AluOpType.is_ge, AluOpType.mult), [(c_ve, 1)])
                b.wait("ve", c_ve)
                b.op("ve", lambda h, lg=lg, cm=cm: h.tensor_tensor(
                    cm, cm, lg, AluOpType.add), [(c_ve, 1)])
                b.wait("ve", c_ve)
                b.op("ve", lambda h, cm=cm: h.reduce_max(m2c, cm, axis=AX.X), [(c_ve, 1)])
                b.op("ve", lambda h: h.tensor_scalar(
                    nm1c, m1c, -1.0, None, AluOpType.mult), [(c_ve, 1)])
                nm1_done = c_ve.n
                b.wait("se", c_ve, nm1_done)
                b.op("se", lambda h, lg=lg, cm=cm: h.activation(
                    cm, lg, AF.Exp, bias=nm1c), [(c_se, 1)])
                eb_done = c_se.n
                b.wait("ve", c_se, eb_done)
                b.op("ve", lambda h, lg=lg: h.tensor_scalar(
                    lg, lg, m2c, None, AluOpType.is_ge), [(c_ve, 1)])
                b.wait("ve", c_ve)
                b.op("ve", lambda h, lg=lg, cm=cm: h.tensor_tensor(
                    cm, cm, lg, AluOpType.mult), [(c_ve, 1)])
                b.wait("ve", c_ve)
                b.op("ve", lambda h, cm=cm: h.reduce_sum(recc, cm, axis=AX.X), [(c_ve, 1)])
                b.wait("ve", c_ve)
                b.op("ve", lambda h: h.reciprocal(recc, recc), [(c_ve, 1)])
                b.wait("ve", c_ve)
                b.op("ve", lambda h, cm=cm: h.tensor_scalar(
                    cm, cm, recc, 0.5, AluOpType.mult, AluOpType.mult), [(c_ve, 1)])
                b.wait("ve", c_ve)
            ctm_done = c_ve.n
            b.wait("te", c_ve, ctm_done)
            cb_ev = {}
            for tt in range(16):
                pvt = (P4, P5)[tt % 2]
                if tt >= 2:
                    b.wait("te", c_ve, cb_ev[tt - 2])
                b.op("te", lambda h, tt=tt, pvt=pvt: h.transpose(
                    pvt[0:16, 0:128],
                    CTM(0, 128, tt * 16, (tt + 1) * 16), IDF[:]), [(c_te, 1)])
                ctr = c_te.n
                b.wait("ve", c_te, ctr)
                b.op("ve", lambda h, tt=tt, pvt=pvt: h.tensor_copy(
                    CBT(0, 16, tt * 128, (tt + 1) * 128), pvt[0:16, 0:128]), [(c_ve, 1)])
                cb_ev[tt] = c_ve.n
            cbt_done = c_ve.n
            b.wait("ge", c_ve, cbt_done)
            c_scw2 = b.ch("c_scw_" + tag, dma=True)
            b.op("ge", lambda h: h.dma_start(
                scr[scrow:scrow + 16, :], CBT(0, 16, 0, T)), [(c_scw2, 16)])
            b.wait("ge", c_scw2)

            b.wait("te", c_ve, shw_ve)
            b.wait("te", c_ve, lgt_done)
            sh_se = 0
            sh_hid_ve = 0
            sh_x_ve = {}
            sh_dn_te = {}
            for t in range(8):
                t0 = t * 256
                for grp in range(2):
                    if sh_se:
                        b.wait("te", c_se, sh_se)
                        b.wait("te", c_ve, sh_hid_ve)
                    for scn4 in range(4):
                        scn = grp * 4 + scn4
                        for cc in range(DC):
                            b.op("te", lambda h, scn=scn, scn4=scn4, cc=cc, t0=t0: h.matmul(
                                PA[:, scn4 * 256:(scn4 + 1) * 256],
                                SHGT(0, 128, cc * 1024 + scn * 128, cc * 1024 + (scn + 1) * 128),
                                xn_sl(cc, t0, 256),
                                start=(cc == 0), stop=(cc == DC - 1)),
                                [(c_te, 1)] if cc == DC - 1 else ())
                    for scn4 in range(4):
                        scn = grp * 4 + scn4
                        for cc in range(DC):
                            b.op("te", lambda h, scn=scn, scn4=scn4, cc=cc, t0=t0: h.matmul(
                                PB[:, scn4 * 256:(scn4 + 1) * 256],
                                SHUT(0, 128, cc * 1024 + scn * 128, cc * 1024 + (scn + 1) * 128),
                                xn_sl(cc, t0, 256),
                                start=(cc == 0), stop=(cc == DC - 1)),
                                [(c_te, 1)] if cc == DC - 1 else ())
                    gu = c_te.n
                    b.wait("se", c_te, gu)
                    if sh_hid_ve:
                        b.wait("se", c_ve, sh_hid_ve)
                    b.op("se", lambda h: h.activation(
                        HS(0, 128, 0, 1024), PA[:], AF.Silu), [(c_se, 1)])
                    sh_se = c_se.n
                    b.wait("ve", c_se, sh_se)
                    b.op("ve", lambda h: h.tensor_tensor(
                        HS(0, 128, 0, 1024), HS(0, 128, 0, 1024), PB[:],
                        AluOpType.mult), [(c_ve, 1)])
                    b.wait("ve", c_ve)
                    b.op("ve", lambda h, grp=grp: h.tensor_scalar(
                        SHHC(0, 128, grp * 1024, (grp + 1) * 1024),
                        HS(0, 128, 0, 1024), 0.5, None, AluOpType.mult), [(c_ve, 1)])
                    sh_hid_ve = c_ve.n
                b.wait("te", c_ve, sh_hid_ve)
                if t >= 1:
                    b.wait("te", c_ve, sh_x_ve[t - 1])
                for v in range(DC):
                    pvt = (P4, P5, P6, P7)[v // 2]
                    off = (v % 2) * 256
                    for scn in range(DC):
                        b.op("te", lambda h, pvt=pvt, off=off, scn=scn, v=v: h.matmul(
                            pvt[:, off:off + 256],
                            SHDT(0, 128, scn * 1024 + v * 128, scn * 1024 + (v + 1) * 128),
                            SHHC(0, 128, scn * 256, (scn + 1) * 256),
                            start=(scn == 0 and v % 2 == 0),
                            stop=(scn == DC - 1 and v % 2 == 1)),
                            [(c_te, 1)] if (scn == DC - 1 and v == DC - 1) else ())
                sh_dn_te[t] = c_te.n
                b.wait("ve", c_te, sh_dn_te[t])
                for bk in range(4):
                    b.op("ve", lambda h, bk=bk, t0=t0: h.tensor_tensor(
                        XSB[:].rearrange("p (c t) -> p c t", c=DC)[:, bk * 2: bk * 2 + 2, t0:t0 + 256],
                        XSB[:].rearrange("p (c t) -> p c t", c=DC)[:, bk * 2: bk * 2 + 2, t0:t0 + 256],
                        (P4, P5, P6, P7)[bk][:].rearrange("p (u t) -> p u t", u=2),
                        AluOpType.add), [(c_ve, 1)])
                sh_x_ve[t] = c_ve.n
            sh_done_te, sh_done_ve, sh_done_se = c_te.n, c_ve.n, c_se.n

            b.wait("ge", c_te, sh_done_te)
            b.wait("ge", c_ve, sh_done_ve)

            c_ew = [b.ch("c_ew0_" + tag, dma=True), b.ch("c_ew1_" + tag, dma=True)]
            c_cb = [b.ch("c_cb0_" + tag, dma=True), b.ch("c_cb1_" + tag, dma=True)]
            gate_r = w["l1_gate"].rearrange("e (cc p) hh -> e p cc hh", p=128)
            up_r = w["l1_up"].rearrange("e (cc p) hh -> e p cc hh", p=128)
            down_r = w["l1_down"].rearrange("e (hc p) v -> e p hc v", p=128)
            eslot = (ESL0, ESL1)
            b.wait("te", c_se, sh_done_se)
            b.wait("te", c_ve, sh_done_ve)
            b.wait("se", c_ve, sh_done_ve)
            e_te_done = {}
            e_ve_done = {}
            silu_se = {}
            hid_ve = {}
            dn_te = {}
            xadd_ve = {}
            def emit_gu(g):
                e, t = g // 8, g % 8
                ES = eslot[e % 2]
                t0 = t * 256
                if g >= 1:
                    b.wait("te", c_se, silu_se[g - 1])
                    b.wait("te", c_ve, hid_ve[g - 1])
                for hc in range(4):
                    for cc in range(DC):
                        b.op("te", lambda h, hc=hc, cc=cc, t0=t0, ES=ES: h.matmul(
                            PA[:, hc * 256:(hc + 1) * 256],
                            ES(0, 128, cc * 512 + hc * 128, cc * 512 + (hc + 1) * 128),
                            xn_sl(cc, t0, 256),
                            start=(cc == 0), stop=(cc == DC - 1)),
                            [(c_te, 1)] if cc == DC - 1 else ())
                for hc in range(4):
                    for cc in range(DC):
                        b.op("te", lambda h, hc=hc, cc=cc, t0=t0, ES=ES: h.matmul(
                            PB[:, hc * 256:(hc + 1) * 256],
                            ES(0, 128, 4096 + cc * 512 + hc * 128, 4096 + cc * 512 + (hc + 1) * 128),
                            xn_sl(cc, t0, 256),
                            start=(cc == 0), stop=(cc == DC - 1)),
                            [(c_te, 1)] if cc == DC - 1 else ())
                gu_done = c_te.n
                b.wait("se", c_te, gu_done)
                if g >= 1:
                    b.wait("se", c_ve, hid_ve[g - 1])
                b.op("se", lambda h: h.activation(
                    HS(0, 128, 0, 1024), PA[:], AF.Silu), [(c_se, 1)])
                silu_se[g] = c_se.n
                HD = (HID0, HID1)[g % 2]
                b.wait("ve", c_se, silu_se[g])
                if g >= 2:
                    b.wait("ve", c_te, dn_te[g - 2])
                sl = e % 2
                b.wait("ve", c_cb[sl], cb_target[e])
                b.op("ve", lambda h, HD=HD: h.tensor_tensor(
                    HD(0, 128, 0, 1024), HS(0, 128, 0, 1024), PB[:],
                    AluOpType.mult), [(c_ve, 1)])
                b.wait("ve", c_ve)
                CBCe = (CBC0, CBC1)[sl]
                for hc in range(4):
                    b.op("ve", lambda h, HD=HD, hc=hc, t0=t0, CBCe=CBCe: h.tensor_tensor(
                        HD(0, 128, hc * 256, (hc + 1) * 256),
                        HD(0, 128, hc * 256, (hc + 1) * 256),
                        CBCe(0, 128, t0, t0 + 256), AluOpType.mult), [(c_ve, 1)])
                hid_ve[g] = c_ve.n

            def emit_down(g):
                e, t = g // 8, g % 8
                ES = eslot[e % 2]
                HD = (HID0, HID1)[g % 2]
                t0 = t * 256
                b.wait("te", c_ve, hid_ve[g])
                if g >= 1:
                    b.wait("te", c_ve, xadd_ve[g - 1])
                for v in range(DC):
                    pvt = (P4, P5, P6, P7)[v // 2]
                    off = (v % 2) * 256
                    for hc in range(4):
                        b.op("te", lambda h, pvt=pvt, off=off, hc=hc, v=v, ES=ES, HD=HD: h.matmul(
                            pvt[:, off:off + 256],
                            ES(0, 128, 8192 + hc * 1024 + v * 128, 8192 + hc * 1024 + (v + 1) * 128),
                            HD(0, 128, hc * 256, (hc + 1) * 256),
                            start=(hc == 0 and v % 2 == 0),
                            stop=(hc == 3 and v % 2 == 1)),
                            [(c_te, 1)] if (hc == 3 and v == DC - 1) else ())
                dn_te[g] = c_te.n
                b.wait("ve", c_te, dn_te[g])
                for bk in range(4):
                    b.op("ve", lambda h, bk=bk, t0=t0: h.tensor_tensor(
                        XSB[:].rearrange("p (c t) -> p c t", c=DC)[:, bk * 2: bk * 2 + 2, t0:t0 + 256],
                        XSB[:].rearrange("p (c t) -> p c t", c=DC)[:, bk * 2: bk * 2 + 2, t0:t0 + 256],
                        (P4, P5, P6, P7)[bk][:].rearrange("p (u t) -> p u t", u=2),
                        AluOpType.add), [(c_ve, 1)])
                xadd_ve[g] = c_ve.n

            cb_target = {}
            for e in range(16):
                sl = e % 2
                ES = eslot[sl]
                CBCe = (CBC0, CBC1)[sl]
                if e >= 2:
                    b.wait("ge", c_te, dn_te[(e - 2) * 8 + 7])
                    b.wait("ge", c_ve, xadd_ve[(e - 2) * 8 + 7])
                b.op("ge", lambda h, e=e, ES=ES: h.dma_start(
                    ES(0, 128, 0, 4096).rearrange("p (cc hh) -> p cc hh", cc=DC),
                    gate_r[e]), [(c_ew[sl], 16)])
                b.op("ge", lambda h, e=e, ES=ES: h.dma_start(
                    ES(0, 128, 4096, 8192).rearrange("p (cc hh) -> p cc hh", cc=DC),
                    up_r[e]), [(c_ew[sl], 16)])
                b.op("ge", lambda h, e=e, ES=ES: h.dma_start(
                    ES(0, 128, 8192, 12288).rearrange("p (hc v) -> p hc v", hc=4),
                    down_r[e]), [(c_ew[sl], 16)])
                b.op("ge", lambda h, e=e, CBCe=CBCe, scrow=scrow: h.dma_start(
                    CBCe(0, 128, 0, T),
                    scr[scrow + e:scrow + e + 1, :].broadcast_to((128, T))), [(c_cb[sl], 16)])
                cb_target[e] = c_cb[sl].n
                b.wait("te", c_ew[sl], 16 * 3 * (e // 2 + 1))
                for t in range(8):
                    g = e * 8 + t
                    emit_gu(g)
                    if g >= 1:
                        emit_down(g - 1)
            emit_down(127)
            return c_ve.n

        def emit_l1_conv():
            ar = Arena()
            W1B = ar.alloc(8192)
            W2B = ar.alloc(8192)
            HB = ar.alloc(16384)
            ETN = ar.alloc(1024, F32)
            ETT = ar.alloc(256, F32)
            DGT = ar.alloc(2560)
            MASK = ar.alloc(128)
            IOT = ar.alloc(512)
            BIASR = ar.alloc(1024)
            ONR = ar.alloc(512)

            entry_te, entry_ve, entry_se = c_te.n, c_ve.n, c_se.n
            xn_done = emit_layernorm()
            for _e in ("ge", "sy"):
                b.wait(_e, c_te, entry_te)
                b.wait(_e, c_ve, entry_ve)
                b.wait(_e, c_se, entry_se)
            b.dma_barrier()

            c_cw = b.ch("c_cw", dma=True)
            c_cwh = b.ch("c_cwh", dma=True)
            w1_r = w["l1_W1"].rearrange("(dc p) o -> p dc o", p=128)
            w2_r = w["l1_W2"].rearrange("(dc p) o -> p dc o", p=128)
            b.op("ge", lambda h: h.dma_start(
                W1B(0, 128, 0, 8192).rearrange("p (dc o) -> p dc o", dc=DC), w1_r),
                [(c_cw, 16)])
            b.op("ge", lambda h: h.dma_start(
                W2B(0, 128, 0, 8192).rearrange("p (dc o) -> p dc o", dc=DC), w2_r),
                [(c_cw, 16)])
            b.op("sy", lambda h: h.dma_start(ETN(0, 21, 0, 1024), w["l1_et"]), [(c_cwh, 16)])
            b.op("ge", lambda h: h.dma_start(
                BIASR(0, 1, 0, 1024), w["l1_bias"].rearrange("(o d) -> o d", o=1)),
                [(c_cw, 16)])
            b.op("ge", lambda h: h.memset(ONR(0, 1, 0, 512), 1.0), [(c_ge, 1)])
            pi32 = IOT(0, 128, 0, 256).bitcast(mybir.dt.int32)
            fi32 = IOT(0, 128, 256, 512).bitcast(mybir.dt.int32)
            b.op("ge", lambda h: h.iota(pi32, pattern=[[0, 128]], channel_multiplier=1),
                 [(c_ge, 1)])
            b.op("ge", lambda h: h.iota(fi32, pattern=[[1, 128]], channel_multiplier=0),
                 [(c_ge, 1)])
            init_ge2 = c_ge.n
            b.wait("ve", c_ge, init_ge2)
            b.op("ve", lambda h: h.tensor_tensor(
                MASK(0, 128, 0, 128), pi32, fi32, AluOpType.is_equal), [(c_ve, 1)])
            mask_done = c_ve.n

            b.wait("te", c_cw)
            b.wait("te", c_cwh)
            b.wait("te", c_ve, entry_ve)
            b.wait("te", c_se, entry_se)
            for dc in range(DC):
                b.op("te", lambda h, dc=dc: h.transpose(
                    P6[:, dc * 32: dc * 32 + 21],
                    ETN(0, 21, dc * 128, (dc + 1) * 128), IDF[0:21, 0:21]),
                    [(c_te, 1)] if dc == DC - 1 else ())
            ett_tr = c_te.n
            b.wait("ve", c_te, ett_tr)
            b.op("ve", lambda h: h.tensor_copy(ETT(0, 128, 0, 256), P6[:, 0:256]),
                 [(c_ve, 1)])
            ett_done = c_ve.n

            b.wait("te", c_ve, xn_done)
            hb_ev = {}
            cv_te = {}
            for dc in range(DC):
                b.wait("ve", c_ve, max(ett_done, mask_done))
                if dc >= 1:
                    b.wait("ve", c_te, cv_te[dc - 1])
                for m in range(1, 21):
                    b.op("ve", lambda h, dc=dc, m=m: h.tensor_scalar(
                        DGT(0, 128, (m - 1) * 128, m * 128),
                        MASK(0, 128, 0, 128),
                        ETT(0, 128, dc * 32 + m, dc * 32 + m + 1), None,
                        AluOpType.mult), [(c_ve, 1)])
                dg_done = c_ve.n
                b.wait("te", c_ve, dg_done)
                for t in range(NT):
                    t0 = t * 512
                    half = (dc * NT + t) % 2
                    if dc * NT + t >= 2:
                        b.wait("te", c_ve, hb_ev[dc * NT + t - 2])
                    for m in range(1, 21):
                        b.op("te", lambda h, dc=dc, m=m, t0=t0, half=half: h.matmul(
                            PA[:, half * 512:(half + 1) * 512],
                            DGT(0, 128, (m - 1) * 128, m * 128),
                            XN[:, dc * PADT + PADW + t0 - m: dc * PADT + PADW + t0 - m + 512],
                            start=(m == 1), stop=(m == 20)),
                            [(c_te, 1)] if m == 20 else ())
                    cv_te[dc] = c_te.n
                    b.wait("ve", c_te, cv_te[dc])
                    b.op("ve", lambda h, dc=dc, t0=t0, half=half: h.tensor_copy(
                        HB(0, 128, dc * T + t0, dc * T + t0 + 512),
                        PA[:, half * 512:(half + 1) * 512]), [(c_ve, 1)])
                    hb_ev[dc * NT + t] = c_ve.n
            conv_ve = c_ve.n

            b.wait("te", c_ve, conv_ve)
            wp_ev = {}
            for t in range(NT):
                t0 = t * 512
                for oc in range(DC):
                    gi = t * DC + oc
                    pvt = (P4, P5, P6)[gi % 3]
                    if gi >= 3:
                        b.wait("te", c_ve, wp_ev[gi - 3])
                    for dc in range(DC):
                        b.op("te", lambda h, dc=dc, oc=oc, t0=t0, pvt=pvt: h.matmul(
                            pvt[:, 0:512],
                            W1B(0, 128, dc * 1024 + oc * 128, dc * 1024 + (oc + 1) * 128),
                            xn_sl(dc, t0, 512),
                            start=(dc == 0), stop=False),
                            ())
                    for dc in range(DC):
                        b.op("te", lambda h, dc=dc, oc=oc, t0=t0, pvt=pvt: h.matmul(
                            pvt[:, 0:512],
                            W2B(0, 128, dc * 1024 + oc * 128, dc * 1024 + (oc + 1) * 128),
                            HB(0, 128, dc * T + t0, dc * T + t0 + 512),
                            start=False, stop=False),
                            ())
                    b.op("te", lambda h, oc=oc, pvt=pvt: h.matmul(
                        pvt[:, 0:512],
                        BIASR(0, 1, oc * 128, (oc + 1) * 128),
                        ONR(0, 1, 0, 512),
                        start=False, stop=True), [(c_te, 1)])
                    wpm = c_te.n
                    b.wait("ve", c_te, wpm)
                    b.op("ve", lambda h, oc=oc, t0=t0, pvt=pvt: h.tensor_tensor(
                        xsb_sl(oc, t0, 512), xsb_sl(oc, t0, 512), pvt[:, 0:512],
                        AluOpType.add), [(c_ve, 1)])
                    wp_ev[gi] = c_ve.n
            return c_ve.n

        res_marks = {}
        for ph in phases:
            if ph in ("l0ffn1", "l0ffn2"):
                emit_l0_ffn(ph)
            elif ph == "l0attn":
                emit_l0_attn()
            elif ph == "l1moe1":
                emit_l1_moe(ph, 16)
            elif ph == "l1moe2":
                emit_l1_moe(ph, 32)
            elif ph == "l1conv":
                emit_l1_conv()

        last_ve = c_ve.n
        b.wait("sy", c_ve, last_ve)
        yTr = yT.rearrange("(c p) t -> p c t", p=128)
        b.op("sy", lambda h: h.dma_start(
            yTr, XSB[:].rearrange("p (c t) -> p c t", c=DC)), [(c_out, 16)])
        b.wait("sy", c_out)

        blk = es.enter_context(nc.Block())
        b.emit(blk)
    return nc


def _mk_inmaps(inputs):
    import ml_dtypes
    x = inputs["L0_attn_x"]
    consts = {
        "identb": np.eye(128, dtype=ml_dtypes.bfloat16),
        "identf": np.eye(128, dtype=np.float32),
        "onesb": np.ones((128, 1), dtype=ml_dtypes.bfloat16),
        "onesfr": np.ones((1, 128), dtype=np.float32),
        "onesbr": np.ones((1, 128), dtype=ml_dtypes.bfloat16),
    }
    wkeys = [k for k in inputs if k != "L0_attn_x"]
    maps = []
    for core in range(8):
        m = {"xT": np.ascontiguousarray(x[core].T)}
        for k in wkeys:
            m[k] = np.ascontiguousarray(inputs[k])
        m.update(consts)
        maps.append(m)
    return maps


def kernel(**inputs):
    nc = build_program(["l0ffn1", "l0attn", "l0ffn2", "l1moe1", "l1conv", "l1moe2"])
    maps = _mk_inmaps(inputs)
    res = run_bass_kernel_spmd(nc, maps, core_ids=list(range(8)))
    out = np.stack([np.asarray(r["yT"]).T for r in res.results], axis=0)
    return np.ascontiguousarray(out.astype(np.float32))


# version 05
# speedup vs baseline: 1.6245x; 1.1389x over previous
import contextlib
import numpy as np
import concourse.bass as bass
import concourse.mybir as mybir
from concourse.bass_utils import run_bass_kernel_spmd
from concourse.alu_op_type import AluOpType

F32 = mybir.dt.float32
BF16 = mybir.dt.bfloat16
AF = mybir.ActivationFunctionType
AX = mybir.AxisListType

D = 1024
T = 2048
DC = D // 128
NT = T // 512
PADW = 24
PADT = T + PADW

ENGINES = ("te", "ve", "se", "sy", "ge")


class Chan:
    __slots__ = ("sem", "n", "name")

    def __init__(self, sem, name):
        self.sem = sem
        self.n = 0
        self.name = name


class Builder:
    def __init__(self, nc, es):
        self.nc = nc
        self.es = es
        self.ops = {e: [] for e in ENGINES}
        self.chans = {}
        self.dma_chans = []

    def ch(self, name, dma=False):
        if name not in self.chans:
            sem = self.es.enter_context(self.nc.semaphore(name))
            self.chans[name] = Chan(sem, name)
            if dma:
                self.dma_chans.append(self.chans[name])
        return self.chans[name]

    def dma_barrier(self, engines=("ge", "sy")):
        for c in self.dma_chans:
            if c.n:
                for e in engines:
                    self.wait(e, c)

    def op(self, eng, f, incs=()):
        for c, a in incs:
            c.n += a
        semincs = tuple((c.sem, a) for c, a in incs)

        def g(h, f=f, semincs=semincs):
            ins = f(h)
            for s, a in semincs:
                ins = ins.then_inc(s, a)
        self.ops[eng].append(g)

    def wait(self, eng, c, tgt=None):
        t = c.n if tgt is None else tgt
        assert t <= c.n, (c.name, t, c.n)
        self.ops[eng].append(lambda h, s=c.sem, t=t: h.wait_ge(s, t))

    def emit(self, blk):
        for name, attr in (("te", "tensor"), ("ve", "vector"), ("se", "scalar"),
                           ("sy", "sync"), ("ge", "gpsimd")):
            ops = self.ops[name]

            def section(h, ops=ops):
                for f in ops:
                    f(h)
            getattr(blk, attr)(section)


def ceil_div(a, b):
    return (a + b - 1) // b


def build_program(phases):
    nc = bass.Bass("TRN2", target_bir_lowering=False)

    def din(name, shape, dtype=F32):
        return nc.dram_tensor(name, list(shape), dtype, kind="ExternalInput").ap()

    xT = din("xT", (D, T))
    w = {}
    w["l0_wq"] = din("L0_attn_wq", (D, D))
    w["l0_wk"] = din("L0_attn_wk", (D, D))
    w["l0_wv"] = din("L0_attn_wv", (D, D))
    w["l0_wo"] = din("L0_attn_wo", (D, D))
    w["l0_sel"] = din("L0_ffn_expert_sel", (16, D))
    w["l0_keys"] = din("L0_ffn_keys", (16, D, 128))
    w["l0_vals"] = din("L0_ffn_values", (16, 128, D))
    w["l1_W1"] = din("L1_attn_W1", (D, D))
    w["l1_W2"] = din("L1_attn_W2", (D, D))
    w["l1_bias"] = din("L1_attn_bias", (D,))
    w["l1_et"] = din("L1_attn_embeddings_table", (21, D))
    w["l1_rw"] = din("L1_ffn_router_w", (16, D))
    w["l1_rb"] = din("L1_ffn_router_bias", (16,))
    w["l1_up"] = din("L1_ffn_up_proj", (16, D, 512))
    w["l1_gate"] = din("L1_ffn_gate_proj", (16, D, 512))
    w["l1_down"] = din("L1_ffn_down_proj", (16, 512, D))
    w["l1_shg"] = din("L1_ffn_sh_gate", (D, D))
    w["l1_shu"] = din("L1_ffn_sh_up", (D, D))
    w["l1_shd"] = din("L1_ffn_sh_down", (D, D))
    identb = din("identb", (128, 128), BF16)
    identf = din("identf", (128, 128), F32)
    onesb = din("onesb", (128, 1), BF16)
    onesfr = din("onesfr", (1, 128), F32)
    onesbr = din("onesbr", (1, 128), BF16)

    yT = nc.dram_tensor("yT", [D, T], F32, kind="ExternalOutput").ap()
    scr = nc.dram_tensor("scr", [64, T], F32, kind="Internal").ap()

    es = contextlib.ExitStack()
    with es:
        b = Builder(nc, es)
        sb = lambda name, shape, dtype: es.enter_context(nc.sbuf_tensor(name, list(shape), dtype))
        ps = lambda name, shape, dtype=F32: es.enter_context(nc.psum_tensor(name, list(shape), dtype))

        XSB = sb("XSB", (128, DC * T), F32)
        XN = sb("XN", (128, DC * PADT), BF16)
        AR = sb("AR", (128, 49088), BF16)
        IDB = sb("IDB", (128, 128), BF16)
        IDF = sb("IDF", (128, 128), F32)
        ONB = sb("ONB", (128, 1), BF16)
        ONFR = sb("ONFR", (1, 128), F32)
        ONBR = sb("ONBR", (1, 128), BF16)
        EPS = sb("EPS", (128, 1), F32)
        NEG8 = sb("NEG8", (128, 1), F32)
        SMa = sb("SMa", (1, 512), F32)
        SMb = sb("SMb", (1, 512), F32)
        SMc = sb("SMc", (1, 512), F32)
        LNC = sb("LNC", (128, 4096), BF16)

        PA = ps("PA", (128, 1024))
        PB = ps("PB", (128, 1024))
        P4 = ps("P4", (128, 512))
        P5 = ps("P5", (128, 512))
        P6 = ps("P6", (128, 512))
        P7 = ps("P7", (128, 512))

        c_te = b.ch("c_te")
        c_ve = b.ch("c_ve")
        c_se = b.ch("c_se")
        c_ge = b.ch("c_ge")
        c_sy = b.ch("c_sy", dma=True)
        c_init = b.ch("c_init", dma=True)
        c_xl = b.ch("c_xl", dma=True)
        c_gw = b.ch("c_gw", dma=True)
        c_bc0 = b.ch("c_bc0", dma=True)
        c_bc1 = b.ch("c_bc1", dma=True)
        c_scw = b.ch("c_scw", dma=True)
        c_out = b.ch("c_out", dma=True)

        def xn_sl(c, t0, tw):
            return XN[:, c * PADT + PADW + t0: c * PADT + PADW + t0 + tw]

        def xsb_sl(c, t0, tw):
            return XSB[:, c * T + t0: c * T + t0 + tw]

        class Arena:
            def __init__(self):
                self.top = 0

            def alloc(self, cols, dtype=BF16):
                bcols = cols * (2 if dtype == F32 else 1)
                bcols = (bcols + 63) // 64 * 64
                c0 = self.top
                self.top += bcols
                assert self.top <= 49088, f"arena overflow {self.top}"
                if dtype == BF16:
                    return lambda p0, p1, f0, f1: AR[p0:p1, c0 + f0: c0 + f1]
                else:
                    arf = AR[:].bitcast(F32)
                    c0f = c0 // 2
                    return lambda p0, p1, f0, f1: arf[p0:p1, c0f + f0: c0f + f1]

            def reset(self, top=0):
                self.top = top

        b.op("sy", lambda h: h.dma_start(IDB[:], identb), [(c_init, 16)])
        b.op("sy", lambda h: h.dma_start(IDF[:], identf), [(c_init, 16)])
        b.op("sy", lambda h: h.dma_start(ONB[:], onesb), [(c_init, 16)])
        b.op("sy", lambda h: h.dma_start(ONFR[:], onesfr), [(c_init, 16)])
        b.op("sy", lambda h: h.dma_start(ONBR[:], onesbr), [(c_init, 16)])
        b.op("ge", lambda h: h.memset(EPS[:], 1e-5), [(c_ge, 1)])
        b.op("ge", lambda h: h.memset(NEG8[:], -8.0), [(c_ge, 1)])
        for c in range(DC):
            b.op("ge", lambda h, c=c: h.memset(XN[:, c * PADT: c * PADT + PADW], 0.0),
                 [(c_ge, 1)])
        init_ge = c_ge.n

        xTr = xT.rearrange("(c p) t -> p c t", p=128)
        b.op("sy", lambda h: h.dma_start(
            XSB[:].rearrange("p (c t) -> p c t", c=DC), xTr), [(c_xl, 16)])

        b.wait("ve", c_xl)
        b.wait("te", c_init)
        b.wait("te", c_ge, init_ge)
        b.wait("se", c_ge, init_ge)

        def emit_layernorm():
            b.wait("ve", c_ve)
            b.wait("te", c_se)
            prev_sumsq = 0
            prev_affine = 0
            for t in range(NT):
                t0 = t * 512
                xv = XSB[:].rearrange("p (c t) -> p c t", c=DC)[:, :, t0:t0 + 512]
                lnc3 = LNC[:].rearrange("p (c t) -> p c t", c=DC)[:, :, 0:512]
                if t > 0:
                    b.wait("ve", c_te, prev_sumsq)
                    b.wait("ve", c_ve, prev_affine)
                b.op("ve", lambda h, xv=xv, lnc3=lnc3: h.tensor_copy(lnc3, xv), [(c_ve, 1)])
                cast_done = c_ve.n
                b.wait("te", c_ve, cast_done)
                for c in range(DC):
                    b.op("te", lambda h, c=c: h.matmul(
                        PA[0:1, 0:512], ONB[:], LNC[:, c * 512:(c + 1) * 512],
                        start=(c == 0), stop=(c == DC - 1)),
                        [(c_te, 1)] if c == DC - 1 else ())
                sum_done = c_te.n
                b.wait("se", c_ve, cast_done)
                b.wait("se", c_te, sum_done)
                b.op("se", lambda h, lnc3=lnc3: h.activation(
                    lnc3, lnc3, AF.Square), [(c_se, 1)])
                sq_done = c_se.n
                b.wait("te", c_se, sq_done)
                for c in range(DC):
                    b.op("te", lambda h, c=c: h.matmul(
                        PA[0:1, 512:1024], ONB[:], LNC[:, c * 512:(c + 1) * 512],
                        start=(c == 0), stop=(c == DC - 1)),
                        [(c_te, 1)] if c == DC - 1 else ())
                sumsq_done = c_te.n
                prev_sumsq = sumsq_done
                b.wait("ve", c_te, sum_done)
                b.op("ve", lambda h: h.tensor_scalar(
                    SMa[:], PA[0:1, 0:512], 1.0 / D, None, AluOpType.mult), [(c_ve, 1)])
                mean_done = c_ve.n
                b.wait("ve", c_te, sumsq_done)
                b.op("ve", lambda h: h.tensor_scalar(
                    SMb[:], PA[0:1, 512:1024], 1.0 / D, None, AluOpType.mult), [(c_ve, 1)])
                b.wait("se", c_ve, mean_done)
                b.op("se", lambda h: h.activation(SMc[:], SMa[:], AF.Square), [(c_se, 1)])
                b.wait("ve", c_se)
                b.wait("ve", c_ve)
                b.op("ve", lambda h: h.tensor_tensor(
                    SMb[:], SMb[:], SMc[:], AluOpType.subtract), [(c_ve, 1)])
                b.wait("se", c_ve)
                b.op("se", lambda h: h.activation(
                    SMb[:], SMb[:], AF.Sqrt, bias=EPS[0:1, :]), [(c_se, 1)])
                b.wait("ve", c_se)
                b.op("ve", lambda h: h.reciprocal(SMb[:], SMb[:]), [(c_ve, 1)])
                rstd_done = c_ve.n
                b.wait("ve", c_ve, rstd_done)
                b.op("ve", lambda h: h.tensor_tensor(
                    SMc[:], SMa[:], SMb[:], AluOpType.mult), [(c_ve, 1)])
                brow_done = c_ve.n
                b.wait("te", c_ve, brow_done)
                b.op("te", lambda h: h.matmul(PB[:, 0:512], ONFR[:], SMb[:], start=True, stop=True),
                     [(c_te, 1)])
                b.op("te", lambda h: h.matmul(PB[:, 512:1024], ONFR[:], SMc[:], start=True, stop=True),
                     [(c_te, 1)])
                rep_done = c_te.n
                b.wait("ve", c_te, rep_done)
                for c in range(DC):
                    b.op("ve", lambda h, c=c, t0=t0: h.tensor_tensor(
                        LNC[:, c * 512:(c + 1) * 512],
                        XSB[:, c * T + t0: c * T + t0 + 512],
                        PB[:, 0:512], AluOpType.mult), [(c_ve, 1)])
                mul_done = c_ve.n
                b.wait("ve", c_ve, mul_done)
                for c in range(DC):
                    b.op("ve", lambda h, c=c, t0=t0: h.tensor_tensor(
                        xn_sl(c, t0, 512), LNC[:, c * 512:(c + 1) * 512],
                        PB[:, 512:1024], AluOpType.subtract), [(c_ve, 1)])
                prev_affine = c_ve.n
            return c_ve.n

        def emit_l0_ffn(tag):
            ar = Arena()
            KEY = ar.alloc(16 * 1024)
            VAL = ar.alloc(16 * 1024)
            EST = ar.alloc(128)
            ESN = ar.alloc(1024)
            SC = ar.alloc(16 * 512)
            SBC0 = ar.alloc(512)
            SBC1 = ar.alloc(512)
            SELF = ar.alloc(2048, F32)

            entry_te, entry_ve, entry_se = c_te.n, c_ve.n, c_se.n
            xn_done = emit_layernorm()

            for _e in ("ge", "sy"):
                b.wait(_e, c_te, entry_te)
                b.wait(_e, c_ve, entry_ve)
                b.wait(_e, c_se, entry_se)
            b.dma_barrier()
            keys_r = w["l0_keys"].rearrange("e (c p) hh -> e p c hh", p=128)
            vals_r = w["l0_vals"]
            for e in range(16):
                b.op("ge", lambda h, e=e: h.dma_start(
                    KEY(0, 128, e * 1024, (e + 1) * 1024).rearrange("p (c hh) -> p c hh", c=DC),
                    keys_r[e]), [(c_gw, 16)])
                b.op("ge", lambda h, e=e: h.dma_start(
                    VAL(0, 128, e * 1024, (e + 1) * 1024), vals_r[e]), [(c_gw, 16)])
            b.op("ge", lambda h: h.dma_start(
                ESN(0, 16, 0, 1024), w["l0_sel"]), [(c_gw, 16)])
            wload = c_gw.n

            b.wait("te", c_gw, wload)
            pa_b = PA[:].bitcast(BF16)
            for c in range(DC):
                b.op("te", lambda h, c=c: h.transpose(
                    pa_b[:, c * 16:(c + 1) * 16],
                    ESN(0, 16, c * 128, (c + 1) * 128), IDB[0:16, 0:16]),
                    [(c_te, 1)] if c == DC - 1 else ())
            est_tr = c_te.n
            b.wait("ve", c_te, est_tr)
            b.op("ve", lambda h: h.tensor_copy(EST(0, 128, 0, 128), pa_b[:, 0:128]),
                 [(c_ve, 1)])
            est_done = c_ve.n

            b.wait("te", c_ve, xn_done)
            b.wait("te", c_ve, est_done)
            sig_mark = {0: 0, 1: 0}
            for t in range(NT):
                t0 = t * 512
                if sig_mark[t % 2]:
                    b.wait("te", c_se, sig_mark[t % 2])
                for c in range(DC):
                    b.op("te", lambda h, c=c, t0=t0, t=t: h.matmul(
                        PB[0:16, (t % 2) * 512:(t % 2) * 512 + 512],
                        EST(0, 128, c * 16, (c + 1) * 16),
                        xn_sl(c, t0, 512),
                        start=(c == 0), stop=(c == DC - 1)),
                        [(c_te, 1)] if c == DC - 1 else ())
                lg = c_te.n
                b.wait("se", c_te, lg)
                b.op("se", lambda h, t=t, t0=t0: h.activation(
                    SELF(0, 16, t0, t0 + 512), PB[0:16, (t % 2) * 512:(t % 2) * 512 + 512],
                    AF.Sigmoid), [(c_se, 1)])
                sig_mark[t % 2] = c_se.n
            sel_sig = c_se.n
            b.wait("ve", c_se, sel_sig)
            b.op("ve", lambda h: h.tensor_scalar(
                SELF(0, 16, 0, T), SELF(0, 16, 0, T), 0.5, None, AluOpType.mult),
                [(c_ve, 1)])
            sel_done = c_ve.n
            b.wait("ge", c_ve, sel_done)
            b.op("ge", lambda h: h.dma_start(scr[0:16, :], SELF(0, 16, 0, T)), [(c_scw, 16)])
            b.wait("ge", c_scw)

            se_relu_mark = {0: 0, 1: 0}
            sbc_lastread = {0: 0, 1: 0}
            bc_target = {}
            vdone_mark = {}
            val_te_mark = 0
            for t in range(NT):
                t0 = t * 512
                for e in range(16):
                    idx = t * 16 + e
                    half = e % 2
                    bc = (c_bc0, c_bc1)[idx % 2]
                    sbc = (SBC0, SBC1)[idx % 2]
                    if sbc_lastread[idx % 2]:
                        b.wait("ge", c_ve, sbc_lastread[idx % 2])
                    b.op("ge", lambda h, e=e, t0=t0, sbc=sbc: h.dma_start(
                        sbc(0, 128, 0, 512),
                        scr[e:e + 1, t0:t0 + 512].broadcast_to((128, 512))), [(bc, 16)])
                    bc_target[idx] = bc.n
                    if se_relu_mark[half]:
                        b.wait("te", c_se, se_relu_mark[half])
                    if e == 0 and val_te_mark:
                        b.wait("se", c_te, val_te_mark)
                    for c in range(DC):
                        b.op("te", lambda h, c=c, e=e, t0=t0, half=half: h.matmul(
                            PA[:, half * 512: half * 512 + 512],
                            KEY(0, 128, e * 1024 + c * 128, e * 1024 + (c + 1) * 128),
                            xn_sl(c, t0, 512),
                            start=(c == 0), stop=(c == DC - 1)),
                            [(c_te, 1)] if c == DC - 1 else ())
                    key_done = c_te.n
                    b.wait("se", c_te, key_done)
                    b.op("se", lambda h, e=e, half=half: h.activation(
                        SC(0, 128, e * 512, (e + 1) * 512),
                        PA[:, half * 512: half * 512 + 512], AF.Relu), [(c_se, 1)])
                    se_relu_mark[half] = c_se.n
                    b.wait("ve", bc, bc_target[idx])
                    b.wait("ve", c_se, se_relu_mark[half])
                    b.op("ve", lambda h, e=e, sbc=sbc: h.tensor_tensor(
                        SC(0, 128, e * 512, (e + 1) * 512),
                        SC(0, 128, e * 512, (e + 1) * 512),
                        sbc(0, 128, 0, 512), AluOpType.mult), [(c_ve, 1)])
                    sbc_lastread[idx % 2] = c_ve.n
                sc_ready = c_ve.n
                b.wait("te", c_ve, sc_ready)
                for v in range(DC):
                    vidx = t * DC + v
                    pv = (P4, P5, P6)[vidx % 3]
                    if vidx >= 3:
                        b.wait("te", c_ve, vdone_mark[vidx - 3])
                    for e in range(16):
                        b.op("te", lambda h, pv=pv, v=v, e=e: h.matmul(
                            pv[:, 0:512],
                            VAL(0, 128, e * 1024 + v * 128, e * 1024 + (v + 1) * 128),
                            SC(0, 128, e * 512, (e + 1) * 512),
                            start=(e == 0), stop=(e == 15)),
                            [(c_te, 1)] if e == 15 else ())
                    val_done = c_te.n
                    b.wait("ve", c_te, val_done)
                    b.op("ve", lambda h, pv=pv, v=v, t0=t0: h.tensor_tensor(
                        xsb_sl(v, t0, 512), xsb_sl(v, t0, 512), pv[:, 0:512],
                        AluOpType.add), [(c_ve, 1)])
                    vdone_mark[vidx] = c_ve.n
                val_te_mark = c_te.n
            return c_ve.n

        def emit_l0_attn():
            ar = Arena()
            WQT = ar.alloc(8192)
            WKT = ar.alloc(8192)
            WVT = ar.alloc(8192)
            WOT = ar.alloc(8192)
            WSTG = ar.alloc(1024)
            QKTM = ar.alloc(1024)
            QTBe = [ar.alloc(1024), ar.alloc(1024)]
            QTBo = [ar.alloc(1024), ar.alloc(1024)]
            KTBe = [ar.alloc(1024), ar.alloc(1024)]
            KTBo = [ar.alloc(1024), ar.alloc(1024)]
            VTM = ar.alloc(1040)
            PT0 = ar.alloc(1024)
            PT1 = ar.alloc(1024)
            AOTh = ar.alloc(1024)
            RBC = ar.alloc(1024)
            SMR = WSTG

            P6b = P6[:].bitcast(BF16)
            P7b = P7[:].bitcast(BF16)

            entry_te, entry_ve, entry_se = c_te.n, c_ve.n, c_se.n
            xn_done = emit_layernorm()
            for _e in ("ge", "sy"):
                b.wait(_e, c_te, entry_te)
                b.wait(_e, c_ve, entry_ve)
                b.wait(_e, c_se, entry_se)
            b.dma_barrier()

            wsrc = [w["l0_wq"], w["l0_wk"], w["l0_wv"], w["l0_wo"]]
            wdst = [WQT, WKT, WVT, WOT]
            c_ws = [b.ch("c_ws0", dma=True), b.ch("c_ws1", dma=True)]
            stg_te_mark = {}
            evac_ve_mark = {}
            b.wait("te", c_ve, entry_ve)
            b.wait("te", c_se, entry_se)
            for idx in range(32):
                wi, oc = idx // 8, idx % 8
                slot = idx % 2
                if idx >= 1:
                    b.wait("ge", c_te, stg_te_mark[idx - 1])
                b.op("ge", lambda h, wi=wi, oc=oc: h.dma_start(
                    WSTG(0, 128, 0, 1024),
                    wsrc[wi][oc * 128:(oc + 1) * 128, :]), [(c_ws[slot], 16)])
                b.wait("te", c_ws[slot], 16 * (idx // 2 + 1))
                if idx >= 2:
                    b.wait("te", c_ve, evac_ve_mark[idx - 2])
                pb_t = (P6b, P7b)[slot]
                for cc in range(DC):
                    b.op("te", lambda h, cc=cc, pb_t=pb_t: h.transpose(
                        pb_t[:, cc * 128:(cc + 1) * 128],
                        WSTG(0, 128, cc * 128, (cc + 1) * 128),
                        IDB[:]), [(c_te, 1)] if cc == DC - 1 else ())
                stg_te_mark[idx] = c_te.n
                b.wait("ve", c_te, stg_te_mark[idx])
                dst = wdst[wi]
                b.op("ve", lambda h, dst=dst, oc=oc, pb_t=pb_t: h.tensor_copy(
                    dst(0, 128, 0, 8192).rearrange("p (cc o) -> p cc o", cc=DC)[
                        :, :, oc * 128:(oc + 1) * 128],
                    pb_t.rearrange("p (cc s) -> p cc s", cc=DC)), [(c_ve, 1)])
                evac_ve_mark[idx] = c_ve.n
            wprep_ve = c_ve.n

            b.op("ge", lambda h: h.memset(
                VTM(0, 128, 0, 1040).rearrange("p (g dd) -> p g dd", dd=65)[:, :, 64:65],
                1.0), [(c_ge, 1)])
            vones = c_ge.n
            b.wait("te", c_ge, vones)
            b.wait("te", c_ve, wprep_ve)
            b.wait("te", c_ve, xn_done)

            marks = {
                "exp2": {0: 0, 1: 0},
                "qk": {0: 0, 1: 0},
                "pt_av": {0: 0, 1: 0},
                "qtm_ev": {}, "v_ev": {},
                "qtb_ev": {}, "ktb_ev": {},
                "rbc_ve": 0,
                "aoth_ev": {},
                "oproj_te": {}, "oproj_ve": {},
                "projlast_ve": 0,
                "smr": 0,
                "p67_ve": 0,
            }

            def emit_proj(h16, which):
                hcol = 128 * h16
                wt = {"q": WQT, "k": WKT, "v": WVT}[which]
                b.wait("te", c_ve, marks["p67_ve"])
                for half in range(2):
                    pvt = (P6, P7)[half]
                    for cc in range(DC):
                        b.op("te", lambda h, cc=cc, half=half, hcol=hcol, wt=wt, pvt=pvt: h.matmul(
                            pvt[:, 0:512],
                            xn_sl(cc, hcol, 128),
                            wt(0, 128, cc * 1024 + half * 512, cc * 1024 + (half + 1) * 512),
                            start=(cc == 0), stop=(cc == DC - 1)),
                            [(c_te, 1)] if cc == DC - 1 else ())
                p_done = c_te.n
                b.wait("ve", c_te, p_done)
                if which == "v":
                    b.wait("ve", c_ge, vones)
                    for half in range(2):
                        pvt = (P6, P7)[half]
                        b.op("ve", lambda h, half=half, pvt=pvt: h.tensor_copy(
                            VTM(0, 128, half * 520, half * 520 + 520).rearrange(
                                "p (g dd) -> p g dd", dd=65)[:, :, 0:64],
                            pvt[:, 0:512].rearrange(
                                "p (g dd) -> p g dd", dd=64)), [(c_ve, 1)])
                    marks["v_ev"][h16] = c_ve.n
                else:
                    if marks["qtm_ev"].get("last_tr"):
                        b.wait("ve", c_te, marks["qtm_ev"]["last_tr"])
                    b.op("ve", lambda h: h.tensor_copy(
                        QKTM(0, 128, 0, 512), P6[:, 0:512]), [(c_ve, 1)])
                    b.op("ve", lambda h: h.tensor_copy(
                        QKTM(0, 128, 512, 1024), P7[:, 0:512]), [(c_ve, 1)])
                    marks["qtm_ev"][(h16, which)] = c_ve.n
                marks["p67_ve"] = c_ve.n

            def emit_tr(h16, which):
                par = h16 % 2
                dste = (QTBe, KTBe)[which == "k"][par]
                dsto = (QTBo, KTBo)[which == "k"][par]
                b.wait("te", c_ve, marks["qtm_ev"][(h16, which)])
                b.wait("te", c_ve, marks["p67_ve"])
                for bb in range(DC):
                    b.op("te", lambda h, bb=bb: h.transpose(
                        P6b[:, bb * 128:(bb + 1) * 128],
                        QKTM(0, 128, bb * 128, (bb + 1) * 128), IDB[:]),
                        [(c_te, 1)] if bb == DC - 1 else ())
                marks["qtm_ev"]["last_tr"] = c_te.n
                b.wait("ve", c_te, marks["qtm_ev"]["last_tr"])
                b.op("ve", lambda h, dste=dste: h.tensor_copy(
                    dste(0, 64, 0, 1024), P6b[0:64, :]), [(c_ve, 1)])
                b.op("ve", lambda h, dsto=dsto: h.tensor_copy(
                    dsto(0, 64, 0, 1024), P6b[64:128, :]), [(c_ve, 1)])
                marks[("qtb_ev", "ktb_ev")[which == "k"]][h16] = c_ve.n
                marks["p67_ve"] = c_ve.n

            def emit_round(h16, pi, inserts=()):
                par = h16 % 2
                QTB = (QTBe, QTBo)[pi][par]
                KTBe_, KTBo_ = KTBe[par], KTBo[par]
                ins = dict(inserts)
                b.wait("te", c_ve, marks["qtb_ev"][h16])
                b.wait("te", c_ve, marks["ktb_ev"][h16])
                b.wait("te", c_ve, marks["v_ev"][h16])

                def emit_qk(jg):
                    KTB = (KTBe_, KTBo_)[jg % 2]
                    jb = jg // 2
                    sc = (PA, PB)[jg % 2]
                    if marks["exp2"][jg % 2]:
                        b.wait("te", c_se, marks["exp2"][jg % 2])
                    for ih in range(2):
                        b.op("te", lambda h, KTB=KTB, jb=jb, QTB=QTB, ih=ih, sc=sc: h.matmul(
                            sc[:, ih * 512:(ih + 1) * 512],
                            KTB(0, 64, jb * 128, (jb + 1) * 128),
                            QTB(0, 64, ih * 512, (ih + 1) * 512),
                            start=True, stop=True),
                            [(c_te, 1)] if ih == 1 else ())
                    marks["qk"][jg % 2] = c_te.n

                emit_qk(0)
                for jg in range(16):
                    if jg in ins:
                        ins[jg]()
                    sc = (PA, PB)[jg % 2]
                    pt = (PT0, PT1)[jg % 2]
                    b.wait("se", c_te, marks["qk"][jg % 2])
                    if marks["pt_av"][jg % 2]:
                        b.wait("se", c_te, marks["pt_av"][jg % 2])
                    b.op("se", lambda h, pt=pt, sc=sc: h.activation(
                        pt(0, 128, 0, 1024), sc[:], AF.Exp,
                        bias=NEG8[:], scale=0.125), [(c_se, 1)])
                    marks["exp2"][jg % 2] = c_se.n
                    if jg < 15:
                        emit_qk(jg + 1)
                    if jg == 0:
                        if pi == 1:
                            b.wait("te", c_ve, marks["aoth_ev"][(h16, 0)])
                        elif marks["oproj_ve"].get(h16 - 1) is not None:
                            b.wait("te", c_ve, marks["oproj_ve"][h16 - 1])
                    b.wait("te", c_se, marks["exp2"][jg % 2])
                    for ih in range(2):
                        pvt = (P4, P5)[ih]
                        b.op("te", lambda h, pvt=pvt, jg=jg, pt=pt, ih=ih: h.matmul(
                            pvt[0:65, 0:512],
                            VTM(0, 128, jg * 65, (jg + 1) * 65),
                            pt(0, 128, ih * 512, (ih + 1) * 512),
                            start=(jg == 0), stop=(jg == 15)),
                            [(c_te, 1)] if ih == 1 else ())
                    marks["pt_av"][jg % 2] = c_te.n
                av_done = c_te.n
                b.wait("ve", c_te, av_done)
                b.wait("ve", c_ve, marks["smr"])
                b.op("ve", lambda h: h.reciprocal(SMa[:], P4[64:65, 0:512]), [(c_ve, 1)])
                b.op("ve", lambda h: h.reciprocal(SMb[:], P5[64:65, 0:512]), [(c_ve, 1)])
                b.wait("ve", c_ve)
                b.op("ve", lambda h: h.tensor_copy(SMR(0, 1, 0, 512), SMa[:]), [(c_ve, 1)])
                b.op("ve", lambda h: h.tensor_copy(SMR(0, 1, 512, 1024), SMb[:]), [(c_ve, 1)])
                rec_done = c_ve.n
                marks["smr"] = rec_done
                b.wait("te", c_ve, rec_done)
                b.wait("te", c_ve, marks["p67_ve"])
                b.op("te", lambda h: h.matmul(P6[0:64, :], ONBR[0:1, 0:64], SMR(0, 1, 0, 512),
                                              start=True, stop=True), [(c_te, 1)])
                b.op("te", lambda h: h.matmul(P7[0:64, :], ONBR[0:1, 0:64], SMR(0, 1, 512, 1024),
                                              start=True, stop=True), [(c_te, 1)])
                rep_done = c_te.n
                b.wait("ve", c_te, rep_done)
                b.op("ve", lambda h: h.tensor_copy(RBC(0, 64, 0, 512), P6[0:64, :]), [(c_ve, 1)])
                b.op("ve", lambda h: h.tensor_copy(RBC(0, 64, 512, 1024), P7[0:64, :]), [(c_ve, 1)])
                marks["rbc_ve"] = c_ve.n
                marks["p67_ve"] = c_ve.n
                b.wait("ve", c_ve, marks["rbc_ve"])
                if marks["oproj_te"].get(h16 - 1) is not None and pi == 0:
                    b.wait("ve", c_te, marks["oproj_te"][h16 - 1])
                for ih in range(2):
                    pvt = (P4, P5)[ih]
                    b.op("ve", lambda h, pvt=pvt, pi=pi, ih=ih: h.tensor_tensor(
                        AOTh(64 * pi, 64 * pi + 64, ih * 512, (ih + 1) * 512),
                        pvt[0:64, 0:512],
                        RBC(0, 64, ih * 512, (ih + 1) * 512),
                        AluOpType.mult), [(c_ve, 1)])
                marks["aoth_ev"][(h16, pi)] = c_ve.n

            def emit_oproj(h16):
                hcol = 128 * h16
                opev = {}
                b.wait("te", c_ve, marks["aoth_ev"][(h16, 1)])
                for ccp in range(DC):
                    pvt = (P4, P5)[ccp % 2]
                    if ccp >= 2:
                        b.wait("te", c_ve, opev[ccp - 2])
                    for oc in range(DC):
                        b.op("te", lambda h, ccp=ccp, oc=oc, pvt=pvt: h.matmul(
                            pvt[:, 0:128],
                            WOT(0, 128, oc * 1024 + ccp * 128, oc * 1024 + (ccp + 1) * 128),
                            AOTh(0, 128, oc * 128, (oc + 1) * 128),
                            start=(oc == 0), stop=(oc == DC - 1)),
                            [(c_te, 1)] if oc == DC - 1 else ())
                    op_done = c_te.n
                    b.wait("ve", c_te, op_done)
                    b.op("ve", lambda h, ccp=ccp, hcol=hcol, pvt=pvt: h.tensor_tensor(
                        xsb_sl(ccp, hcol, 128), xsb_sl(ccp, hcol, 128),
                        pvt[:, 0:128], AluOpType.add), [(c_ve, 1)])
                    opev[ccp] = c_ve.n
                marks["oproj_te"][h16] = c_te.n
                marks["oproj_ve"][h16] = c_ve.n

            emit_proj(0, "q")
            emit_tr(0, "q")
            emit_proj(0, "k")
            emit_tr(0, "k")
            emit_proj(0, "v")
            for h16 in range(16):
                if h16 < 15:
                    emit_round(h16, 0, inserts=[
                        (4, lambda: emit_proj(h16 + 1, "q")),
                        (10, lambda: emit_tr(h16 + 1, "q"))])
                    emit_round(h16, 1, inserts=[
                        (3, lambda: emit_proj(h16 + 1, "k")),
                        (8, lambda: emit_tr(h16 + 1, "k"))])
                    emit_proj(h16 + 1, "v")
                else:
                    emit_round(h16, 0)
                    emit_round(h16, 1)
                emit_oproj(h16)
            return c_ve.n

        def emit_l1_moe(tag, scrow):
            ar = Arena()
            CBC0 = ar.alloc(2048)
            CBC1 = ar.alloc(2048)
            RWT = ar.alloc(128)
            RWN = ar.alloc(1024)
            LGT = ar.alloc(2048, F32)
            LTM = ar.alloc(256, F32)
            CTM = ar.alloc(256, F32)
            CBT = ar.alloc(2048, F32)
            RB16 = ar.alloc(1, F32)
            SCL = ar.alloc(8, F32)
            HS = ar.alloc(1024)
            HID0 = ar.alloc(1024)
            HID1 = ar.alloc(1024)
            SHHC = ar.alloc(2048)
            ovl = ar.top
            SHGT = ar.alloc(8192)
            SHUT = ar.alloc(8192)
            SHDT = ar.alloc(8192)
            WSTG2 = ar.alloc(2048)
            ar.reset(ovl)
            ESL0 = ar.alloc(12288)
            ESL1 = ar.alloc(12288)

            P6b = P6[:].bitcast(BF16)
            P7b = P7[:].bitcast(BF16)
            m1c = SCL(0, 128, 0, 1)
            m2c = SCL(0, 128, 1, 2)
            nm1c = SCL(0, 128, 2, 3)
            recc = SCL(0, 128, 3, 4)

            entry_te, entry_ve, entry_se = c_te.n, c_ve.n, c_se.n
            xn_done = emit_layernorm()
            b.wait("ge", c_te, entry_te)
            b.wait("ge", c_ve, entry_ve)
            b.wait("ge", c_se, entry_se)

            c_rt = b.ch("c_rt_" + tag, dma=True)
            b.op("ge", lambda h: h.dma_start(RWN(0, 16, 0, 1024), w["l1_rw"]), [(c_rt, 16)])
            b.op("ge", lambda h: h.dma_start(
                RB16(0, 16, 0, 1), w["l1_rb"].rearrange("(e o) -> e o", o=1)), [(c_rt, 16)])

            c_sh = [b.ch("c_sh0_" + tag, dma=True), b.ch("c_sh1_" + tag, dma=True)]
            stg_te = {}
            evac_ve = {}
            shsrc = [w["l1_shg"], w["l1_shu"], w["l1_shd"]]
            shdst = [SHGT, SHUT, SHDT]
            b.wait("te", c_ve, entry_ve)
            b.wait("te", c_se, entry_se)
            for idx in range(24):
                wi, oc = idx // 8, idx % 8
                slot = idx % 2
                if idx >= 2:
                    b.wait("ge", c_te, stg_te[idx - 2])
                b.op("ge", lambda h, wi=wi, oc=oc, slot=slot: h.dma_start(
                    WSTG2(0, 128, slot * 1024, slot * 1024 + 1024),
                    shsrc[wi][oc * 128:(oc + 1) * 128, :]), [(c_sh[slot], 16)])
                b.wait("te", c_sh[slot], 16 * (idx // 2 + 1))
                if idx >= 2:
                    b.wait("te", c_ve, evac_ve[idx - 2])
                pb_t = (P6b, P7b)[slot]
                for cc in range(DC):
                    b.op("te", lambda h, cc=cc, slot=slot, pb_t=pb_t: h.transpose(
                        pb_t[:, cc * 128:(cc + 1) * 128],
                        WSTG2(0, 128, slot * 1024 + cc * 128, slot * 1024 + (cc + 1) * 128),
                        IDB[:]), [(c_te, 1)] if cc == DC - 1 else ())
                stg_te[idx] = c_te.n
                b.wait("ve", c_te, stg_te[idx])
                b.op("ve", lambda h, wi=wi, oc=oc, pb_t=pb_t: h.tensor_copy(
                    shdst[wi](0, 128, 0, 8192).rearrange("p (cc o) -> p cc o", cc=DC)[
                        :, :, oc * 128:(oc + 1) * 128],
                    pb_t.rearrange("p (cc s) -> p cc s", cc=DC)), [(c_ve, 1)])
                evac_ve[idx] = c_ve.n
            shw_ve = c_ve.n

            b.wait("te", c_rt)
            b.wait("te", c_ve, xn_done)
            b.wait("te", c_ve, shw_ve)
            for cc in range(DC):
                b.op("te", lambda h, cc=cc: h.transpose(
                    P6b[:, cc * 16:(cc + 1) * 16],
                    RWN(0, 16, cc * 128, (cc + 1) * 128), IDB[0:16, 0:16]),
                    [(c_te, 1)] if cc == DC - 1 else ())
            rwt_tr = c_te.n
            b.wait("ve", c_te, rwt_tr)
            b.op("ve", lambda h: h.tensor_copy(RWT(0, 128, 0, 128), P6b[:, 0:128]), [(c_ve, 1)])
            rwt_done = c_ve.n
            b.wait("te", c_ve, rwt_done)
            lg_se = {0: 0, 1: 0}
            for t in range(NT):
                t0 = t * 512
                if lg_se[t % 2]:
                    b.wait("te", c_ve, lg_se[t % 2])
                for cc in range(DC):
                    b.op("te", lambda h, cc=cc, t0=t0, t=t: h.matmul(
                        PB[0:16, (t % 2) * 512:(t % 2) * 512 + 512],
                        RWT(0, 128, cc * 16, (cc + 1) * 16),
                        xn_sl(cc, t0, 512),
                        start=(cc == 0), stop=(cc == DC - 1)),
                        [(c_te, 1)] if cc == DC - 1 else ())
                lgm = c_te.n
                b.wait("ve", c_te, lgm)
                b.wait("ve", c_rt)
                b.op("ve", lambda h, t=t, t0=t0: h.tensor_scalar(
                    LGT(0, 16, t0, t0 + 512), PB[0:16, (t % 2) * 512:(t % 2) * 512 + 512],
                    RB16(0, 16, 0, 1), None, AluOpType.add), [(c_ve, 1)])
                lg_se[t % 2] = c_ve.n
            lgt_done = c_ve.n
            b.wait("te", c_ve, lgt_done)
            for tt in range(16):
                pvt = (P4, P5)[tt // 8]
                b.op("te", lambda h, tt=tt, pvt=pvt: h.transpose(
                    pvt[:, (tt % 8) * 16:(tt % 8) * 16 + 16],
                    LGT(0, 16, tt * 128, (tt + 1) * 128), IDF[0:16, 0:16]),
                    [(c_te, 1)] if tt in (7, 15) else ())
            ltm_tr = c_te.n
            b.wait("ve", c_te, ltm_tr)
            b.op("ve", lambda h: h.tensor_copy(LTM(0, 128, 0, 128), P4[:, 0:128]), [(c_ve, 1)])
            b.op("ve", lambda h: h.tensor_copy(LTM(0, 128, 128, 256), P5[:, 0:128]), [(c_ve, 1)])
            ltm_done = c_ve.n
            b.wait("ve", c_ve, ltm_done)
            for tt in range(16):
                lg = LTM(0, 128, tt * 16, (tt + 1) * 16)
                cm = CTM(0, 128, tt * 16, (tt + 1) * 16)
                b.op("ve", lambda h, lg=lg: h.reduce_max(m1c, lg, axis=AX.X), [(c_ve, 1)])
                b.wait("ve", c_ve)
                b.op("ve", lambda h, lg=lg, cm=cm: h.tensor_scalar(
                    cm, lg, m1c, -1e30, AluOpType.is_ge, AluOpType.mult), [(c_ve, 1)])
                b.wait("ve", c_ve)
                b.op("ve", lambda h, lg=lg, cm=cm: h.tensor_tensor(
                    cm, cm, lg, AluOpType.add), [(c_ve, 1)])
                b.wait("ve", c_ve)
                b.op("ve", lambda h, cm=cm: h.reduce_max(m2c, cm, axis=AX.X), [(c_ve, 1)])
                b.op("ve", lambda h: h.tensor_scalar(
                    nm1c, m1c, -1.0, None, AluOpType.mult), [(c_ve, 1)])
                nm1_done = c_ve.n
                b.wait("se", c_ve, nm1_done)
                b.op("se", lambda h, lg=lg, cm=cm: h.activation(
                    cm, lg, AF.Exp, bias=nm1c), [(c_se, 1)])
                eb_done = c_se.n
                b.wait("ve", c_se, eb_done)
                b.op("ve", lambda h, lg=lg: h.tensor_scalar(
                    lg, lg, m2c, None, AluOpType.is_ge), [(c_ve, 1)])
                b.wait("ve", c_ve)
                b.op("ve", lambda h, lg=lg, cm=cm: h.tensor_tensor(
                    cm, cm, lg, AluOpType.mult), [(c_ve, 1)])
                b.wait("ve", c_ve)
                b.op("ve", lambda h, cm=cm: h.reduce_sum(recc, cm, axis=AX.X), [(c_ve, 1)])
                b.wait("ve", c_ve)
                b.op("ve", lambda h: h.reciprocal(recc, recc), [(c_ve, 1)])
                b.wait("ve", c_ve)
                b.op("ve", lambda h, cm=cm: h.tensor_scalar(
                    cm, cm, recc, 0.5, AluOpType.mult, AluOpType.mult), [(c_ve, 1)])
                b.wait("ve", c_ve)
            ctm_done = c_ve.n
            b.wait("te", c_ve, ctm_done)
            cb_ev = {}
            for tt in range(16):
                pvt = (P4, P5)[tt % 2]
                if tt >= 2:
                    b.wait("te", c_ve, cb_ev[tt - 2])
                b.op("te", lambda h, tt=tt, pvt=pvt: h.transpose(
                    pvt[0:16, 0:128],
                    CTM(0, 128, tt * 16, (tt + 1) * 16), IDF[:]), [(c_te, 1)])
                ctr = c_te.n
                b.wait("ve", c_te, ctr)
                b.op("ve", lambda h, tt=tt, pvt=pvt: h.tensor_copy(
                    CBT(0, 16, tt * 128, (tt + 1) * 128), pvt[0:16, 0:128]), [(c_ve, 1)])
                cb_ev[tt] = c_ve.n
            cbt_done = c_ve.n
            b.wait("ge", c_ve, cbt_done)
            c_scw2 = b.ch("c_scw_" + tag, dma=True)
            b.op("ge", lambda h: h.dma_start(
                scr[scrow:scrow + 16, :], CBT(0, 16, 0, T)), [(c_scw2, 16)])
            b.wait("ge", c_scw2)

            b.wait("te", c_ve, shw_ve)
            b.wait("te", c_ve, lgt_done)
            sh_se = 0
            sh_hid_ve = 0
            sh_x_ve = {}
            sh_dn_te = {}
            for t in range(8):
                t0 = t * 256
                for grp in range(2):
                    if sh_se:
                        b.wait("te", c_se, sh_se)
                        b.wait("te", c_ve, sh_hid_ve)
                    for scn4 in range(4):
                        scn = grp * 4 + scn4
                        for cc in range(DC):
                            b.op("te", lambda h, scn=scn, scn4=scn4, cc=cc, t0=t0: h.matmul(
                                PA[:, scn4 * 256:(scn4 + 1) * 256],
                                SHGT(0, 128, cc * 1024 + scn * 128, cc * 1024 + (scn + 1) * 128),
                                xn_sl(cc, t0, 256),
                                start=(cc == 0), stop=(cc == DC - 1)),
                                [(c_te, 1)] if cc == DC - 1 else ())
                    for scn4 in range(4):
                        scn = grp * 4 + scn4
                        for cc in range(DC):
                            b.op("te", lambda h, scn=scn, scn4=scn4, cc=cc, t0=t0: h.matmul(
                                PB[:, scn4 * 256:(scn4 + 1) * 256],
                                SHUT(0, 128, cc * 1024 + scn * 128, cc * 1024 + (scn + 1) * 128),
                                xn_sl(cc, t0, 256),
                                start=(cc == 0), stop=(cc == DC - 1)),
                                [(c_te, 1)] if cc == DC - 1 else ())
                    gu = c_te.n
                    b.wait("se", c_te, gu)
                    if sh_hid_ve:
                        b.wait("se", c_ve, sh_hid_ve)
                    b.op("se", lambda h: h.activation(
                        HS(0, 128, 0, 1024), PA[:], AF.Silu), [(c_se, 1)])
                    sh_se = c_se.n
                    b.wait("ve", c_se, sh_se)
                    b.op("ve", lambda h: h.tensor_tensor(
                        HS(0, 128, 0, 1024), HS(0, 128, 0, 1024), PB[:],
                        AluOpType.mult), [(c_ve, 1)])
                    b.wait("ve", c_ve)
                    b.op("ve", lambda h, grp=grp: h.tensor_scalar(
                        SHHC(0, 128, grp * 1024, (grp + 1) * 1024),
                        HS(0, 128, 0, 1024), 0.5, None, AluOpType.mult), [(c_ve, 1)])
                    sh_hid_ve = c_ve.n
                b.wait("te", c_ve, sh_hid_ve)
                if t >= 1:
                    b.wait("te", c_ve, sh_x_ve[t - 1])
                for v in range(DC):
                    pvt = (P4, P5, P6, P7)[v // 2]
                    off = (v % 2) * 256
                    for scn in range(DC):
                        b.op("te", lambda h, pvt=pvt, off=off, scn=scn, v=v: h.matmul(
                            pvt[:, off:off + 256],
                            SHDT(0, 128, scn * 1024 + v * 128, scn * 1024 + (v + 1) * 128),
                            SHHC(0, 128, scn * 256, (scn + 1) * 256),
                            start=(scn == 0 and v % 2 == 0),
                            stop=(scn == DC - 1 and v % 2 == 1)),
                            [(c_te, 1)] if (scn == DC - 1 and v == DC - 1) else ())
                sh_dn_te[t] = c_te.n
                b.wait("ve", c_te, sh_dn_te[t])
                for bk in range(4):
                    b.op("ve", lambda h, bk=bk, t0=t0: h.tensor_tensor(
                        XSB[:].rearrange("p (c t) -> p c t", c=DC)[:, bk * 2: bk * 2 + 2, t0:t0 + 256],
                        XSB[:].rearrange("p (c t) -> p c t", c=DC)[:, bk * 2: bk * 2 + 2, t0:t0 + 256],
                        (P4, P5, P6, P7)[bk][:].rearrange("p (u t) -> p u t", u=2),
                        AluOpType.add), [(c_ve, 1)])
                sh_x_ve[t] = c_ve.n
            sh_done_te, sh_done_ve, sh_done_se = c_te.n, c_ve.n, c_se.n

            b.wait("ge", c_te, sh_done_te)
            b.wait("ge", c_ve, sh_done_ve)

            c_ew = [b.ch("c_ew0_" + tag, dma=True), b.ch("c_ew1_" + tag, dma=True)]
            c_cb = [b.ch("c_cb0_" + tag, dma=True), b.ch("c_cb1_" + tag, dma=True)]
            gate_r = w["l1_gate"].rearrange("e (cc p) hh -> e p cc hh", p=128)
            up_r = w["l1_up"].rearrange("e (cc p) hh -> e p cc hh", p=128)
            down_r = w["l1_down"].rearrange("e (hc p) v -> e p hc v", p=128)
            eslot = (ESL0, ESL1)
            b.wait("te", c_se, sh_done_se)
            b.wait("te", c_ve, sh_done_ve)
            b.wait("se", c_ve, sh_done_ve)
            e_te_done = {}
            e_ve_done = {}
            silu_se = {}
            hid_ve = {}
            dn_te = {}
            xadd_ve = {}
            def emit_gu(g):
                e, t = g // 8, g % 8
                ES = eslot[e % 2]
                t0 = t * 256
                if g >= 1:
                    b.wait("te", c_se, silu_se[g - 1])
                    b.wait("te", c_ve, hid_ve[g - 1])
                for hc in range(4):
                    for cc in range(DC):
                        b.op("te", lambda h, hc=hc, cc=cc, t0=t0, ES=ES: h.matmul(
                            PA[:, hc * 256:(hc + 1) * 256],
                            ES(0, 128, cc * 512 + hc * 128, cc * 512 + (hc + 1) * 128),
                            xn_sl(cc, t0, 256),
                            start=(cc == 0), stop=(cc == DC - 1)),
                            [(c_te, 1)] if cc == DC - 1 else ())
                for hc in range(4):
                    for cc in range(DC):
                        b.op("te", lambda h, hc=hc, cc=cc, t0=t0, ES=ES: h.matmul(
                            PB[:, hc * 256:(hc + 1) * 256],
                            ES(0, 128, 4096 + cc * 512 + hc * 128, 4096 + cc * 512 + (hc + 1) * 128),
                            xn_sl(cc, t0, 256),
                            start=(cc == 0), stop=(cc == DC - 1)),
                            [(c_te, 1)] if cc == DC - 1 else ())
                gu_done = c_te.n
                b.wait("se", c_te, gu_done)
                if g >= 1:
                    b.wait("se", c_ve, hid_ve[g - 1])
                b.op("se", lambda h: h.activation(
                    HS(0, 128, 0, 1024), PA[:], AF.Silu), [(c_se, 1)])
                silu_se[g] = c_se.n
                HD = (HID0, HID1)[g % 2]
                b.wait("ve", c_se, silu_se[g])
                if g >= 2:
                    b.wait("ve", c_te, dn_te[g - 2])
                sl = e % 2
                b.wait("ve", c_cb[sl], cb_target[e])
                b.op("ve", lambda h, HD=HD: h.tensor_tensor(
                    HD(0, 128, 0, 1024), HS(0, 128, 0, 1024), PB[:],
                    AluOpType.mult), [(c_ve, 1)])
                b.wait("ve", c_ve)
                CBCe = (CBC0, CBC1)[sl]
                for hc in range(4):
                    b.op("ve", lambda h, HD=HD, hc=hc, t0=t0, CBCe=CBCe: h.tensor_tensor(
                        HD(0, 128, hc * 256, (hc + 1) * 256),
                        HD(0, 128, hc * 256, (hc + 1) * 256),
                        CBCe(0, 128, t0, t0 + 256), AluOpType.mult), [(c_ve, 1)])
                hid_ve[g] = c_ve.n

            def emit_down(g):
                e, t = g // 8, g % 8
                ES = eslot[e % 2]
                HD = (HID0, HID1)[g % 2]
                t0 = t * 256
                b.wait("te", c_ve, hid_ve[g])
                if g >= 1:
                    b.wait("te", c_ve, xadd_ve[g - 1])
                for v in range(DC):
                    pvt = (P4, P5, P6, P7)[v // 2]
                    off = (v % 2) * 256
                    for hc in range(4):
                        b.op("te", lambda h, pvt=pvt, off=off, hc=hc, v=v, ES=ES, HD=HD: h.matmul(
                            pvt[:, off:off + 256],
                            ES(0, 128, 8192 + hc * 1024 + v * 128, 8192 + hc * 1024 + (v + 1) * 128),
                            HD(0, 128, hc * 256, (hc + 1) * 256),
                            start=(hc == 0 and v % 2 == 0),
                            stop=(hc == 3 and v % 2 == 1)),
                            [(c_te, 1)] if (hc == 3 and v == DC - 1) else ())
                dn_te[g] = c_te.n
                b.wait("ve", c_te, dn_te[g])
                for bk in range(4):
                    b.op("ve", lambda h, bk=bk, t0=t0: h.tensor_tensor(
                        XSB[:].rearrange("p (c t) -> p c t", c=DC)[:, bk * 2: bk * 2 + 2, t0:t0 + 256],
                        XSB[:].rearrange("p (c t) -> p c t", c=DC)[:, bk * 2: bk * 2 + 2, t0:t0 + 256],
                        (P4, P5, P6, P7)[bk][:].rearrange("p (u t) -> p u t", u=2),
                        AluOpType.add), [(c_ve, 1)])
                xadd_ve[g] = c_ve.n

            cb_target = {}
            for e in range(16):
                sl = e % 2
                ES = eslot[sl]
                CBCe = (CBC0, CBC1)[sl]
                if e >= 2:
                    b.wait("ge", c_te, dn_te[(e - 2) * 8 + 7])
                    b.wait("ge", c_ve, xadd_ve[(e - 2) * 8 + 7])
                b.op("ge", lambda h, e=e, ES=ES: h.dma_start(
                    ES(0, 128, 0, 4096).rearrange("p (cc hh) -> p cc hh", cc=DC),
                    gate_r[e]), [(c_ew[sl], 16)])
                b.op("ge", lambda h, e=e, ES=ES: h.dma_start(
                    ES(0, 128, 4096, 8192).rearrange("p (cc hh) -> p cc hh", cc=DC),
                    up_r[e]), [(c_ew[sl], 16)])
                b.op("ge", lambda h, e=e, ES=ES: h.dma_start(
                    ES(0, 128, 8192, 12288).rearrange("p (hc v) -> p hc v", hc=4),
                    down_r[e]), [(c_ew[sl], 16)])
                b.op("ge", lambda h, e=e, CBCe=CBCe, scrow=scrow: h.dma_start(
                    CBCe(0, 128, 0, T),
                    scr[scrow + e:scrow + e + 1, :].broadcast_to((128, T))), [(c_cb[sl], 16)])
                cb_target[e] = c_cb[sl].n
                b.wait("te", c_ew[sl], 16 * 3 * (e // 2 + 1))
                for t in range(8):
                    g = e * 8 + t
                    emit_gu(g)
                    if g >= 1:
                        emit_down(g - 1)
            emit_down(127)
            return c_ve.n

        def emit_l1_conv():
            ar = Arena()
            W1B = ar.alloc(8192)
            W2B = ar.alloc(8192)
            HB = ar.alloc(16384)
            ETN = ar.alloc(1024, F32)
            ETT = ar.alloc(256, F32)
            DGT = ar.alloc(2560)
            MASK = ar.alloc(128)
            IOT = ar.alloc(512)
            BIASR = ar.alloc(1024)
            ONR = ar.alloc(512)

            entry_te, entry_ve, entry_se = c_te.n, c_ve.n, c_se.n
            xn_done = emit_layernorm()
            for _e in ("ge", "sy"):
                b.wait(_e, c_te, entry_te)
                b.wait(_e, c_ve, entry_ve)
                b.wait(_e, c_se, entry_se)
            b.dma_barrier()

            c_cw = b.ch("c_cw", dma=True)
            c_cwh = b.ch("c_cwh", dma=True)
            w1_r = w["l1_W1"].rearrange("(dc p) o -> p dc o", p=128)
            w2_r = w["l1_W2"].rearrange("(dc p) o -> p dc o", p=128)
            b.op("ge", lambda h: h.dma_start(
                W1B(0, 128, 0, 8192).rearrange("p (dc o) -> p dc o", dc=DC), w1_r),
                [(c_cw, 16)])
            b.op("ge", lambda h: h.dma_start(
                W2B(0, 128, 0, 8192).rearrange("p (dc o) -> p dc o", dc=DC), w2_r),
                [(c_cw, 16)])
            b.op("sy", lambda h: h.dma_start(ETN(0, 21, 0, 1024), w["l1_et"]), [(c_cwh, 16)])
            b.op("ge", lambda h: h.dma_start(
                BIASR(0, 1, 0, 1024), w["l1_bias"].rearrange("(o d) -> o d", o=1)),
                [(c_cw, 16)])
            b.op("ge", lambda h: h.memset(ONR(0, 1, 0, 512), 1.0), [(c_ge, 1)])
            pi32 = IOT(0, 128, 0, 256).bitcast(mybir.dt.int32)
            fi32 = IOT(0, 128, 256, 512).bitcast(mybir.dt.int32)
            b.op("ge", lambda h: h.iota(pi32, pattern=[[0, 128]], channel_multiplier=1),
                 [(c_ge, 1)])
            b.op("ge", lambda h: h.iota(fi32, pattern=[[1, 128]], channel_multiplier=0),
                 [(c_ge, 1)])
            init_ge2 = c_ge.n
            b.wait("ve", c_ge, init_ge2)
            b.op("ve", lambda h: h.tensor_tensor(
                MASK(0, 128, 0, 128), pi32, fi32, AluOpType.is_equal), [(c_ve, 1)])
            mask_done = c_ve.n

            b.wait("te", c_cw)
            b.wait("te", c_cwh)
            b.wait("te", c_ve, entry_ve)
            b.wait("te", c_se, entry_se)
            for dc in range(DC):
                b.op("te", lambda h, dc=dc: h.transpose(
                    P6[:, dc * 32: dc * 32 + 21],
                    ETN(0, 21, dc * 128, (dc + 1) * 128), IDF[0:21, 0:21]),
                    [(c_te, 1)] if dc == DC - 1 else ())
            ett_tr = c_te.n
            b.wait("ve", c_te, ett_tr)
            b.op("ve", lambda h: h.tensor_copy(ETT(0, 128, 0, 256), P6[:, 0:256]),
                 [(c_ve, 1)])
            ett_done = c_ve.n

            b.wait("te", c_ve, xn_done)
            hb_ev = {}
            cv_te = {}
            for dc in range(DC):
                b.wait("ve", c_ve, max(ett_done, mask_done))
                if dc >= 1:
                    b.wait("ve", c_te, cv_te[dc - 1])
                for m in range(1, 21):
                    b.op("ve", lambda h, dc=dc, m=m: h.tensor_scalar(
                        DGT(0, 128, (m - 1) * 128, m * 128),
                        MASK(0, 128, 0, 128),
                        ETT(0, 128, dc * 32 + m, dc * 32 + m + 1), None,
                        AluOpType.mult), [(c_ve, 1)])
                dg_done = c_ve.n
                b.wait("te", c_ve, dg_done)
                for t in range(NT):
                    t0 = t * 512
                    half = (dc * NT + t) % 2
                    if dc * NT + t >= 2:
                        b.wait("te", c_ve, hb_ev[dc * NT + t - 2])
                    for m in range(1, 21):
                        b.op("te", lambda h, dc=dc, m=m, t0=t0, half=half: h.matmul(
                            PA[:, half * 512:(half + 1) * 512],
                            DGT(0, 128, (m - 1) * 128, m * 128),
                            XN[:, dc * PADT + PADW + t0 - m: dc * PADT + PADW + t0 - m + 512],
                            start=(m == 1), stop=(m == 20)),
                            [(c_te, 1)] if m == 20 else ())
                    cv_te[dc] = c_te.n
                    b.wait("ve", c_te, cv_te[dc])
                    b.op("ve", lambda h, dc=dc, t0=t0, half=half: h.tensor_copy(
                        HB(0, 128, dc * T + t0, dc * T + t0 + 512),
                        PA[:, half * 512:(half + 1) * 512]), [(c_ve, 1)])
                    hb_ev[dc * NT + t] = c_ve.n
            conv_ve = c_ve.n

            b.wait("te", c_ve, conv_ve)
            wp_ev = {}
            for t in range(NT):
                t0 = t * 512
                for oc in range(DC):
                    gi = t * DC + oc
                    pvt = (P4, P5, P6)[gi % 3]
                    if gi >= 3:
                        b.wait("te", c_ve, wp_ev[gi - 3])
                    for dc in range(DC):
                        b.op("te", lambda h, dc=dc, oc=oc, t0=t0, pvt=pvt: h.matmul(
                            pvt[:, 0:512],
                            W1B(0, 128, dc * 1024 + oc * 128, dc * 1024 + (oc + 1) * 128),
                            xn_sl(dc, t0, 512),
                            start=(dc == 0), stop=False),
                            ())
                    for dc in range(DC):
                        b.op("te", lambda h, dc=dc, oc=oc, t0=t0, pvt=pvt: h.matmul(
                            pvt[:, 0:512],
                            W2B(0, 128, dc * 1024 + oc * 128, dc * 1024 + (oc + 1) * 128),
                            HB(0, 128, dc * T + t0, dc * T + t0 + 512),
                            start=False, stop=False),
                            ())
                    b.op("te", lambda h, oc=oc, pvt=pvt: h.matmul(
                        pvt[:, 0:512],
                        BIASR(0, 1, oc * 128, (oc + 1) * 128),
                        ONR(0, 1, 0, 512),
                        start=False, stop=True), [(c_te, 1)])
                    wpm = c_te.n
                    b.wait("ve", c_te, wpm)
                    b.op("ve", lambda h, oc=oc, t0=t0, pvt=pvt: h.tensor_tensor(
                        xsb_sl(oc, t0, 512), xsb_sl(oc, t0, 512), pvt[:, 0:512],
                        AluOpType.add), [(c_ve, 1)])
                    wp_ev[gi] = c_ve.n
            return c_ve.n

        res_marks = {}
        for ph in phases:
            if ph in ("l0ffn1", "l0ffn2"):
                emit_l0_ffn(ph)
            elif ph == "l0attn":
                emit_l0_attn()
            elif ph == "l1moe1":
                emit_l1_moe(ph, 16)
            elif ph == "l1moe2":
                emit_l1_moe(ph, 32)
            elif ph == "l1conv":
                emit_l1_conv()

        last_ve = c_ve.n
        b.wait("sy", c_ve, last_ve)
        yTr = yT.rearrange("(c p) t -> p c t", p=128)
        b.op("sy", lambda h: h.dma_start(
            yTr, XSB[:].rearrange("p (c t) -> p c t", c=DC)), [(c_out, 16)])
        b.wait("sy", c_out)

        blk = es.enter_context(nc.Block())
        b.emit(blk)
    return nc


def _mk_inmaps(inputs):
    import ml_dtypes
    x = inputs["L0_attn_x"]
    consts = {
        "identb": np.eye(128, dtype=ml_dtypes.bfloat16),
        "identf": np.eye(128, dtype=np.float32),
        "onesb": np.ones((128, 1), dtype=ml_dtypes.bfloat16),
        "onesfr": np.ones((1, 128), dtype=np.float32),
        "onesbr": np.ones((1, 128), dtype=ml_dtypes.bfloat16),
    }
    wkeys = [k for k in inputs if k != "L0_attn_x"]
    maps = []
    for core in range(8):
        m = {"xT": np.ascontiguousarray(x[core].T)}
        for k in wkeys:
            m[k] = np.ascontiguousarray(inputs[k])
        m.update(consts)
        maps.append(m)
    return maps


def kernel(**inputs):
    nc = build_program(["l0ffn1", "l0attn", "l0ffn2", "l1moe1", "l1conv", "l1moe2"])
    maps = _mk_inmaps(inputs)
    res = run_bass_kernel_spmd(nc, maps, core_ids=list(range(8)))
    out = np.stack([np.asarray(r["yT"]).T for r in res.results], axis=0)
    return np.ascontiguousarray(out.astype(np.float32))
